# version 10
# baseline (speedup 1.0000x reference)
import numpy as np
import ml_dtypes
from contextlib import ExitStack

import concourse.bass as bass
import concourse.mybir as mybir
from concourse.bass_utils import run_bass_kernel_spmd

F32 = mybir.dt.float32
BF16 = mybir.dt.bfloat16
AF = mybir.ActivationFunctionType
ALU = mybir.AluOpType
AX = mybir.AxisListType
BFD = ml_dtypes.bfloat16

T = 4096
D = 1024
DB = D // 128
DI = 2048
DIB = DI // 128
NS = 64
NH = 16
HD = 64
NE = 8
FF = 4096
TCH = 512
NCH = T // TCH
SQ = 128
NSUB = TCH // SQ
SMID = SQ // 2 - 1

EPS_OUT = 1e-5
EPS_IN = 1e-6

ENGS = ("tensor", "vector", "scalar", "gpsimd", "sync")
DOMS = ("tensor", "vector", "scalar", "gpsimd", "sync_d", "gpsimd_d",
        "scalar_d")
DMA_DOM = {"sync": "sync_d", "gpsimd": "gpsimd_d", "scalar": "scalar_d"}
DOM_ENG = {"sync_d": "sync", "gpsimd_d": "gpsimd", "scalar_d": "scalar"}


class TB:
    __slots__ = ("w", "r", "name")

    def __init__(self, name=""):
        self.w = None
        self.r = {}
        self.name = name


class Prog:

    def __init__(self, block, sems):
        self.block = block
        self.sems = sems
        self.cnt = {d: 0 for d in DOMS}
        self.q = {e: [] for e in ENGS}
        self.wm = {e: {d: 0 for d in DOMS} for e in ENGS}
        self.fence_v = {d: 0 for d in ("sync_d", "gpsimd_d", "scalar_d")}

    def _fence(self, dom, upto):
        if self.fence_v[dom] >= upto:
            return self.fence_v[dom]
        eng = DOM_ENG[dom]
        v = self.cnt[dom]
        sem = self.sems[dom]
        self.q[eng].append(lambda e, s=sem, v=v: e.wait_ge(s, v))
        self.fence_v[dom] = v
        self.wm[eng][dom] = max(self.wm[eng][dom], v)
        return v

    def _dep(self, eng, deps):
        need = {}
        for t in deps:
            if t is None:
                continue
            de, dv = t
            if de in self.fence_v:
                dv = self._fence(de, dv)
            if de in DOM_ENG and DOM_ENG[de] == eng:
                continue
            if de == eng and eng == "tensor":
                continue
            if dv > self.wm[eng][de] and dv > need.get(de, 0):
                need[de] = dv
        for de in sorted(need):
            dv = need[de]
            sem = self.sems[de]
            self.q[eng].append(lambda e, s=sem, v=dv: e.wait_ge(s, v))
            self.wm[eng][de] = dv

    def op(self, eng, fn, reads=(), writes=(), inc=1, dom=None):
        if dom is None:
            dom = eng
        deps = []
        for b in reads:
            deps.append(b.w)
        for b in writes:
            deps.append(b.w)
            deps.extend(b.r.items())
        self._dep(eng, deps)
        sem = self.sems[dom]

        def run(e, fn=fn, s=sem, k=inc):
            fn(e).then_inc(s, k)

        self.q[eng].append(run)
        self.cnt[dom] += inc
        tok = (dom, self.cnt[dom])
        for b in reads:
            if b.r.get(dom, 0) < tok[1]:
                b.r[dom] = tok[1]
        for b in writes:
            b.w = tok
            b.r = {}
        return tok

    def dma(self, eng, fn, reads=(), writes=()):
        return self.op(eng, fn, reads, writes, inc=16, dom=DMA_DOM[eng])

    def barrier(self):
        for d in ("sync_d", "gpsimd_d", "scalar_d"):
            self._fence(d, self.cnt[d])
        snap = dict(self.cnt)
        for e in ENGS:
            self._dep(e, [(d, snap[d]) for d in DOMS
                          if d != e and DOM_ENG.get(d) != e])

    def wait_all(self, eng):
        for d in ("sync_d", "gpsimd_d", "scalar_d"):
            self._fence(d, self.cnt[d])
        snap = dict(self.cnt)
        self._dep(eng, [(d, snap[d]) for d in DOMS if d != eng])

    def finalize(self):
        for e in ENGS:
            ops = self.q[e]
            if not ops:
                continue

            def body(eng, ops=ops):
                for f in ops:
                    f(eng)

            getattr(self.block, e)(body)


def _consts():
    c = {}
    c["ident128"] = np.eye(128, dtype=BFD)
    c["one_f32"] = np.ones((1, 1), dtype=np.float32)
    c["ones_col"] = np.ones((128, 1), dtype=BFD)
    c["ones_row64"] = np.ones((1, 64), dtype=BFD)
    c["ones_row128"] = np.ones((1, 128), dtype=BFD)
    r = np.arange(SQ)[:, None]
    t = np.arange(SQ)[None, :]
    c["m_ref"] = ((r <= t).astype(np.float32)
                  - (r <= SMID).astype(np.float32)).astype(BFD)
    c["tril"] = np.triu(np.ones((SQ, SQ), dtype=np.float32))
    sel = np.zeros((8, 8 * 128), dtype=BFD)
    for e8 in range(8):
        sel[e8, e8 * 128:(e8 + 1) * 128] = 1
    c["sel8"] = sel
    wcol = np.zeros((SQ, 2), dtype=BFD)
    wcol[SMID + 1:, 0] = 1
    wcol[:SMID + 1, 1] = 1
    c["wcols"] = wcol
    return c


INPUT_SPECS = [
    ("L0_attn_hidden_states", (T, D)),
    ("L0_attn_router_w", (D, 4)), ("L0_attn_router_b", (4,)),
    ("L0_attn_ln_g", (4, D)), ("L0_attn_ln_b", (4, D)),
    ("L0_attn_m_win", (D, 2 * DI)), ("L0_attn_m_conv", (4, DI)),
    ("L0_attn_m_alog", (DI,)), ("L0_attn_m_wbc", (DI, 128)),
    ("L0_attn_m_wdt", (DI,)), ("L0_attn_m_dtb", (DI,)),
    ("L0_attn_m_dskip", (DI,)), ("L0_attn_m_wout", (DI, D)),
    ("L0_attn_tc_theta", ()), ("L0_attn_tc_w", (D, D)), ("L0_attn_tc_b", (D,)),
    ("L0_attn_w_q", (D, D)), ("L0_attn_w_k", (D, D)),
    ("L0_attn_w_v", (D, D)), ("L0_attn_w_o", (D, D)),
    ("L0_attn_wlam_conv", (4, D)),
    ("L0_attn_moe_wg", (D, NE)), ("L0_attn_moe_w1", (NE, D, FF)),
    ("L0_attn_moe_b1", (NE, FF)), ("L0_attn_moe_w2", (NE, FF, D)),
    ("L0_attn_moe_b2", (NE, D)),
    ("L0_ffn_W_r", (D, D)), ("L0_ffn_W_k", (D, D)), ("L0_ffn_W_v", (D, D)),
    ("L0_ffn_mu_r", (D,)), ("L0_ffn_mu_k", (D,)),
]


def build_program(dump=(), nphase=6):
    nc = bass.Bass("TRN2", target_bir_lowering=False, debug=False)
    IN = {}
    for name, shape in INPUT_SPECS:
        shp = list(shape) if shape else [1]
        IN[name] = nc.dram_tensor(name, shp, F32, kind="ExternalInput").ap()
    out_hbm = nc.dram_tensor("out", [T, D], F32, kind="ExternalOutput").ap()

    dumps = {}

    def dumpt(name, shape, dt=F32):
        if name in dump:
            dumps[name] = nc.dram_tensor("dump_" + name, list(shape), dt,
                                         kind="ExternalOutput").ap()
            return dumps[name]
        return None

    scr_x0fm = nc.dram_tensor("scr_x0fm", [128, T * DB], BF16, kind="Internal").ap()
    scr_h = nc.dram_tensor("scr_h", [128, DB * T], BF16, kind="Internal").ap()
    scr_x3 = nc.dram_tensor("scr_x3", [128, DB * T], BF16, kind="Internal").ap()
    scr_dp = nc.dram_tensor("scr_dp", [128, DB * (T // 2)], BF16, kind="Internal").ap()
    scr_y = nc.dram_tensor("scr_y", [128, DB * T], BF16, kind="Internal").ap()
    scr_w1 = nc.dram_tensor("scr_w1", [NE * D, FF], BF16, kind="Internal").ap()
    scr_w2 = nc.dram_tensor("scr_w2", [NE * FF, D], BF16, kind="Internal").ap()
    scr_win = nc.dram_tensor("scr_win", [D, 2 * DI], BF16, kind="Internal").ap()
    scr_gf = nc.dram_tensor("scr_gf", [8, T], BF16, kind="Internal").ap()
    scr_wout = nc.dram_tensor("scr_wout", [DI, D], BF16, kind="Internal").ap()

    def sup(scr):
        return scr.rearrange("p (s b t) -> p s b t", s=T // 512, b=DB)

    CN = _consts()
    CONST = {k: nc.inline_tensor(v, name="c_" + k).ap() for k, v in CN.items()}

    with ExitStack() as top:
        sems = {d: top.enter_context(nc.semaphore("sem_" + d)) for d in DOMS}
        block = top.enter_context(nc.Block())
        P = Prog(block, sems)

        def sb(st, name, shape, dt):
            ap = st.enter_context(nc.sbuf_tensor(name, list(shape), dt))
            return ap, TB(name)

        def ps(st, name, shape, dt=F32):
            ap = st.enter_context(nc.psum_tensor(name, list(shape), dt))
            return ap, TB(name)

        scrh_tb = [TB("scrh%d" % i) for i in range(T // 512)]
        scrx0_tb = TB("scrx0")
        scrdp_tb = TB("scrdp")
        scry_tb = TB("scry")
        scrx3_tb = TB("scrx3")
        scrw1_tb = TB("scrw1")
        scrw2_tb = TB("scrw2")
        scrwin_tb = TB("scrwin")
        scrgf_tb = TB("scrgf")
        scrwout_tb = TB("scrwout")
        K = {}
        KT = {}
        for k, v in CN.items():
            a, tb = sb(top, "k_" + k, v.shape, BF16 if v.dtype == BFD else F32)
            K[k], KT[k] = a, tb
            P.dma("sync", lambda e, a=a, k=k: e.dma_start(out=a[:], in_=CONST[k]),
                  writes=(tb,))
        mskbc, mskbc_t = sb(top, "mskbc", (128, 4), F32)
        lng, lng_t = sb(top, "lng", (128, 4 * DB), F32)
        lnb, lnb_t = sb(top, "lnb", (128, 4 * DB), F32)
        lngn, lngn_t = sb(top, "lngn", (128, 4 * DB), F32)

        rowbuf, rowbuf_t = sb(top, "rowbuf", (1, DI), F32)
        rowbufb, rowbufb_t = sb(top, "rowbufb", (1, DI), BF16)
        CPS = [None, None]

        def load_cols(src_row_ap, n, dst_ap, dst_tb, dst_cols=None, scale=None,
                      act=None):
            colps, colps_t = CPS
            """src_row_ap: HBM AP viewable as (1, n). dst columns written at
            dst_ap[:, c] for c in dst_cols (default range(n//128)).
            Optional ACT func + scale applied to the row before transposing."""
            nb = n // 128
            if dst_cols is None:
                dst_cols = list(range(nb))
            P.dma("sync", lambda e: e.dma_start(out=rowbuf[0:1, 0:n], in_=src_row_ap),
                  writes=(rowbuf_t,))
            if act is not None:
                P.op("scalar", lambda e: e.activation(rowbuf[0:1, 0:n],
                                                      rowbuf[0:1, 0:n], act),
                     reads=(rowbuf_t,), writes=(rowbuf_t,))
            for i in range(nb):
                P.op("tensor", lambda e, i=i: e.matmul(
                    colps[:, i % 4:i % 4 + 1], rowbuf[0:1, i * 128:(i + 1) * 128],
                    K["one_f32"][:], start=True, stop=True),
                    reads=(rowbuf_t, KT["one_f32"]), writes=(colps_t,))
                c = dst_cols[i]
                if scale is None:
                    P.op("vector", lambda e, i=i, c=c: e.tensor_copy(
                        dst_ap[:, c:c + 1], colps[:, i % 4:i % 4 + 1]),
                        reads=(colps_t,), writes=(dst_tb,))
                else:
                    P.op("vector", lambda e, i=i, c=c: e.tensor_single_scalar(
                        dst_ap[:, c:c + 1], colps[:, i % 4:i % 4 + 1], scale, ALU.mult),
                        reads=(colps_t,), writes=(dst_tb,))



        with ExitStack() as st:
            x0 = IN["L0_attn_hidden_states"]
            xt, xt_t = sb(st, "p0_xt", (128, 2 * D), F32)
            xhb, xhb_t = sb(st, "p0_xhb", (128, 2 * D), BF16)
            x0b, x0b_t = sb(st, "p0_x0b", (128, 2 * D), BF16)
            sqf, sqf_t = sb(st, "p0_sqf", (128, 2 * D), F32)
            cols, cols_t = sb(st, "p0_cols", (128, 8), F32)
            x0f, x0f_t = sb(st, "p0_x0f", (128, 2 * D), BF16)
            hstage, hstage_t = sb(st, "p0_hstage", (128, 2 * D), BF16)
            pp, pp_t = ps(st, "p0_pool", (1, D))
            cps0, cps0_t = ps(st, "p0_cps", (128, 512))
            CPS[0], CPS[1] = cps0, cps0_t
            P.dma("gpsimd", lambda e: e.dma_start(
                out=scr_win, in_=IN["L0_attn_m_win"]), writes=(scrwin_tb,))
            P.dma("gpsimd", lambda e: e.dma_start(
                out=scr_wout, in_=IN["L0_attn_m_wout"]), writes=(scrwout_tb,))
            for j in range(4):
                load_cols(IN["L0_attn_ln_g"][j, :][None, :], D, lng, lng_t,
                          dst_cols=[j * DB + i for i in range(DB)])
                load_cols(IN["L0_attn_ln_b"][j, :][None, :], D, lnb, lnb_t,
                          dst_cols=[j * DB + i for i in range(DB)])
            P.op("vector", lambda e: e.tensor_single_scalar(
                lngn[:], lng[:], -1.0, ALU.mult),
                reads=(lng_t,), writes=(lngn_t,))
            pt, pt_t = ps(st, "p0_pt", (128, 1024), BF16)
            pt2, pt2_t = ps(st, "p0_pt2", (128, 1024), BF16)

            for tt in range(T // 128):
                ri = tt % 2
                xv = xt[:, ri * D:(ri + 1) * D]
                xh = xhb[:, ri * D:(ri + 1) * D]
                xraw = x0b[:, ri * D:(ri + 1) * D]
                sq = sqf[:, ri * D:(ri + 1) * D]
                P.dma("sync", lambda e, tt=tt, xv=xv: e.dma_start(
                    out=xv, in_=x0[tt * 128:(tt + 1) * 128, :]), writes=(xt_t,))
                s_c = cols[:, 0:1]
                s2_c = cols[:, 1:2]
                m_c = cols[:, 2:3]
                v2_c = cols[:, 3:4]
                v3_c = cols[:, 4:5]
                srt_c = cols[:, 5:6]
                rst_c = cols[:, 6:7]
                mr_c = cols[:, 7:8]
                P.op("vector", lambda e, xv=xv, s_c=s_c: e.reduce_sum(s_c, xv, axis=AX.X),
                     reads=(xt_t,), writes=(cols_t,))
                P.op("scalar", lambda e, xv=xv, sq=sq: e.activation(sq, xv, AF.Square),
                     reads=(xt_t,), writes=(sqf_t,))
                P.op("vector", lambda e, sq=sq, s2_c=s2_c: e.reduce_sum(s2_c, sq, axis=AX.X),
                     reads=(sqf_t,), writes=(cols_t,))
                P.op("vector", lambda e, s_c=s_c, m_c=m_c: e.tensor_single_scalar(
                    m_c, s_c, 1.0 / D, ALU.mult), reads=(cols_t,), writes=(cols_t,))
                P.op("vector", lambda e, s2_c=s2_c, v2_c=v2_c: e.tensor_scalar(
                    v2_c, s2_c, 1.0 / D, EPS_OUT, ALU.mult, ALU.add),
                    reads=(cols_t,), writes=(cols_t,))
                P.op("vector", lambda e, m_c=m_c, v2_c=v2_c, v3_c=v3_c:
                     e.scalar_tensor_tensor(v3_c, m_c, m_c, v2_c, ALU.mult, ALU.subtract),
                     reads=(cols_t,), writes=(cols_t,))
                P.op("scalar", lambda e, v3_c=v3_c, srt_c=srt_c: e.activation(
                    srt_c, v3_c, AF.Ln, scale=-1.0), reads=(cols_t,), writes=(cols_t,))
                P.op("scalar", lambda e, srt_c=srt_c, rst_c=rst_c: e.activation(
                    rst_c, srt_c, AF.Exp, scale=-0.5), reads=(cols_t,), writes=(cols_t,))
                P.op("vector", lambda e, m_c=m_c, rst_c=rst_c, mr_c=mr_c: e.tensor_tensor(
                    mr_c, m_c, rst_c, ALU.mult), reads=(cols_t,), writes=(cols_t,))
                P.op("vector", lambda e, xv=xv, rst_c=rst_c, mr_c=mr_c, xh=xh:
                     e.tensor_scalar(xh, xv, rst_c, mr_c, ALU.mult, ALU.subtract),
                     reads=(xt_t, cols_t), writes=(xhb_t,))
                P.op("vector", lambda e, xv=xv, xraw=xraw: e.tensor_copy(xraw, xv),
                     reads=(xt_t,), writes=(x0b_t,))
                for half in range(2):
                    P.op("tensor", lambda e, xh=xh, half=half, tt=tt: e.matmul(
                        pp[0:1, half * 512:(half + 1) * 512], K["ones_col"][:],
                        xh[:, half * 512:(half + 1) * 512],
                        start=(tt == 0), stop=(tt == T // 128 - 1)),
                        reads=(xhb_t, KT["ones_col"]), writes=(pp_t,))
                s_idx, tloc = tt // 4, (tt % 4) * 128
                for b in range(DB):
                    P.op("tensor", lambda e, xh=xh, b=b: e.transpose(
                        pt[:, b * 128:(b + 1) * 128], xh[:, b * 128:(b + 1) * 128],
                        K["ident128"][:]), reads=(xhb_t, KT["ident128"]), writes=(pt_t,))
                hstg = hstage[:, ri * D:(ri + 1) * D]
                P.op("vector", lambda e, hstg=hstg: e.tensor_copy(hstg, pt[:]),
                     reads=(pt_t,), writes=(hstage_t,))
                P.dma("gpsimd", lambda e, hstg=hstg, s_idx=s_idx, tloc=tloc:
                      e.dma_start(
                          out=sup(scr_h)[:, s_idx, :, tloc:tloc + 128],
                          in_=hstg.rearrange("p (b t) -> p b t", b=DB)),
                      reads=(hstage_t,), writes=(scrh_tb[s_idx],))
                for b in range(DB):
                    P.op("tensor", lambda e, xraw=xraw, b=b: e.transpose(
                        pt2[:, b * 128:(b + 1) * 128], xraw[:, b * 128:(b + 1) * 128],
                        K["ident128"][:]), reads=(x0b_t, KT["ident128"]), writes=(pt2_t,))
                xf = x0f[:, ri * D:(ri + 1) * D]
                P.op("vector", lambda e, xf=xf: e.tensor_copy(xf, pt2[:]),
                     reads=(pt2_t,), writes=(x0f_t,))
                P.dma("gpsimd", lambda e, xf=xf, s_idx=s_idx, tloc=tloc: e.dma_start(
                    out=sup(scr_x0fm)[:, s_idx, :, tloc:tloc + 128],
                    in_=xf.rearrange("p (b t) -> p b t", b=DB)),
                    reads=(x0f_t,), writes=(scrx0_tb,))

            poolrow, poolrow_t = sb(st, "p0_poolrow", (1, D), F32)
            P.op("scalar", lambda e: e.copy(poolrow[:], pp[:]),
                 reads=(pp_t,), writes=(poolrow_t,))
            pfm, pfm_t = ps(st, "p0_pfm", (128, 8))
            for b in range(DB):
                P.op("tensor", lambda e, b=b: e.matmul(
                    pfm[:, b:b + 1], poolrow[0:1, b * 128:(b + 1) * 128],
                    K["one_f32"][:], start=True, stop=True),
                    reads=(poolrow_t, KT["one_f32"]), writes=(pfm_t,))
            pfs, pfs_t = sb(st, "p0_pfs", (128, 8), F32)
            P.op("vector", lambda e: e.tensor_single_scalar(pfs[:], pfm[:], 1.0 / T,
                                                            ALU.mult),
                 reads=(pfm_t,), writes=(pfs_t,))
            rwf, rwf_t = sb(st, "p0_rwf", (128, 32), F32)
            P.dma("sync", lambda e: e.dma_start(
                out=rwf.rearrange("p (a j) -> p a j", j=4),
                in_=IN["L0_attn_router_w"].rearrange("(a p) j -> p a j", p=128)),
                writes=(rwf_t,))
            prep, prep_t = sb(st, "p0_prep", (128, 32), F32)
            for j in range(4):
                P.op("vector", lambda e, j=j: e.tensor_copy(
                    prep.rearrange("p (a j) -> p a j", j=4)[:, :, j], pfs[:]),
                    reads=(pfs_t,), writes=(prep_t,))
            prod, prod_t = sb(st, "p0_prod", (128, 32), F32)
            P.op("vector", lambda e: e.tensor_tensor(prod[:], prep[:], rwf[:], ALU.mult),
                 reads=(prep_t, rwf_t), writes=(prod_t,))
            spk, spk_t = sb(st, "p0_spk", (128, 4), F32)
            for j in range(4):
                P.op("vector", lambda e, j=j: e.reduce_sum(
                    spk[:, j:j + 1],
                    prod.rearrange("p (a j) -> p a j", j=4)[:, :, j], axis=AX.X),
                    reads=(prod_t,), writes=(spk_t,))
            tmp4, tmp4_t = sb(st, "p0_tmp4", (128, 4), F32)
            lvl = 64
            while lvl >= 1:
                P.dma("sync", lambda e, lvl=lvl: e.dma_start(
                    out=tmp4[0:lvl, :], in_=spk[lvl:2 * lvl, :]),
                    reads=(spk_t,), writes=(tmp4_t,))
                P.op("vector", lambda e, lvl=lvl: e.tensor_tensor(
                    spk[0:lvl, :], spk[0:lvl, :], tmp4[0:lvl, :], ALU.add),
                    reads=(tmp4_t, spk_t), writes=(spk_t,))
                lvl //= 2
            rbrow, rbrow_t = sb(st, "p0_rbrow", (1, 4), F32)
            P.dma("sync", lambda e: e.dma_start(out=rbrow[:],
                                                in_=IN["L0_attn_router_b"][None, :]),
                  writes=(rbrow_t,))
            P.op("vector", lambda e: e.tensor_tensor(
                spk[0:1, :], spk[0:1, :], rbrow[0:1, :], ALU.add),
                reads=(rbrow_t, spk_t), writes=(spk_t,))
            P.op("scalar", lambda e: e.activation(spk[0:1, 1 * 0:4], spk[0:1, 0:4],
                                                  AF.Sigmoid),
                 reads=(spk_t,), writes=(spk_t,))
            mrow, mrow_t = sb(st, "p0_mrow", (1, 4), BF16)
            P.op("vector", lambda e: e.tensor_single_scalar(mrow[:], spk[0:1, 0:4], 0.3,
                                                            ALU.is_gt),
                 reads=(spk_t,), writes=(mrow_t,))
            mps, mps_t = ps(st, "p0_mps", (128, 512))
            P.op("tensor", lambda e: e.matmul(mps[:, 0:4], K["ones_row128"][:], mrow[:],
                                              start=True, stop=True),
                 reads=(mrow_t, KT["ones_row128"]), writes=(mps_t,))
            P.op("vector", lambda e: e.tensor_copy(mskbc[:], mps[:, 0:4]),
                 reads=(mps_t,), writes=(mskbc_t,))

            dpf = dumpt("pfs", (128, 8))
            if dpf is not None:
                P.dma("gpsimd", lambda e: e.dma_start(out=dpf, in_=pfs[:]), reads=(pfs_t,))
            dpr = dumpt("prod", (128, 32))
            if dpr is not None:
                P.dma("gpsimd", lambda e: e.dma_start(out=dpr, in_=prod[:]), reads=(prod_t,))
            dm = dumpt("mskbc", (128, 4))
            if dm is not None:
                P.dma("gpsimd", lambda e: e.dma_start(out=dm, in_=mskbc[:]),
                      reads=(mskbc_t,))
            dsc = dumpt("scores", (128, 4))
            if dsc is not None:
                P.dma("gpsimd", lambda e: e.dma_start(out=dsc, in_=spk[:]),
                      reads=(spk_t,))
            P.barrier()

        def phase_mamba(st):
            NSC = 4
            CDB = DB * 512
            pA, _ = ps(st, "m_pA", (128, 2048))
            pA_g = [TB("pAg%d" % i) for i in range(4)]
            pB, _ = ps(st, "m_pB", (128, 1024))
            pB_g = [TB("pBg%d" % i) for i in range(2)]
            pT, pT_t = ps(st, "m_pT", (128, 1024), BF16)
            pT2, pT2_t = ps(st, "m_pT2", (128, 1024), BF16)

            CPS[0], CPS[1] = pB[:, 0:512], pB_g[0]
            convc, convc_t = sb(st, "m_convc", (128, 64), F32)
            for kk in range(4):
                load_cols(IN["L0_attn_m_conv"][kk, :][None, :], DI, convc, convc_t,
                          dst_cols=[a * 4 + kk for a in range(DIB)])
            dskipc, dskipc_t = sb(st, "m_dskipc", (128, 16), F32)
            load_cols(IN["L0_attn_m_dskip"][None, :], DI, dskipc, dskipc_t)
            prow, prow_t = rowbuf, rowbuf_t
            prowb, prowb_t = rowbufb, rowbufb_t
            wdt_bc, wdt_bc_t = sb(st, "m_wdtbc", (128, DI), BF16)
            dtb_bc, dtb_bc_t = sb(st, "m_dtbbc", (128, DI), BF16)
            A_bc, A_bc_t = sb(st, "m_Abc", (128, DI), BF16)
            for wsrc, dst, dst_t, neg_exp in (
                    (IN["L0_attn_m_wdt"], wdt_bc, wdt_bc_t, False),
                    (IN["L0_attn_m_dtb"], dtb_bc, dtb_bc_t, False),
                    (IN["L0_attn_m_alog"], A_bc, A_bc_t, True)):
                P.dma("sync", lambda e, wsrc=wsrc: e.dma_start(out=prow[:],
                                                               in_=wsrc[None, :]),
                      reads=(prow_t,), writes=(prow_t,))
                if neg_exp:
                    P.op("scalar", lambda e: e.activation(prow[:], prow[:], AF.Exp),
                         reads=(prow_t,), writes=(prow_t,))
                    P.op("vector", lambda e: e.tensor_single_scalar(
                        prowb[:], prow[:], -1.0, ALU.mult),
                        reads=(prow_t,), writes=(prowb_t,))
                else:
                    P.op("vector", lambda e: e.tensor_copy(prowb[:], prow[:]),
                         reads=(prow_t,), writes=(prowb_t,))
                for g in range(4):
                    P.op("tensor", lambda e, g=g: e.matmul(
                        pA[:, g * 512:(g + 1) * 512], K["ones_row128"][:],
                        prowb[0:1, g * 512:(g + 1) * 512], start=True, stop=True),
                        reads=(prowb_t, KT["ones_row128"]), writes=(pA_g[g],))
                P.op("vector", lambda e, dst=dst: e.tensor_copy(dst[:], pA[:]),
                     reads=tuple(pA_g), writes=(dst_t,))
            cast_jobs = []
            w1f = IN["L0_attn_moe_w1"].rearrange("e d f -> (e d) f")
            w2f = IN["L0_attn_moe_w2"].rearrange("e f d -> (e f) d")
            for i in range(32):
                cast_jobs.append((scr_w1[i * 256:(i + 1) * 256, :],
                                  w1f[i * 256:(i + 1) * 256, :], scrw1_tb))
            for i in range(32):
                cast_jobs.append((scr_w2[i * 1024:(i + 1) * 1024, :],
                                  w2f[i * 1024:(i + 1) * 1024, :], scrw2_tb))
            wbc_sb, wbc_sb_t = sb(st, "m_wbc", (128, DIB * 128), BF16)
            P.dma("gpsimd", lambda e: e.dma_start(
                out=wbc_sb.rearrange("p (a j) -> p a j", j=128),
                in_=IN["L0_attn_m_wbc"].rearrange(
                    "(a p) j -> p a j", p=128)), writes=(wbc_sb_t,))

            hcm, _hcm = sb(st, "m_hcm", (128, 2 * CDB), BF16)
            hcm_rt = [TB("hcmr0"), TB("hcmr1")]
            xh, _xh = sb(st, "m_xh", (128, 2 * CDB), BF16)
            xh_rt = [TB("xhr0"), TB("xhr1")]
            sqb, sqb_t = sb(st, "m_sqb", (128, 2 * 512), BF16)
            rows, rows_t = sb(st, "m_rows", (1, 2 * 512), F32)
            rowsb, rowsb_t = sb(st, "m_rowsb", (1, 2 * 512), BF16)
            Rsb, Rsb_t = sb(st, "m_Rsb", (128, 512), BF16)
            MRsb, MRsb_t = sb(st, "m_MRsb", (128, 512), BF16)
            winp, winp_t = sb(st, "m_winp", (128, 2 * 8 * 128), BF16)
            winp_tb = [TB("winp0"), TB("winp1")]
            xcp, _xcp = sb(st, "m_xcp", (128, 2 * DIB * 515), BF16)
            xcp_ct = [[TB("xcp%d_%d" % (s, ct)) for ct in range(DIB)]
                      for s in range(2)]
            tail3, tail3_t = sb(st, "m_tail3", (128, 2 * DIB * 3), BF16)
            tail3_rt = [TB("tl0"), TB("tl1")]
            zbuf, zbuf_t = sb(st, "m_zbuf", (128, 2 * 512), BF16)
            xct, _xct = sb(st, "m_xct", (128, NSC * 2048), BF16)
            xct_qt = [TB("xctq%d" % q) for q in range(NSC)]
            dtsub, dtsub_t = sb(st, "m_dtsub", (128, 2048), BF16)
            usub, usub_t = dtsub, dtsub_t
            convb, convb_t = sb(st, "m_convb", (128, 4 * 512), F32)
            Bn, _Bn = sb(st, "m_Bn", (64, 2 * 512), BF16)
            Bn_rt = [TB("Bn0"), TB("Bn1")]
            Cn, _Cn = sb(st, "m_Cn", (64, 2 * 512), BF16)
            Cn_rt = [TB("Cn0"), TB("Cn1")]
            bct, _bct = sb(st, "m_bct", (128, 2 * NSC * 128), BF16)
            bct_rt = [TB("bct0"), TB("bct1")]
            dtA, dtA_t = sb(st, "m_dtA", (128, 2 * 2048), BF16)
            dtA_tb = [TB("dtA0"), TB("dtA1")]
            dcl, dcl_t = sb(st, "m_dcl", (128, 2048), BF16)
            Et, Et_t = sb(st, "m_Et", (128, 2048), BF16)
            Wrow = Et[0:1, 0:2048]
            e1t, e1t_t = dcl, dcl_t
            Ut, Ut_t = usub, usub_t
            Gm, Gm_t = sb(st, "m_Gm", (128, 128), BF16)
            hst, hst_t = sb(st, "m_hst", (64, 2048), BF16)
            tmpst, tmpst_t = sb(st, "m_tmpst", (64, 2048), BF16)
            yt, yt_t = sb(st, "m_yt", (128, NSC * 2048), BF16)
            Pf, Pf_t = sb(st, "m_Pf", (128, DIB * 512), BF16)
            woutp, woutp_t = sb(st, "m_woutp", (128, 2 * 4 * 128), BF16)
            woutp_tb = [TB("wop0"), TB("wop1")]

            P.op("vector", lambda e: e.memset(hst[:], 0.0), writes=(hst_t,))
            P.op("vector", lambda e: e.memset(tail3[:], 0.0),
                 writes=(tail3_rt[0], tail3_rt[1]))

            def xv(slot, ct):
                return xcp[:, slot * DIB * 515 + ct * 515 + 3:
                           slot * DIB * 515 + ct * 515 + 515]

            def xcp3(slot):
                return xcp.rearrange("p (s c k) -> p s c k", s=2, k=515)[:, slot]

            def t3v(slot):
                return tail3.rearrange("p (s c k) -> p s c k", s=2, k=3)[:, slot]

            def cast_drip(c):
                if c > 0 and cast_jobs:
                    dst, src2, tbw = cast_jobs.pop(0)
                    P.dma("gpsimd", lambda e, dst=dst, src2=src2:
                          e.dma_start(out=dst, in_=src2), writes=(tbw,))

            def stageA(c):
                slot = c % 2
                ycnt = [0]
                _oy = None
                hcv = hcm[:, slot * CDB:(slot + 1) * CDB]
                xhv = xh[:, slot * CDB:(slot + 1) * CDB]
                P.dma("sync", lambda e: e.dma_start(
                    out=hcv, in_=sup(scr_h)[:, c, :, :]),
                    reads=(scrh_tb[c],), writes=(hcm_rt[slot],))
                ycnt[0] += 1
                if ycnt[0] % 8 == 0:
                    cast_drip(c)
                yield
                P.op("vector", lambda e: e.tensor_copy(
                    xcp3(slot)[:, :, 0:3], t3v((c - 1) % 2)),
                    reads=(tail3_rt[(c - 1) % 2],),
                    writes=tuple(xcp_ct[slot]))
                ycnt[0] += 1
                if ycnt[0] % 8 == 0:
                    cast_drip(c)
                yield
                for b in range(DB):
                    sv = sqb[:, (b % 2) * 512:(b % 2 + 1) * 512]
                    P.op("scalar", lambda e, b=b, sv=sv: e.activation(
                        sv, hcv[:, b * 512:(b + 1) * 512], AF.Square),
                        reads=(hcm_rt[slot],), writes=(sqb_t,))
                    P.op("tensor", lambda e, b=b: e.matmul(
                        pB[0:1, 0:512], K["ones_col"][:], hcv[:, b * 512:(b + 1) * 512],
                        start=(b == 0), stop=(b == DB - 1)),
                        reads=(hcm_rt[slot], KT["ones_col"]), writes=(pB_g[0],))
                    P.op("tensor", lambda e, b=b, sv=sv: e.matmul(
                        pB[0:1, 512:1024], K["ones_col"][:], sv,
                        start=(b == 0), stop=(b == DB - 1)),
                        reads=(sqb_t, KT["ones_col"]), writes=(pB_g[1],))
                rm = rows[0:1, 0:512]
                rv = rows[0:1, 512:1024]
                P.op("vector", lambda e: e.tensor_single_scalar(
                    rm, pB[0:1, 0:512], 1.0 / D, ALU.mult),
                    reads=(pB_g[0],), writes=(rows_t,))
                P.op("vector", lambda e: e.tensor_scalar(
                    rv, pB[0:1, 512:1024], 1.0 / D, EPS_IN, ALU.mult, ALU.add),
                    reads=(pB_g[1],), writes=(rows_t,))
                ycnt[0] += 1
                if ycnt[0] % 8 == 0:
                    cast_drip(c)
                yield
                P.op("vector", lambda e: e.tensor_tensor(
                    rowsb[0:1, 0:512], rm, rm, ALU.mult),
                    reads=(rows_t,), writes=(rowsb_t,))
                P.op("vector", lambda e: e.tensor_tensor(
                    rv, rv, rowsb[0:1, 0:512], ALU.subtract),
                    reads=(rows_t, rowsb_t), writes=(rows_t,))
                P.op("scalar", lambda e: e.activation(rv, rv, AF.Ln),
                     reads=(rows_t,), writes=(rows_t,))
                P.op("scalar", lambda e: e.activation(rv, rv, AF.Exp, scale=-0.5),
                     reads=(rows_t,), writes=(rows_t,))
                P.op("vector", lambda e: e.tensor_copy(
                    rowsb[0:1, 0:512], rv), reads=(rows_t,), writes=(rowsb_t,))
                P.op("vector", lambda e: e.tensor_tensor(
                    rowsb[0:1, 512:1024], rm, rv, ALU.mult),
                    reads=(rows_t,), writes=(rowsb_t,))
                ycnt[0] += 1
                if ycnt[0] % 8 == 0:
                    cast_drip(c)
                yield
                P.op("tensor", lambda e: e.matmul(
                    pB[:, 0:512], K["ones_row128"][:], rowsb[0:1, 0:512],
                    start=True, stop=True),
                    reads=(rowsb_t, KT["ones_row128"]), writes=(pB_g[0],))
                P.op("vector", lambda e: e.tensor_copy(Rsb[:], pB[:, 0:512]),
                     reads=(pB_g[0],), writes=(Rsb_t,))
                P.op("tensor", lambda e: e.matmul(
                    pB[:, 512:1024], K["ones_row128"][:], rowsb[0:1, 512:1024],
                    start=True, stop=True),
                    reads=(rowsb_t, KT["ones_row128"]), writes=(pB_g[1],))
                P.op("vector", lambda e: e.tensor_copy(MRsb[:], pB[:, 512:1024]),
                     reads=(pB_g[1],), writes=(MRsb_t,))
                ycnt[0] += 1
                if ycnt[0] % 8 == 0:
                    cast_drip(c)
                yield
                for b in range(DB):
                    tv = convb[:, 0:512]
                    P.op("vector", lambda e, b=b, tv=tv: e.tensor_tensor(
                        tv, hcv[:, b * 512:(b + 1) * 512], Rsb[:], ALU.mult),
                        reads=(hcm_rt[slot], Rsb_t), writes=(convb_t,))
                    P.op("vector", lambda e, tv=tv: e.tensor_tensor(
                        tv, tv, MRsb[:], ALU.subtract),
                        reads=(MRsb_t, convb_t), writes=(convb_t,))
                    P.op("vector", lambda e, b=b, tv=tv: e.tensor_scalar(
                        xhv[:, b * 512:(b + 1) * 512], tv, lng[:, b:b + 1],
                        lnb[:, b:b + 1], ALU.mult, ALU.add),
                        reads=(convb_t, lng_t, lnb_t), writes=(xh_rt[slot],))
                    ycnt[0] += 1
                    if ycnt[0] % 8 == 0:
                        cast_drip(c)
                    yield
                for pc in range(16):
                    wsl = pc % 2
                    wv = winp[:, wsl * 1024:(wsl + 1) * 1024]
                    P.dma("sync", lambda e, pc=pc, wv=wv: e.dma_start(
                        out=wv.rearrange("p (k f) -> p k f", f=128),
                        in_=scr_win[:, pc * 128:(pc + 1) * 128].rearrange(
                            "(k p) f -> p k f", p=128)),
                        reads=(scrwin_tb,), writes=(winp_tb[wsl],))
                    g = pc % 2
                    def mmfn(e, wv=wv, g=g):
                        r = None
                        for kk in range(8):
                            r = e.matmul(
                                pB[:, g * 512:(g + 1) * 512],
                                wv[:, kk * 128:(kk + 1) * 128],
                                xhv[:, kk * 512:(kk + 1) * 512],
                                start=(kk == 0), stop=(kk == 7))
                        return r
                    P.op("tensor", mmfn, reads=(winp_tb[wsl], xh_rt[slot]),
                         writes=(pB_g[g],))
                    P.op("scalar", lambda e, g=g, pc=pc: e.copy(
                        xcp[:, slot * DIB * 515 + pc * 515 + 3:
                            slot * DIB * 515 + pc * 515 + 515],
                        pB[:, g * 512:(g + 1) * 512]),
                        reads=(pB_g[g],), writes=(xcp_ct[slot][pc],))
                    ycnt[0] += 1
                    if ycnt[0] % 8 == 0:
                        cast_drip(c)
                    yield
                P.op("vector", lambda e: e.tensor_copy(
                    t3v(slot), xcp3(slot)[:, :, 512:515]),
                    reads=tuple(xcp_ct[slot]), writes=(tail3_rt[slot],))
                ycnt[0] += 1
                if ycnt[0] % 8 == 0:
                    cast_drip(c)
                yield
                for ct in range(DIB):
                    cb = convb[:, (ct % 2) * 512:(ct % 2 + 1) * 512]
                    base = slot * DIB * 515 + ct * 515
                    P.op("vector", lambda e, cb=cb, base=base, ct=ct: e.tensor_scalar(
                        cb, xcp[:, base:base + 512], convc[:, ct * 4:ct * 4 + 1],
                        0.0, ALU.mult, ALU.add),
                        reads=(xcp_ct[slot][ct], convc_t), writes=(convb_t,))
                    for kk in range(1, 4):
                        P.op("vector", lambda e, cb=cb, base=base, ct=ct, kk=kk:
                             e.scalar_tensor_tensor(
                                 cb, xcp[:, base + kk:base + kk + 512],
                                 convc[:, ct * 4 + kk:ct * 4 + kk + 1], cb,
                                 ALU.mult, ALU.add),
                             reads=(xcp_ct[slot][ct], convc_t, convb_t),
                             writes=(convb_t,))
                    P.op("scalar", lambda e, cb=cb, ct=ct, base=base: e.activation(
                        xcp[:, base + 3:base + 515], cb, AF.Silu),
                        reads=(convb_t, tail3_rt[slot]),
                        writes=(xcp_ct[slot][ct],))
                    ycnt[0] += 1
                    if ycnt[0] % 8 == 0:
                        cast_drip(c)
                    yield
                rd_all = tuple(xcp_ct[slot])
                def bcfn(e):
                    r = None
                    for kk in range(DIB):
                        r = e.matmul(pB[:, 0:512], wbc_sb[:, kk * 128:(kk + 1) * 128],
                                     xv(slot, kk), start=(kk == 0),
                                     stop=(kk == DIB - 1))
                    return r
                P.op("tensor", bcfn, reads=(wbc_sb_t,) + rd_all, writes=(pB_g[0],))
                P.op("vector", lambda e: e.tensor_copy(
                    Bn[:, slot * 512:(slot + 1) * 512], pB[0:64, 0:512]),
                    reads=(pB_g[0],), writes=(Bn_rt[slot],))
                P.op("vector", lambda e: e.tensor_copy(
                    Cn[:, slot * 512:(slot + 1) * 512], pB[64:128, 0:512]),
                    reads=(pB_g[0],), writes=(Cn_rt[slot],))
                ycnt[0] += 1
                if ycnt[0] % 8 == 0:
                    cast_drip(c)
                yield
                for m in range(NSC):
                    def bctfn(e, m=m):
                        r = None
                        for kk in range(DIB):
                            r = e.matmul(
                                pB[:, 512:640],
                                xv(slot, kk)[:, m * 128:(m + 1) * 128],
                                wbc_sb[:, kk * 128:(kk + 1) * 128],
                                start=(kk == 0), stop=(kk == DIB - 1))
                        return r
                    P.op("tensor", bctfn, reads=(wbc_sb_t,) + rd_all,
                         writes=(pB_g[1],))
                    P.op("vector", lambda e, m=m: e.tensor_copy(
                        bct[:, slot * 512 + m * 128:slot * 512 + (m + 1) * 128],
                        pB[:, 512:640]),
                        reads=(pB_g[1],), writes=(bct_rt[slot],))
                    ycnt[0] += 1
                    if ycnt[0] % 8 == 0:
                        cast_drip(c)
                    yield
                for q in range(NSC):
                    for half in range(2):
                        def trfn(e, q=q, half=half):
                            r = None
                            for j in range(8):
                                ct = half * 8 + j
                                r = e.transpose(
                                    pT[:, j * 128:(j + 1) * 128],
                                    xv(slot, ct)[:, q * 128:(q + 1) * 128],
                                    K["ident128"][:])
                            return r
                        P.op("tensor", trfn, reads=rd_all + (KT["ident128"],),
                             writes=(pT_t,))
                        P.op("vector", lambda e, q=q, half=half: e.tensor_copy(
                            xct[:, q * 2048 + half * 1024:q * 2048 + (half + 1) * 1024],
                            pT[:]), reads=(pT_t,), writes=(xct_qt[q],))
                        ycnt[0] += 1
                        if ycnt[0] % 8 == 0:
                            cast_drip(c)
                        yield

            def stageB(c):
                slot = c % 2
                hcv = hcm[:, slot * CDB:(slot + 1) * CDB]
                xhv = xh[:, slot * CDB:(slot + 1) * CDB]
                for q in range(NSC):
                    qg = c * NSC + q
                    qs = slice(q * 2048, (q + 1) * 2048)
                    sl = dtA[:, (qg % 2) * 2048:(qg % 2 + 1) * 2048]
                    slp = dtA[:, ((qg - 1) % 2) * 2048:(((qg - 1) % 2) + 1) * 2048]
                    P.op("vector", lambda e, qs=qs, q=q: e.tensor_tensor(
                        dtsub[:], xct[:, qs], wdt_bc[:], ALU.mult),
                        reads=(xct_qt[q], wdt_bc_t), writes=(dtsub_t,))
                    P.op("vector", lambda e: e.tensor_tensor(
                        dtsub[:], dtsub[:], dtb_bc[:], ALU.add),
                        reads=(dtsub_t, dtb_bc_t), writes=(dtsub_t,))
                    yield
                    P.op("scalar", lambda e: e.activation(
                        dtsub[:], dtsub[:], AF.Exp),
                        reads=(dtsub_t,), writes=(dtsub_t,))
                    P.op("scalar", lambda e: e.activation(
                        dtsub[:], dtsub[:], AF.Ln, bias=1.0),
                        reads=(dtsub_t,), writes=(dtsub_t,))
                    yield
                    P.op("vector", lambda e, sl=sl: e.tensor_tensor(
                        sl, dtsub[:], A_bc[:], ALU.mult),
                        reads=(dtsub_t, A_bc_t), writes=(dtA_tb[qg % 2],))
                    P.op("vector", lambda e, qs=qs, q=q: e.tensor_tensor(
                        usub[:], dtsub[:], xct[:, qs], ALU.mult),
                        reads=(dtsub_t, xct_qt[q]), writes=(usub_t,))
                    yield
                    if qg > 0:
                        def wfn(e, sl=sl, slp=slp):
                            r = None
                            for g in range(4):
                                e.matmul(pA[0:1, g * 512:(g + 1) * 512],
                                         K["wcols"][:, 0:1],
                                         slp[:, g * 512:(g + 1) * 512],
                                         start=True, stop=False)
                                r = e.matmul(pA[0:1, g * 512:(g + 1) * 512],
                                             K["wcols"][:, 1:2],
                                             sl[:, g * 512:(g + 1) * 512],
                                             start=False, stop=True)
                            return r
                        P.op("tensor", wfn,
                             reads=(dtA_tb[qg % 2], dtA_tb[(qg - 1) % 2], KT["wcols"]),
                             writes=tuple(pA_g))
                        P.op("scalar", lambda e: e.activation(
                            Wrow, pA[0:1, 0:2048], AF.Exp),
                            reads=tuple(pA_g), writes=(Et_t,))
                        def wbcfn(e):
                            r = None
                            for g in range(4):
                                r = e.matmul(pA[0:64, g * 512:(g + 1) * 512],
                                             K["ones_row64"][:],
                                             Wrow[0:1, g * 512:(g + 1) * 512],
                                             start=True, stop=True)
                            return r
                        P.op("tensor", wbcfn, reads=(Et_t, KT["ones_row64"]),
                             writes=tuple(pA_g))
                        P.op("vector", lambda e: e.tensor_tensor(
                            hst[:], tmpst[:], pA[0:64, 0:2048], ALU.mult),
                            reads=(tmpst_t,) + tuple(pA_g), writes=(hst_t,))
                        yield
                    def d1fn(e, sl=sl):
                        r = None
                        for g in range(4):
                            r = e.matmul(pA[:, g * 512:(g + 1) * 512], K["m_ref"][:],
                                         sl[:, g * 512:(g + 1) * 512],
                                         start=True, stop=True)
                        return r
                    P.op("tensor", d1fn, reads=(dtA_tb[qg % 2], KT["m_ref"]),
                         writes=tuple(pA_g))
                    P.op("vector", lambda e: e.tensor_scalar(
                        dcl[:], pA[:, 0:2048], -80.0, 80.0, ALU.max, ALU.min),
                        reads=tuple(pA_g), writes=(dcl_t,))
                    yield
                    P.op("scalar", lambda e: e.activation(Et[:], dcl[:], AF.Exp),
                         reads=(dcl_t,), writes=(Et_t,))
                    P.op("scalar", lambda e: e.activation(e1t[:], dcl[:], AF.Exp,
                                                          scale=-1.0),
                         reads=(dcl_t,), writes=(e1t_t,))
                    yield
                    P.op("vector", lambda e: e.tensor_tensor(
                        Ut[:], e1t[:], usub[:], ALU.mult),
                        reads=(e1t_t, usub_t), writes=(Ut_t,))
                    yield
                    P.op("tensor", lambda e, q=q: e.matmul(
                        pA[:, 0:128],
                        Bn[:, slot * 512 + q * 128:slot * 512 + (q + 1) * 128],
                        Cn[:, slot * 512 + q * 128:slot * 512 + (q + 1) * 128],
                        start=True, stop=True),
                        reads=(Bn_rt[slot], Cn_rt[slot]), writes=(pA_g[0],))
                    P.op("vector", lambda e: e.tensor_tensor(
                        Gm[:], pA[:, 0:128], K["tril"][:], ALU.mult),
                        reads=(pA_g[0], KT["tril"]), writes=(Gm_t,))
                    yield
                    def yfn(e, q=q):
                        r = None
                        for g in range(4):
                            e.matmul(pA[:, g * 512:(g + 1) * 512], Gm[:],
                                     Ut[:, g * 512:(g + 1) * 512],
                                     start=True, stop=False)
                            r = e.matmul(pA[:, g * 512:(g + 1) * 512],
                                         Cn[:, slot * 512 + q * 128:
                                             slot * 512 + (q + 1) * 128],
                                         hst[:, g * 512:(g + 1) * 512],
                                         start=False, stop=True)
                        return r
                    P.op("tensor", yfn, reads=(Gm_t, Ut_t, Cn_rt[slot], hst_t),
                         writes=tuple(pA_g))
                    P.op("vector", lambda e, q=q: e.tensor_tensor(
                        yt[:, q * 2048:(q + 1) * 2048], pA[:, 0:2048], Et[:],
                        ALU.mult), reads=tuple(pA_g) + (Et_t,), writes=(yt_t,))
                    yield
                    if qg < T // SQ - 1:
                        def dhfn(e, q=q):
                            r = None
                            for g in range(4):
                                e.matmul(pA[0:64, g * 512:(g + 1) * 512],
                                         bct[:, slot * 512 + q * 128:
                                             slot * 512 + q * 128 + 64],
                                         Ut[:, g * 512:(g + 1) * 512],
                                         start=True, stop=False)
                                r = e.matmul(pA[0:64, g * 512:(g + 1) * 512],
                                             K["ident128"][0:64, 0:64],
                                             hst[:, g * 512:(g + 1) * 512],
                                             start=False, stop=True)
                            return r
                        P.op("tensor", dhfn,
                             reads=(bct_rt[slot], Ut_t, hst_t, KT["ident128"]),
                             writes=tuple(pA_g))
                        P.op("scalar", lambda e: e.copy(
                            tmpst[:], pA[0:64, 0:2048]),
                            reads=tuple(pA_g), writes=(tmpst_t,))
                        yield
                for ct in range(DIB):
                    ys = ct % 2
                    pz = 16 + ct
                    wsl = pz % 2
                    wv = winp[:, wsl * 1024:(wsl + 1) * 1024]
                    P.dma("sync", lambda e, pz=pz, wv=wv: e.dma_start(
                        out=wv.rearrange("p (k f) -> p k f", f=128),
                        in_=scr_win[:, pz * 128:(pz + 1) * 128].rearrange(
                            "(k p) f -> p k f", p=128)),
                        reads=(scrwin_tb,), writes=(winp_tb[wsl],))
                    g = ct % 2
                    def zmm(e, wv=wv, g=g):
                        r = None
                        for kk in range(8):
                            r = e.matmul(
                                pB[:, g * 512:(g + 1) * 512],
                                wv[:, kk * 128:(kk + 1) * 128],
                                xhv[:, kk * 512:(kk + 1) * 512],
                                start=(kk == 0), stop=(kk == 7))
                        return r
                    P.op("tensor", zmm, reads=(winp_tb[wsl], xh_rt[slot]),
                         writes=(pB_g[g],))
                    P.op("scalar", lambda e, g=g, ys=ys: e.activation(
                        zbuf[:, ys * 512:(ys + 1) * 512],
                        pB[:, g * 512:(g + 1) * 512], AF.Silu),
                        reads=(pB_g[g],), writes=(zbuf_t,))
                    yield
                    def ytr(e, ct=ct):
                        r = None
                        for q in range(NSC):
                            r = e.transpose(
                                pT2[:, q * 128:(q + 1) * 128],
                                yt[:, q * 2048 + ct * 128:q * 2048 + (ct + 1) * 128],
                                K["ident128"][:])
                        return r
                    P.op("tensor", ytr, reads=(yt_t, KT["ident128"]), writes=(pT2_t,))
                    cb = convb[:, (2 + ys) * 512:(3 + ys) * 512]
                    P.op("vector", lambda e, ct=ct, cb=cb:
                         e.scalar_tensor_tensor(
                             cb, xv(slot, ct), dskipc[:, ct:ct + 1],
                             pT2[:, 0:512], ALU.mult, ALU.add),
                         reads=(xcp_ct[slot][ct], dskipc_t, pT2_t),
                         writes=(convb_t,))
                    yield
                    P.op("vector", lambda e, ct=ct, cb=cb, ys=ys: e.tensor_tensor(
                        Pf[:, ct * 512:(ct + 1) * 512], cb,
                        zbuf[:, ys * 512:(ys + 1) * 512], ALU.mult),
                        reads=(convb_t, zbuf_t), writes=(Pf_t,))
                    yield
                for m in range(DB):
                    g = m % 2
                    for hp in range(4):
                        wsl = hp % 2
                        wv = woutp[:, wsl * 512:(wsl + 1) * 512]
                        P.dma("gpsimd", lambda e, m=m, hp=hp, wv=wv: e.dma_start(
                            out=wv.rearrange("p (k d) -> p k d", d=128),
                            in_=scr_wout[hp * 512:(hp + 1) * 512,
                                         m * 128:(m + 1) * 128].rearrange(
                                "(k p) d -> p k d", p=128)),
                            reads=(scrwout_tb,), writes=(woutp_tb[wsl],))
                        def wofn(e, wv=wv, g=g, hp=hp):
                            r = None
                            for kk4 in range(4):
                                kk = hp * 4 + kk4
                                r = e.matmul(pB[:, g * 512:(g + 1) * 512],
                                             wv[:, kk4 * 128:(kk4 + 1) * 128],
                                             Pf[:, kk * 512:(kk + 1) * 512],
                                             start=(kk == 0), stop=(kk == DIB - 1))
                            return r
                        P.op("tensor", wofn, reads=(woutp_tb[wsl], Pf_t),
                             writes=(pB_g[g],))
                    P.op("vector", lambda e, m=m, g=g: e.scalar_tensor_tensor(
                        hcv[:, m * 512:(m + 1) * 512],
                        pB[:, g * 512:(g + 1) * 512], mskbc[:, 0:1],
                        hcv[:, m * 512:(m + 1) * 512], ALU.mult, ALU.add),
                        reads=(pB_g[g], mskbc_t, hcm_rt[slot]),
                        writes=(hcm_rt[slot],))
                    yield
                P.dma("gpsimd", lambda e: e.dma_start(
                    out=sup(scr_h)[:, c, :, :], in_=hcv),
                    reads=(hcm_rt[slot],), writes=(scrh_tb[c],))
                yield

            SEN = object()
            ga = stageA(0)
            for _ in ga:
                pass
            for c in range(T // 512):
                gb = stageB(c)
                ga = stageA(c + 1) if c + 1 < T // 512 else None
                done_b = done_a = ga is None
                done_b = False
                while not (done_b and done_a):
                    if not done_b:
                        done_b = next(gb, SEN) is SEN
                    if not done_a:
                        done_a = next(ga, SEN) is SEN
            while cast_jobs:
                dst, src2, tbw = cast_jobs.pop(0)
                P.dma("gpsimd", lambda e, dst=dst, src2=src2:
                      e.dma_start(out=dst, in_=src2), writes=(tbw,))

        if nphase >= 2:
            with ExitStack() as st:
                phase_mamba(st)
            P.barrier()

        def phase_tc(st):
            pA, _ = ps(st, "t_pA", (128, 2048))
            pA_g = [TB("tpAg%d" % i) for i in range(4)]
            pB, _ = ps(st, "t_pB", (128, 1024))
            pB_g = [TB("tpBg%d" % i) for i in range(2)]
            srow, srow_t = sb(st, "t_srow", (1, 16), F32)
            srowb, srowb_t = sb(st, "t_srowb", (1, 2), BF16)
            C_HI = 6.28125
            C_LO = 2.0 * np.pi - C_HI
            RND = 12582912.0
            P.dma("sync", lambda e: e.dma_start(
                out=srow[0:1, 0:1], in_=IN["L0_attn_tc_theta"][None, :]),
                writes=(srow_t,))
            P.op("vector", lambda e: e.tensor_scalar(
                srow[0:1, 1:2], srow[0:1, 0:1], 1.0, float(np.pi / 2),
                ALU.mult, ALU.add), reads=(srow_t,), writes=(srow_t,))
            for i, (src, red) in enumerate(((0, 4), (1, 5))):
                P.op("vector", lambda e, src=src, i=i: e.tensor_scalar(
                    srow[0:1, 2 + i:3 + i], srow[0:1, src:src + 1],
                    float(1.0 / (2 * np.pi)), RND, ALU.mult, ALU.add),
                    reads=(srow_t,), writes=(srow_t,))
                P.op("vector", lambda e, i=i: e.tensor_scalar(
                    srow[0:1, 2 + i:3 + i], srow[0:1, 2 + i:3 + i], 1.0, -RND,
                    ALU.mult, ALU.add), reads=(srow_t,), writes=(srow_t,))
                P.op("vector", lambda e, src=src, i=i, red=red:
                     e.scalar_tensor_tensor(
                         srow[0:1, red:red + 1], srow[0:1, 2 + i:3 + i],
                         -C_HI, srow[0:1, src:src + 1], ALU.mult, ALU.add),
                     reads=(srow_t,), writes=(srow_t,))
                P.op("vector", lambda e, i=i, red=red: e.scalar_tensor_tensor(
                    srow[0:1, red:red + 1], srow[0:1, 2 + i:3 + i], -C_LO,
                    srow[0:1, red:red + 1], ALU.mult, ALU.add),
                    reads=(srow_t,), writes=(srow_t,))
            P.op("scalar", lambda e: e.activation(srow[0:1, 6:8], srow[0:1, 4:6],
                                                  AF.Sin),
                 reads=(srow_t,), writes=(srow_t,))
            P.op("vector", lambda e: e.tensor_copy(srowb[:], srow[0:1, 6:8]),
                 reads=(srow_t,), writes=(srowb_t,))
            P.op("tensor", lambda e: e.matmul(pB[:, 0:2], K["ones_row128"][:],
                                              srowb[:], start=True, stop=True),
                 reads=(srowb_t, KT["ones_row128"]), writes=(pB_g[0],))
            msc, msc_t = sb(st, "t_msc", (128, 2), F32)
            P.op("vector", lambda e: e.tensor_scalar(
                msc[:, 0:1], pB[:, 0:1], mskbc[:, 1:2], 0.0, ALU.mult, ALU.add),
                reads=(pB_g[0], mskbc_t), writes=(msc_t,))
            P.op("vector", lambda e: e.tensor_scalar(
                msc[:, 1:2], pB[:, 1:2], mskbc[:, 1:2], 0.0, ALU.mult, ALU.add),
                reads=(pB_g[0], mskbc_t), writes=(msc_t,))
            CPS[0], CPS[1] = pB[:, 0:512], pB_g[0]
            tcb_c, tcb_c_t = sb(st, "t_tcbc", (128, DB), F32)
            load_cols(IN["L0_attn_tc_b"][None, :], D, tcb_c, tcb_c_t)

            hcm, hcm_t = sb(st, "t_hcm", (128, DB * 512), BF16)
            xh, xh_t = sb(st, "t_xh", (128, DB * 512), BF16)
            sqb, sqb_t = sb(st, "t_sqb", (128, 2 * 512), BF16)
            rows, rows_t = sb(st, "t_rows", (1, 4 * 512), F32)
            rowsb, rowsb_t = sb(st, "t_rowsb", (1, 2 * 512), BF16)
            Rsb, Rsb_t = sb(st, "t_Rsb", (128, 512), BF16)
            MRsb, MRsb_t = sb(st, "t_MRsb", (128, 512), BF16)
            t1f, t1f_t = sb(st, "t_t1f", (128, 512), F32)
            tnh, tnh_t = sb(st, "t_tnh", (128, 2 * 512), BF16)
            wp, wp_t = sb(st, "t_wp", (128, 2 * DB * 128), BF16)
            wp_tb = [TB("twp0"), TB("twp1")]

            def ln_pass(hcv, j, eps):
                for b in range(DB):
                    sv = sqb[:, (b % 2) * 512:(b % 2 + 1) * 512]
                    P.op("scalar", lambda e, hcv=hcv, b=b, sv=sv: e.activation(
                        sv, hcv[:, b * 512:(b + 1) * 512], AF.Square),
                        reads=(hcm_t,), writes=(sqb_t,))
                    P.op("tensor", lambda e, hcv=hcv, b=b: e.matmul(
                        pB[0:1, 0:512], K["ones_col"][:],
                        hcv[:, b * 512:(b + 1) * 512],
                        start=(b == 0), stop=(b == DB - 1)),
                        reads=(hcm_t, KT["ones_col"]), writes=(pB_g[0],))
                    P.op("tensor", lambda e, b=b, sv=sv: e.matmul(
                        pB[0:1, 512:1024], K["ones_col"][:], sv,
                        start=(b == 0), stop=(b == DB - 1)),
                        reads=(sqb_t, KT["ones_col"]), writes=(pB_g[1],))
                rm = rows[0:1, 0:512]
                rv = rows[0:1, 512:1024]
                rsd = rows[0:1, 1024:1536]
                rmr = rows[0:1, 1536:2048]
                P.op("vector", lambda e, rm=rm: e.tensor_single_scalar(
                    rm, pB[0:1, 0:512], 1.0 / D, ALU.mult),
                    reads=(pB_g[0],), writes=(rows_t,))
                P.op("vector", lambda e, rv=rv: e.tensor_scalar(
                    rv, pB[0:1, 512:1024], 1.0 / D, eps, ALU.mult, ALU.add),
                    reads=(pB_g[1],), writes=(rows_t,))
                P.op("vector", lambda e, rm=rm, rmr=rmr: e.tensor_tensor(
                    rmr, rm, rm, ALU.mult), reads=(rows_t,), writes=(rows_t,))
                P.op("vector", lambda e, rv=rv, rmr=rmr: e.tensor_tensor(
                    rv, rmr, rv, ALU.subtract), reads=(rows_t,), writes=(rows_t,))
                P.op("scalar", lambda e, rv=rv: e.activation(rv, rv, AF.Ln,
                                                             scale=-1.0),
                     reads=(rows_t,), writes=(rows_t,))
                P.op("scalar", lambda e, rv=rv, rsd=rsd: e.activation(
                    rsd, rv, AF.Exp, scale=-0.5),
                     reads=(rows_t,), writes=(rows_t,))
                P.op("vector", lambda e, rm=rm, rsd=rsd, rmr=rmr: e.tensor_tensor(
                    rmr, rm, rsd, ALU.mult), reads=(rows_t,), writes=(rows_t,))
                P.op("vector", lambda e, rsd=rsd: e.tensor_copy(
                    rowsb[0:1, 0:512], rsd), reads=(rows_t,), writes=(rowsb_t,))
                P.op("vector", lambda e, rmr=rmr: e.tensor_copy(
                    rowsb[0:1, 512:1024], rmr), reads=(rows_t,), writes=(rowsb_t,))
                P.op("tensor", lambda e: e.matmul(
                    pB[:, 0:512], K["ones_row128"][:], rowsb[0:1, 0:512],
                    start=True, stop=True),
                    reads=(rowsb_t, KT["ones_row128"]), writes=(pB_g[0],))
                P.op("tensor", lambda e: e.matmul(
                    pB[:, 512:1024], K["ones_row128"][:], rowsb[0:1, 512:1024],
                    start=True, stop=True),
                    reads=(rowsb_t, KT["ones_row128"]), writes=(pB_g[1],))
                P.op("vector", lambda e: e.tensor_copy(Rsb[:], pB[:, 0:512]),
                     reads=(pB_g[0],), writes=(Rsb_t,))
                P.op("vector", lambda e: e.tensor_copy(MRsb[:], pB[:, 512:1024]),
                     reads=(pB_g[1],), writes=(MRsb_t,))
                for b in range(DB):
                    tv = t1f[:, 0:512]
                    P.op("vector", lambda e, hcv=hcv, b=b, tv=tv: e.tensor_tensor(
                        tv, hcv[:, b * 512:(b + 1) * 512], Rsb[:], ALU.mult),
                        reads=(hcm_t, Rsb_t), writes=(t1f_t,))
                    P.op("vector", lambda e, tv=tv: e.tensor_tensor(
                        tv, tv, MRsb[:], ALU.subtract),
                        reads=(MRsb_t, t1f_t), writes=(t1f_t,))
                    if j is None:
                        P.op("vector", lambda e, b=b, tv=tv: e.tensor_copy(
                            xh[:, b * 512:(b + 1) * 512], tv),
                            reads=(t1f_t,), writes=(xh_t,))
                    else:
                        P.op("vector", lambda e, b=b, tv=tv, j=j: e.tensor_scalar(
                            xh[:, b * 512:(b + 1) * 512], tv,
                            lng[:, j * DB + b:j * DB + b + 1],
                            lnb[:, j * DB + b:j * DB + b + 1], ALU.mult, ALU.add),
                            reads=(t1f_t, lng_t, lnb_t), writes=(xh_t,))

            tcw = IN["L0_attn_tc_w"]
            for c in range(T // 512):
                hcv = hcm[:, 0:DB * 512]
                P.dma("sync", lambda e, hcv=hcv, c=c: e.dma_start(
                    out=hcv, in_=sup(scr_h)[:, c, :, :]),
                    reads=(scrh_tb[c],), writes=(hcm_t,))
                ln_pass(hcv, 1, EPS_IN)
                for m in range(DB):
                    slot = m % 2
                    wv = wp[:, slot * DB * 128:(slot + 1) * DB * 128]
                    P.dma("gpsimd", lambda e, m=m, wv=wv: e.dma_start(
                        out=wv.rearrange("p (a d) -> p a d", d=128),
                        in_=tcw[:, m * 128:(m + 1) * 128].rearrange(
                            "(a p) d -> p a d", p=128)), writes=(wp_tb[slot],))
                    g = m % 2
                    def tcmm(e, wv=wv, g=g):
                        r = None
                        for kk in range(DB):
                            r = e.matmul(pB[:, g * 512:(g + 1) * 512],
                                         wv[:, kk * 128:(kk + 1) * 128],
                                         xh[:, kk * 512:(kk + 1) * 512],
                                         start=(kk == 0), stop=(kk == DB - 1))
                        return r
                    P.op("tensor", tcmm, reads=(wp_tb[slot], xh_t), writes=(pB_g[g],))
                    tv = tnh[:, g * 512:(g + 1) * 512]
                    P.op("scalar", lambda e, g=g, m=m, tv=tv: e.activation(
                        tv, pB[:, g * 512:(g + 1) * 512], AF.Tanh,
                        bias=tcb_c[:, m:m + 1]),
                        reads=(pB_g[g], tcb_c_t), writes=(tnh_t,))
                    P.op("vector", lambda e, m=m, tv=tv, hcv=hcv:
                         e.scalar_tensor_tensor(
                             hcv[:, m * 512:(m + 1) * 512], tv, msc[:, 0:1],
                             hcv[:, m * 512:(m + 1) * 512], ALU.mult, ALU.add),
                         reads=(tnh_t, msc_t, hcm_t), writes=(hcm_t,))
                    P.op("vector", lambda e, m=m, hcv=hcv: e.scalar_tensor_tensor(
                        hcv[:, m * 512:(m + 1) * 512],
                        xh[:, m * 512:(m + 1) * 512], msc[:, 1:2],
                        hcv[:, m * 512:(m + 1) * 512], ALU.mult, ALU.add),
                        reads=(xh_t, msc_t, hcm_t), writes=(hcm_t,))
                P.dma("gpsimd", lambda e, hcv=hcv, c=c: e.dma_start(
                    out=sup(scr_h)[:, c, :, :], in_=hcv),
                    reads=(hcm_t,), writes=(scrh_tb[c],))
            return ln_pass

        if nphase >= 3:
            with ExitStack() as st:
                phase_tc(st)
            P.barrier()

        def make_ln(st, tag, g0ap, g0tb, g1ap, g1tb):
            sqb, sqb_t = sb(st, tag + "_sqb", (128, 2 * 512), BF16)
            rows, rows_t = sb(st, tag + "_rows", (1, 4 * 512), F32)
            rowsb, rowsb_t = sb(st, tag + "_rowsb", (1, 2 * 512), BF16)
            Rsb, Rsb_t = sb(st, tag + "_Rsb", (128, 512), BF16)
            MRsb, MRsb_t = sb(st, tag + "_MRsb", (128, 512), BF16)
            t1f, t1f_t = sb(st, tag + "_t1f", (128, 512), F32)

            def ln_pass(hcv, hc_tb, xhap, xh_tb, j, eps):
                for b in range(DB):
                    sv = sqb[:, (b % 2) * 512:(b % 2 + 1) * 512]
                    P.op("scalar", lambda e, hcv=hcv, b=b, sv=sv: e.activation(
                        sv, hcv[:, b * 512:(b + 1) * 512], AF.Square),
                        reads=(hc_tb,), writes=(sqb_t,))
                    P.op("tensor", lambda e, hcv=hcv, b=b: e.matmul(
                        g0ap[0:1, :], K["ones_col"][:],
                        hcv[:, b * 512:(b + 1) * 512],
                        start=(b == 0), stop=(b == DB - 1)),
                        reads=(hc_tb, KT["ones_col"]), writes=(g0tb,))
                    P.op("tensor", lambda e, b=b, sv=sv: e.matmul(
                        g1ap[0:1, :], K["ones_col"][:], sv,
                        start=(b == 0), stop=(b == DB - 1)),
                        reads=(sqb_t, KT["ones_col"]), writes=(g1tb,))
                rm = rows[0:1, 0:512]
                rv = rows[0:1, 512:1024]
                rsd = rows[0:1, 1024:1536]
                rmr = rows[0:1, 1536:2048]
                P.op("vector", lambda e, rm=rm: e.tensor_single_scalar(
                    rm, g0ap[0:1, :], 1.0 / D, ALU.mult),
                    reads=(g0tb,), writes=(rows_t,))
                P.op("vector", lambda e, rv=rv, eps=eps: e.tensor_scalar(
                    rv, g1ap[0:1, :], 1.0 / D, eps, ALU.mult, ALU.add),
                    reads=(g1tb,), writes=(rows_t,))
                P.op("vector", lambda e, rm=rm, rmr=rmr: e.tensor_tensor(
                    rmr, rm, rm, ALU.mult), reads=(rows_t,), writes=(rows_t,))
                P.op("vector", lambda e, rv=rv, rmr=rmr: e.tensor_tensor(
                    rv, rmr, rv, ALU.subtract), reads=(rows_t,), writes=(rows_t,))
                P.op("scalar", lambda e, rv=rv: e.activation(rv, rv, AF.Ln,
                                                             scale=-1.0),
                     reads=(rows_t,), writes=(rows_t,))
                P.op("scalar", lambda e, rv=rv, rsd=rsd: e.activation(
                    rsd, rv, AF.Exp, scale=-0.5),
                     reads=(rows_t,), writes=(rows_t,))
                P.op("vector", lambda e, rm=rm, rsd=rsd, rmr=rmr: e.tensor_tensor(
                    rmr, rm, rsd, ALU.mult), reads=(rows_t,), writes=(rows_t,))
                P.op("vector", lambda e, rsd=rsd: e.tensor_copy(
                    rowsb[0:1, 0:512], rsd), reads=(rows_t,), writes=(rowsb_t,))
                P.op("vector", lambda e, rmr=rmr: e.tensor_copy(
                    rowsb[0:1, 512:1024], rmr), reads=(rows_t,), writes=(rowsb_t,))
                P.op("tensor", lambda e: e.matmul(
                    g0ap[:, :], K["ones_row128"][:], rowsb[0:1, 0:512],
                    start=True, stop=True),
                    reads=(rowsb_t, KT["ones_row128"]), writes=(g0tb,))
                P.op("tensor", lambda e: e.matmul(
                    g1ap[:, :], K["ones_row128"][:], rowsb[0:1, 512:1024],
                    start=True, stop=True),
                    reads=(rowsb_t, KT["ones_row128"]), writes=(g1tb,))
                P.op("vector", lambda e: e.tensor_copy(Rsb[:], g0ap[:, :]),
                     reads=(g0tb,), writes=(Rsb_t,))
                P.op("vector", lambda e: e.tensor_copy(MRsb[:], g1ap[:, :]),
                     reads=(g1tb,), writes=(MRsb_t,))
                for b in range(DB):
                    tv = t1f[:, 0:512]
                    P.op("vector", lambda e, hcv=hcv, b=b, tv=tv: e.tensor_tensor(
                        tv, hcv[:, b * 512:(b + 1) * 512], Rsb[:], ALU.mult),
                        reads=(hc_tb, Rsb_t), writes=(t1f_t,))
                    P.op("vector", lambda e, tv=tv: e.tensor_tensor(
                        tv, tv, MRsb[:], ALU.subtract),
                        reads=(MRsb_t, t1f_t), writes=(t1f_t,))
                    if j is None:
                        P.op("vector", lambda e, b=b, tv=tv: e.tensor_copy(
                            xhap[:, b * 512:(b + 1) * 512], tv),
                            reads=(t1f_t,), writes=(xh_tb,))
                    else:
                        P.op("vector", lambda e, b=b, tv=tv, j=j: e.tensor_scalar(
                            xhap[:, b * 512:(b + 1) * 512], tv,
                            lng[:, j * DB + b:j * DB + b + 1],
                            lnb[:, j * DB + b:j * DB + b + 1], ALU.mult, ALU.add),
                            reads=(t1f_t, lng_t, lnb_t), writes=(xh_tb,))
            return ln_pass

        def phase_wlam(st):
            L = T // 2
            pS, _ = ps(st, "w_pS", (128, 2048))
            pS_g = [TB("wpSg%d" % i) for i in range(4)]
            pO = None
            pO_g = [TB("wpOg%d" % i) for i in range(4)]
            CPS[0], CPS[1] = pS[:, 0:512], pS_g[0]
            wlc, wlc_t = sb(st, "w_wlc", (128, DB * 4), F32)
            for kk in range(4):
                load_cols(IN["L0_attn_wlam_conv"][kk, :][None, :], D, wlc, wlc_t,
                          dst_cols=[b * 4 + kk for b in range(DB)])
            Ap, Ap_t = sb(st, "w_Ap", (128, DB * L), BF16)
            st1 = ExitStack()
            hcm, hcm_t = sb(st1, "w_hcm", (128, DB * 512), BF16)
            xh, xh_t = sb(st1, "w_xh", (128, DB * 512), BF16)
            ln = make_ln(st1, "w", pS[:, 0:512], pS_g[0], pS[:, 512:1024], pS_g[1])
            ddb, ddb_t = sb(st1, "w_ddb", (128, DB * 256), BF16)
            for s_i in range(T // 512):
                hcv = hcm[:, :]
                P.dma("sync", lambda e, hcv=hcv, s_i=s_i: e.dma_start(
                    out=hcv, in_=sup(scr_h)[:, s_i, :, :]),
                    reads=(scrh_tb[s_i],), writes=(hcm_t,))
                ln(hcv, hcm_t, xh, xh_t, 2, EPS_IN)
                for b in range(DB):
                    ev = xh.rearrange("p (b two l) -> p b two l", b=DB, two=256)
                    P.op("vector", lambda e, b=b, s_i=s_i, ev=ev: e.tensor_tensor(
                        Ap[:, b * L + s_i * 256:b * L + (s_i + 1) * 256],
                        ev[:, b, :, 0], ev[:, b, :, 1], ALU.add),
                        reads=(xh_t,), writes=(Ap_t,))
                    P.op("vector", lambda e, b=b, ev=ev: e.tensor_tensor(
                        ddb[:, b * 256:(b + 1) * 256],
                        ev[:, b, :, 0], ev[:, b, :, 1], ALU.subtract),
                        reads=(xh_t,), writes=(ddb_t,))
                    P.op("vector", lambda e, b=b: e.tensor_single_scalar(
                        ddb[:, b * 256:(b + 1) * 256],
                        ddb[:, b * 256:(b + 1) * 256], 0.5, ALU.mult),
                        reads=(ddb_t,), writes=(ddb_t,))
                P.dma("gpsimd", lambda e, s_i=s_i: e.dma_start(
                    out=scr_dp.rearrange("p (b l) -> p b l", b=DB)[
                        :, :, s_i * 256:(s_i + 1) * 256],
                    in_=ddb.rearrange("p (b l) -> p b l", b=DB)),
                    reads=(ddb_t,), writes=(scrdp_tb,))
            st1.close()
            qT, qT_t = sb(st, "w_qT", (128, DB * L), BF16)
            kT, kT_t = sb(st, "w_kT", (128, DB * L), BF16)
            vext, vext_t = sb(st, "w_vext", (128, 16 * NH * 65), BF16)
            wqp, wqp_t = sb(st, "w_wqp", (128, 2 * DB * 128), BF16)
            wqp_tb = [TB("wq0"), TB("wq1")]
            P.op("vector", lambda e: e.memset(
                vext.rearrange("p (kt h j) -> p kt h j", h=NH, j=65)[:, :, :, 64:65],
                1.0), writes=(vext_t,))
            for wsrc, dstap, dsttb in ((IN["L0_attn_w_q"], qT, qT_t),
                                       (IN["L0_attn_w_k"], kT, kT_t)):
                for m in range(DB):
                    slot = m % 2
                    wv = wqp[:, slot * DB * 128:(slot + 1) * DB * 128]
                    P.dma("gpsimd", lambda e, m=m, wv=wv, wsrc=wsrc: e.dma_start(
                        out=wv.rearrange("p (a d) -> p a d", d=128),
                        in_=wsrc[:, m * 128:(m + 1) * 128].rearrange(
                            "(a p) d -> p a d", p=128)), writes=(wqp_tb[slot],))
                    def qmm(e, wv=wv):
                        r = None
                        for kk in range(DB):
                            for n4 in range(4):
                                r = e.matmul(
                                    pS[:, n4 * 512:(n4 + 1) * 512],
                                    wv[:, kk * 128:(kk + 1) * 128],
                                    Ap[:, kk * L + n4 * 512:kk * L + (n4 + 1) * 512],
                                    start=(kk == 0), stop=(kk == DB - 1))
                        return r
                    P.op("tensor", qmm, reads=(wqp_tb[slot], Ap_t),
                         writes=tuple(pS_g))
                    P.op("scalar", lambda e, m=m, dstap=dstap: e.copy(
                        dstap[:, m * L:(m + 1) * L], pS[:, 0:2048]),
                        reads=tuple(pS_g), writes=(dsttb,))
            vex4 = vext.rearrange("p (kt h j) -> p kt h j", h=NH, j=65)
            vtmp, vtmp_t = sb(st, "w_vtmp", (128, L), BF16)
            stv = ExitStack()
            pTb, pTb_t = ps(stv, "w_pTb", (128, 1024), BF16)
            for m in range(DB):
                slot = m % 2
                wv = wqp[:, slot * DB * 128:(slot + 1) * DB * 128]
                P.dma("gpsimd", lambda e, m=m, wv=wv: e.dma_start(
                    out=wv.rearrange("p (a d) -> p a d", d=128),
                    in_=IN["L0_attn_w_v"][:, m * 128:(m + 1) * 128].rearrange(
                        "(a p) d -> p a d", p=128)), writes=(wqp_tb[slot],))
                def vmmf(e, wv=wv):
                    r = None
                    for kk in range(DB):
                        for n4 in range(4):
                            r = e.matmul(
                                pS[:, n4 * 512:(n4 + 1) * 512],
                                wv[:, kk * 128:(kk + 1) * 128],
                                Ap[:, kk * L + n4 * 512:kk * L + (n4 + 1) * 512],
                                start=(kk == 0), stop=(kk == DB - 1))
                    return r
                P.op("tensor", vmmf, reads=(wqp_tb[slot], Ap_t), writes=tuple(pS_g))
                P.op("scalar", lambda e: e.copy(vtmp[:], pS[:, 0:2048]),
                     reads=tuple(pS_g), writes=(vtmp_t,))
                for kt in range(16):
                    P.op("tensor", lambda e, kt=kt: e.transpose(
                        pTb[:, (kt % 8) * 128:((kt % 8) + 1) * 128],
                        vtmp[:, kt * 128:(kt + 1) * 128], K["ident128"][:]),
                        reads=(vtmp_t, KT["ident128"]), writes=(pTb_t,))
                    P.op("vector", lambda e, kt=kt, m=m: e.tensor_copy(
                        vex4[:, kt, 2 * m:2 * m + 2, 0:64],
                        pTb[:, (kt % 8) * 128:((kt % 8) + 1) * 128].rearrange(
                            "p (h j) -> p h j", j=64)),
                        reads=(pTb_t,), writes=(vext_t,))
            stv.close()
            pO, _ = ps(st, "w_pO", (128, 2048))
            Pb2, _pb = sb(st, "w_Pb2", (128, 2 * 2048), BF16)
            Pb2_tb = [TB("wPb0"), TB("wPb1")]
            recb, recb_t = sb(st, "w_recb", (1, 2048), BF16)
            recl, recl_t = sb(st, "w_recl", (1, 2048), BF16)
            ohb, ohb_t = sb(st, "w_ohb", (64, 2048), BF16)
            oTall, oTall_t = Ap, Ap_t
            for hh in range(NH):
                phh = 64 * (hh % 2)
                blk = hh // 2
                def emit_s(kt):
                    po = (kt % 2) * 0
                    for half in range(2):
                        def smm(e, hh=hh, phh=phh, blk=blk, kt=kt, half=half):
                            r = None
                            for n2 in range(2):
                                n4 = half * 2 + n2
                                r = e.matmul(
                                    pS[:, n4 * 512:(n4 + 1) * 512],
                                    kT[phh:phh + 64,
                                       blk * L + kt * 128:blk * L + (kt + 1) * 128],
                                    qT[phh:phh + 64,
                                       blk * L + n4 * 512:blk * L + (n4 + 1) * 512],
                                    start=True, stop=True)
                            return r
                        P.op("tensor", smm, reads=(kT_t, qT_t),
                             writes=(pS_g[half * 2], pS_g[half * 2 + 1]))
                        P.op("scalar", lambda e, half=half, kt=kt: e.activation(
                            Pb2[:, (kt % 2) * 2048 + half * 1024:
                                (kt % 2) * 2048 + (half + 1) * 1024],
                            pS[:, half * 1024:(half + 1) * 1024], AF.Exp,
                            scale=1.0 / 16.0),
                            reads=(pS_g[half * 2], pS_g[half * 2 + 1]),
                            writes=(Pb2_tb[kt % 2],))
                def emit_pv(kt):
                    def pvmm(e, hh=hh, kt=kt):
                        r = None
                        for n4 in range(4):
                            r = e.matmul(
                                pO[0:65, n4 * 512:(n4 + 1) * 512],
                                vext[:, kt * NH * 65 + hh * 65:
                                     kt * NH * 65 + (hh + 1) * 65],
                                Pb2[:, (kt % 2) * 2048 + n4 * 512:
                                    (kt % 2) * 2048 + (n4 + 1) * 512],
                                start=(kt == 0), stop=(kt == 15))
                        return r
                    P.op("tensor", pvmm, reads=(vext_t, Pb2_tb[kt % 2]),
                         writes=tuple(pO_g))
                emit_s(0)
                for kt in range(1, 16):
                    emit_s(kt)
                    emit_pv(kt - 1)
                emit_pv(15)
                P.op("scalar", lambda e: e.activation(recl[:], pO[64:65, 0:2048],
                                                      AF.Ln),
                     reads=tuple(pO_g), writes=(recl_t,))
                P.op("scalar", lambda e: e.activation(recb[:], recl[:], AF.Exp,
                                                      scale=-1.0),
                     reads=(recl_t,), writes=(recb_t,))
                P.op("scalar", lambda e: e.copy(ohb[:], pO[0:64, 0:2048]),
                     reads=tuple(pO_g), writes=(ohb_t,))
                def rbc(e):
                    r = None
                    for n4 in range(4):
                        r = e.matmul(pO[0:64, n4 * 512:(n4 + 1) * 512],
                                     K["ones_row64"][:],
                                     recb[0:1, n4 * 512:(n4 + 1) * 512],
                                     start=True, stop=True)
                    return r
                P.op("tensor", rbc, reads=(recb_t, KT["ones_row64"]),
                     writes=tuple(pO_g))
                P.op("vector", lambda e, phh=phh, blk=blk: e.tensor_tensor(
                    oTall[phh:phh + 64, blk * L:(blk + 1) * L], ohb[:],
                    pO[0:64, 0:2048], ALU.mult),
                    reads=(ohb_t,) + tuple(pO_g), writes=(oTall_t,))
            dpb, dpb_t = sb(st, "w_dpb", (128, L), BF16)
            dph, dph_t = dpb, dpb_t
            ystg, ystg_t = sb(st, "w_ystg", (128, T), BF16)
            for m in range(DB):
                slot = m % 2
                wv = wqp[:, slot * DB * 128:(slot + 1) * DB * 128]
                P.dma("gpsimd", lambda e, m=m, wv=wv: e.dma_start(
                    out=wv.rearrange("p (a d) -> p a d", d=128),
                    in_=IN["L0_attn_w_o"][:, m * 128:(m + 1) * 128].rearrange(
                        "(a p) d -> p a d", p=128)), writes=(wqp_tb[slot],))
                def womm(e, wv=wv):
                    r = None
                    for kk in range(DB):
                        for n4 in range(4):
                            r = e.matmul(
                                pS[:, n4 * 512:(n4 + 1) * 512],
                                wv[:, kk * 128:(kk + 1) * 128],
                                oTall[:, kk * L + n4 * 512:kk * L + (n4 + 1) * 512],
                                start=(kk == 0), stop=(kk == DB - 1))
                    return r
                P.op("tensor", womm, reads=(wqp_tb[slot], oTall_t),
                     writes=tuple(pS_g))
                P.dma("sync", lambda e, m=m: e.dma_start(
                    out=dpb[:], in_=scr_dp[:, m * L:(m + 1) * L]),
                    reads=(scrdp_tb,), writes=(dpb_t,))
                ys2 = ystg.rearrange("p (l two) -> p l two", two=2)
                P.op("vector", lambda e, ys2=ys2: e.scalar_tensor_tensor(
                    ys2[:, :, 0], pS[:, 0:2048], 0.5, dph[:], ALU.mult, ALU.add),
                    reads=tuple(pS_g) + (dph_t,), writes=(ystg_t,))
                P.op("vector", lambda e, ys2=ys2: e.scalar_tensor_tensor(
                    ys2[:, :, 1], pS[:, 0:2048], 0.5, dph[:], ALU.mult,
                    ALU.subtract), reads=tuple(pS_g) + (dph_t,), writes=(ystg_t,))
                P.dma("gpsimd", lambda e, m=m: e.dma_start(
                    out=scr_y[:, m * T:(m + 1) * T], in_=ystg[:]),
                    reads=(ystg_t,), writes=(scry_tb,))
            ywin, ywin_t = sb(st, "w_ywin", (128, 520), BF16)
            cacc, cacc_t = sb(st, "w_cacc", (128, 512), F32)
            hcm2, hcm2_t = sb(st, "w_hcm2", (128, DB * 512), BF16)
            hcm, hcm_t = hcm2, hcm2_t
            for s_i in range(T // 512):
                hcv = hcm[:, :]
                P.dma("sync", lambda e, hcv=hcv, s_i=s_i: e.dma_start(
                    out=hcv, in_=sup(scr_h)[:, s_i, :, :]),
                    reads=(scrh_tb[s_i],), writes=(hcm_t,))
                for b in range(DB):
                    lo = s_i * 512 - 2
                    j0 = max(0, -lo)
                    j1 = min(515, T - lo)
                    if j0 > 0 or j1 < 515:
                        P.op("vector", lambda e: e.memset(ywin[:], 0.0),
                             writes=(ywin_t,))
                    P.dma("sync", lambda e, b=b, lo=lo, j0=j0, j1=j1: e.dma_start(
                        out=ywin[:, j0:j1],
                        in_=scr_y[:, b * T + lo + j0:b * T + lo + j1]),
                        reads=(ywin_t, scry_tb), writes=(ywin_t,))
                    P.op("vector", lambda e, b=b: e.tensor_scalar(
                        cacc[:], ywin[:, 0:512], wlc[:, b * 4:b * 4 + 1], 0.0,
                        ALU.mult, ALU.add),
                        reads=(ywin_t, wlc_t), writes=(cacc_t,))
                    for kk in range(1, 4):
                        P.op("vector", lambda e, b=b, kk=kk: e.scalar_tensor_tensor(
                            cacc[:], ywin[:, kk:kk + 512],
                            wlc[:, b * 4 + kk:b * 4 + kk + 1], cacc[:],
                            ALU.mult, ALU.add),
                            reads=(ywin_t, wlc_t, cacc_t), writes=(cacc_t,))
                    P.op("vector", lambda e, b=b, hcv=hcv: e.scalar_tensor_tensor(
                        hcv[:, b * 512:(b + 1) * 512], cacc[:], mskbc[:, 2:3],
                        hcv[:, b * 512:(b + 1) * 512], ALU.mult, ALU.add),
                        reads=(cacc_t, mskbc_t, hcm_t), writes=(hcm_t,))
                P.dma("gpsimd", lambda e, hcv=hcv, s_i=s_i: e.dma_start(
                    out=sup(scr_h)[:, s_i, :, :], in_=hcv),
                    reads=(hcm_t,), writes=(scrh_tb[s_i],))

        if nphase >= 4:
            with ExitStack() as st:
                phase_wlam(st)
            P.barrier()

        def phase_moe(st):
            GELU_C3 = 0.044715
            GELU_2S = 2.0 * 0.7978845608028654
            GT, GT_t = sb(st, "e_GT", (128, 32 * NE), F32)
            gfm, gfm_t = sb(st, "e_gfm", (8, 512), BF16)
            gfb, gfb_t = sb(st, "e_gfb", (8, 1024), BF16)
            b2b, b2b_t = sb(st, "e_b2b", (8, D), BF16)
            b1c, b1c_t = sb(st, "e_b1c", (128, NE * 32), F32)
            yac, yac_t = sb(st, "e_yac", (128, 8 * D), F32)
            pMA, _ = ps(st, "e_pMA", (128, 2048))
            pMA_g = [TB("epMAg%d" % i) for i in range(4)]
            pMB, _ = ps(st, "e_pMB", (128, 2048))
            pMB_g = [TB("epMBg%d" % i) for i in range(4)]
            pall_g = pMA_g + pMB_g
            pGa, pGa_t = pMA[:, 0:512], pMA_g[0]
            pGb, pGb_t = pMA[:, 512:1024], pMA_g[1]
            pGt, pGt_ts = pMB[:, 0:512], (pMB_g[0],)
            CPS[0], CPS[1] = pGa, pGa_t
            idf, idf_t = sb(st, "e_idf", (128, 128), F32)
            P.op("vector", lambda e: e.tensor_copy(idf[:], K["ident128"][:]),
                 reads=(KT["ident128"],), writes=(idf_t,))
            hcm, hcm_t = sb(st, "e_hcm", (128, DB * 512), BF16)
            xh, xh_t = sb(st, "e_xh", (128, DB * 512), BF16)
            x0b, x0b_t = sb(st, "e_x0b", (128, DB * 512), BF16)
            ln = make_ln(st, "e", pGa, pGa_t, pGb, pGb_t)
            wgb, wgb_t = sb(st, "e_wgb", (128, DB * NE), BF16)
            lgs, lgs_t = sb(st, "e_lgs", (128, 64), F32)
            xT, xT_t = sb(st, "e_xT", (128, DB * 1024), BF16)
            hact, hact_t = sb(st, "e_hact", (128, 32 * 1024), BF16)
            hact_kt = [TB("ehact%d" % k) for k in range(32)]
            w1p, w1p_t = sb(st, "e_w1p", (128, 2 * 8 * 512), BF16)
            w1p_tb = [TB("ew10"), TB("ew11")]
            w2p, w2p_t = sb(st, "e_w2p", (128, 2 * 4 * 256), BF16)
            gbt, gbt_t = sb(st, "e_gbt", (128, 1024), BF16)
            gyt, gyt_t = sb(st, "e_gyt", (128, 512), F32)
            w2p_tb = [TB("ew20"), TB("ew21")]
            gx, gx_t = sb(st, "e_gx", (128, 1024), BF16)
            gq, gq_t = sb(st, "e_gq", (128, 1024), BF16)
            gt2, gt2_t = gq, gq_t
            P.dma("gpsimd", lambda e: e.dma_start(out=b2b[:],
                                                  in_=IN["L0_attn_moe_b2"]),
                  writes=(b2b_t,))
            P.dma("gpsimd", lambda e: e.dma_start(
                out=wgb.rearrange("p (a j) -> p a j", j=NE),
                in_=IN["L0_attn_moe_wg"].rearrange("(a p) j -> p a j", p=128)),
                writes=(wgb_t,))
            b1r, b1r_t = sb(st, "e_b1r", (32, 128), F32)
            for ee in range(NE):
                P.dma("sync", lambda e, ee=ee: e.dma_start(
                    out=b1r[:], in_=IN["L0_attn_moe_b1"][ee, :].rearrange(
                        "(a f) -> a f", f=128)), writes=(b1r_t,))
                P.op("tensor", lambda e: e.transpose(
                    pGt[0:128, 0:32], b1r[:], idf[0:32, 0:32]),
                    reads=(b1r_t, idf_t), writes=pGt_ts)
                P.op("vector", lambda e, ee=ee: e.tensor_copy(
                    b1c[:, ee * 32:(ee + 1) * 32], pGt[0:128, 0:32]),
                    reads=pGt_ts, writes=(b1c_t,))

            for s_i in range(T // 512):
                P.dma("sync", lambda e, s_i=s_i: e.dma_start(
                    out=hcm[:], in_=sup(scr_h)[:, s_i, :, :]),
                    reads=(scrh_tb[s_i],), writes=(hcm_t,))
                ln(hcm, hcm_t, xh, xh_t, 3, EPS_IN)
                P.dma("gpsimd", lambda e, s_i=s_i: e.dma_start(
                    out=sup(scr_x3)[:, s_i, :, :],
                    in_=xh.rearrange("p (b t) -> p b t", b=DB)),
                    reads=(xh_t,), writes=(scrx3_tb,))
                for tt in range(4):
                    tg = s_i * 4 + tt
                    def lgmm(e, tt=tt):
                        r = None
                        for kk in range(DB):
                            r = e.matmul(
                                pGa[:, 0:NE],
                                xh.rearrange("p (b t) -> p b t", b=DB)[
                                    :, kk, tt * 128:(tt + 1) * 128],
                                wgb[:, kk * NE:(kk + 1) * NE],
                                start=(kk == 0), stop=(kk == DB - 1))
                        return r
                    P.op("tensor", lgmm, reads=(xh_t, wgb_t), writes=(pGa_t,))
                    lg = lgs[:, 0:8]
                    m1c = lgs[:, 8:9]
                    m2c = lgs[:, 9:10]
                    dcol = lgs[:, 10:11]
                    g1c = lgs[:, 11:12]
                    g2c = lgs[:, 12:13]
                    eq1 = lgs[:, 16:24]
                    eq2 = lgs[:, 24:32]
                    l2 = lgs[:, 32:40]
                    ga = lgs[:, 40:48]
                    P.op("vector", lambda e, lg=lg: e.tensor_copy(lg, pGa[:, 0:NE]),
                         reads=(pGa_t,), writes=(lgs_t,))
                    P.op("vector", lambda e, lg=lg, m1c=m1c: e.reduce_max(
                        m1c, lg, axis=AX.X), reads=(lgs_t,), writes=(lgs_t,))
                    P.op("vector", lambda e, lg=lg, m1c=m1c, eq1=eq1:
                         e.tensor_scalar(eq1, lg, m1c, 0.0, ALU.is_ge, ALU.add),
                         reads=(lgs_t,), writes=(lgs_t,))
                    P.op("vector", lambda e, eq1=eq1, lg=lg, l2=l2:
                         e.scalar_tensor_tensor(l2, eq1, -1e30, lg,
                                                ALU.mult, ALU.add),
                         reads=(lgs_t,), writes=(lgs_t,))
                    P.op("vector", lambda e, l2=l2, m2c=m2c: e.reduce_max(
                        m2c, l2, axis=AX.X), reads=(lgs_t,), writes=(lgs_t,))
                    P.op("vector", lambda e, m1c=m1c, m2c=m2c, dcol=dcol:
                         e.tensor_tensor(dcol, m1c, m2c, ALU.subtract),
                         reads=(lgs_t,), writes=(lgs_t,))
                    P.op("scalar", lambda e, dcol=dcol, g1c=g1c: e.activation(
                        g1c, dcol, AF.Sigmoid), reads=(lgs_t,), writes=(lgs_t,))
                    P.op("vector", lambda e, g1c=g1c, g2c=g2c: e.tensor_scalar(
                        g2c, g1c, -1.0, 1.0, ALU.mult, ALU.add),
                        reads=(lgs_t,), writes=(lgs_t,))
                    P.op("vector", lambda e, lg=lg, m2c=m2c, eq2=eq2:
                         e.tensor_scalar(eq2, lg, m2c, 0.0, ALU.is_ge, ALU.add),
                         reads=(lgs_t,), writes=(lgs_t,))
                    P.op("vector", lambda e, eq2=eq2, eq1=eq1: e.tensor_tensor(
                        eq2, eq2, eq1, ALU.subtract),
                        reads=(lgs_t,), writes=(lgs_t,))
                    P.op("vector", lambda e, eq1=eq1, g1c=g1c, ga=ga:
                         e.tensor_scalar(ga, eq1, g1c, 0.0, ALU.mult, ALU.add),
                         reads=(lgs_t,), writes=(lgs_t,))
                    P.op("vector", lambda e, eq2=eq2, g2c=g2c, ga=ga, tg=tg:
                         e.scalar_tensor_tensor(
                             GT[:, tg * NE:(tg + 1) * NE], eq2, g2c, ga,
                             ALU.mult, ALU.add),
                         reads=(lgs_t,), writes=(GT_t,))
            for grp in range(8):
                def gtr(e, grp=grp):
                    r = None
                    for j in range(4):
                        tg = grp * 4 + j
                        r = e.transpose(pGt[0:8, j * 128:(j + 1) * 128],
                                        GT[:, tg * NE:(tg + 1) * NE], idf[:])
                    return r
                P.op("tensor", gtr, reads=(GT_t, idf_t), writes=pGt_ts)
                P.op("vector", lambda e: e.tensor_copy(
                    gfm[0:8, 0:512], pGt[0:8, 0:512]),
                    reads=pGt_ts, writes=(gfm_t,))
                P.dma("gpsimd", lambda e, grp=grp: e.dma_start(
                    out=scr_gf[0:8, grp * 512:(grp + 1) * 512], in_=gfm[0:8, :]),
                    reads=(gfm_t,), writes=(scrgf_tb,))

            w1src = scr_w1
            w2src = scr_w2
            for big in range(4):
                P.dma("sync", lambda e, big=big: e.dma_start(
                    out=gfb[0:8, :],
                    in_=scr_gf[0:8, big * 1024:(big + 1) * 1024]),
                    reads=(scrgf_tb,), writes=(gfb_t,))
                for sh in range(2):
                    P.dma("sync", lambda e, big=big, sh=sh: e.dma_start(
                        out=xT.rearrange("p (b t) -> p b t", b=DB)[
                            :, :, sh * 512:(sh + 1) * 512],
                        in_=sup(scr_x3)[:, 2 * big + sh, :, :]),
                        reads=(scrx3_tb,), writes=(xT_t,))
                for ee in range(NE):
                    for kk in range(32):
                        if kk % 4 == 0 and not (kk == 0 and ee > 0):
                            slot = (kk // 4) % 2
                            wv = w1p[:, slot * 4096:(slot + 1) * 4096]
                            P.dma("gpsimd", lambda e, ee=ee, kk=kk, wv=wv:
                                  e.dma_start(
                                      out=wv.rearrange("p (a f) -> p a f", f=512),
                                      in_=w1src[ee * D:(ee + 1) * D,
                                                kk * 128:(kk + 4) * 128]
                                      .rearrange("(a p) f -> p a f", p=128)),
                                  reads=(scrw1_tb,), writes=(w1p_tb[slot],))
                        slot = (kk // 4) % 2
                        wv = w1p[:, slot * 4096:(slot + 1) * 4096]
                        gg = (kk % 2) * 2
                        def m1fn(e, wv=wv, kk=kk, gg=gg):
                            r = None
                            for kp in range(DB):
                                for n2 in range(2):
                                    r = e.matmul(
                                        pMA[:, (gg + n2) * 512:(gg + n2 + 1) * 512],
                                        wv[:, kp * 512 + (kk % 4) * 128:
                                           kp * 512 + (kk % 4 + 1) * 128],
                                        xT[:, kp * 1024 + n2 * 512:
                                           kp * 1024 + (n2 + 1) * 512],
                                        start=(kp == 0), stop=(kp == DB - 1))
                            return r
                        P.op("tensor", m1fn, reads=(w1p_tb[slot], xT_t),
                             writes=(pMA_g[gg], pMA_g[gg + 1]))
                        pin = pMA[:, gg * 512:(gg + 2) * 512]
                        pgs = (pMA_g[gg], pMA_g[gg + 1])
                        bcol = b1c[:, ee * 32 + kk:ee * 32 + kk + 1]
                        P.op("scalar", lambda e, pin=pin, bcol=bcol: e.activation(
                            gx[:], pin, AF.Identity, bias=bcol),
                            reads=pgs + (b1c_t,), writes=(gx_t,))
                        P.op("scalar", lambda e, pin=pin, bcol=bcol: e.activation(
                            gq[:], pin, AF.Square, bias=bcol),
                            reads=pgs + (b1c_t,), writes=(gq_t,))
                        P.op("vector", lambda e: e.tensor_scalar(
                            gt2[:], gq[:], GELU_C3, 1.0, ALU.mult, ALU.add),
                            reads=(gq_t,), writes=(gt2_t,))
                        P.op("vector", lambda e: e.tensor_tensor(
                            gt2[:], gt2[:], gx[:], ALU.mult),
                            reads=(gt2_t, gx_t), writes=(gt2_t,))
                        P.op("scalar", lambda e: e.activation(
                            gt2[:], gt2[:], AF.Sigmoid, scale=GELU_2S),
                            reads=(gt2_t,), writes=(gt2_t,))
                        P.op("vector", lambda e, kk=kk: e.tensor_tensor(
                            hact[:, kk * 1024:(kk + 1) * 1024], gt2[:], gx[:],
                            ALU.mult), reads=(gt2_t, gx_t),
                            writes=(hact_kt[kk],))
                    if ee + 1 < NE:
                        wv0 = w1p[:, 0:4096]
                        P.dma("gpsimd", lambda e, ee=ee, wv0=wv0: e.dma_start(
                            out=wv0.rearrange("p (a f) -> p a f", f=512),
                            in_=w1src[(ee + 1) * D:(ee + 2) * D, 0:512]
                            .rearrange("(a p) f -> p a f", p=128)),
                            reads=(scrw1_tb,), writes=(w1p_tb[0],))
                    def gbtmm(e, ee=ee, big=big):
                        r = None
                        for n2 in range(2):
                            r = e.matmul(
                                pMB[:, n2 * 512:(n2 + 1) * 512],
                                K["sel8"][:, ee * 128:(ee + 1) * 128],
                                gfb[0:8, n2 * 512:(n2 + 1) * 512],
                                start=True, stop=True)
                        return r
                    P.op("tensor", gbtmm, reads=(KT["sel8"], gfb_t),
                         writes=(pMB_g[0], pMB_g[1]))
                    P.op("vector", lambda e: e.tensor_copy(
                        gbt[:], pMB[:, 0:1024]),
                        reads=(pMB_g[0], pMB_g[1]), writes=(gbt_t,))
                    for dp in range(4):
                        ppp = pMB if dp % 2 == 0 else pMA
                        ppg = pMB_g if dp % 2 == 0 else pMA_g
                        for kkg in range(8):
                            slot = kkg % 2
                            wv = w2p[:, slot * 1024:(slot + 1) * 1024]
                            P.dma("sync" if kkg % 2 == 0 else "gpsimd",
                                  lambda e, ee=ee, kkg=kkg, dp=dp,
                                  wv=wv: e.dma_start(
                                      out=wv.rearrange("p (a f) -> p a f", f=256),
                                      in_=w2src[ee * FF + kkg * 512:
                                                ee * FF + (kkg + 1) * 512,
                                                dp * 256:(dp + 1) * 256]
                                      .rearrange("(a p) f -> p a f", p=128)),
                                  reads=(scrw2_tb,), writes=(w2p_tb[slot],))
                            def m2fn(e, wv=wv, kkg=kkg, ppp=ppp):
                                r = None
                                for kl in range(4):
                                    kk = kkg * 4 + kl
                                    for dmb2 in range(2):
                                        for th in range(2):
                                            gidx = dmb2 * 2 + th
                                            r = e.matmul(
                                                ppp[:, gidx * 512:
                                                    (gidx + 1) * 512],
                                                wv[:, kl * 256 + dmb2 * 128:
                                                   kl * 256 + (dmb2 + 1) * 128],
                                                hact[:, kk * 1024 + th * 512:
                                                     kk * 1024 + (th + 1) * 512],
                                                start=(kk == 0), stop=(kk == 31))
                                return r
                            P.op("tensor", m2fn,
                                 reads=(w2p_tb[slot],)
                                 + tuple(hact_kt[kkg * 4:kkg * 4 + 4]),
                                 writes=tuple(ppg))
                        for dmb2 in range(2):
                            for th in range(2):
                                gidx = dmb2 * 2 + th
                                dmb = dp * 2 + dmb2
                                yv = yac[:, dmb * 1024 + th * 512:
                                         dmb * 1024 + (th + 1) * 512]
                                P.op("vector", lambda e, gidx=gidx, th=th,
                                     ppp=ppp: e.tensor_tensor(
                                         gyt[:], ppp[:, gidx * 512:
                                                     (gidx + 1) * 512],
                                         gbt[:, th * 512:(th + 1) * 512],
                                         ALU.mult),
                                     reads=(ppg[gidx], gbt_t),
                                     writes=(gyt_t,))
                                if ee == 0:
                                    P.op("vector", lambda e, yv=yv:
                                         e.tensor_copy(yv, gyt[:]),
                                         reads=(gyt_t,), writes=(yac_t,))
                                else:
                                    P.op("vector", lambda e, yv=yv:
                                         e.tensor_tensor(
                                             yv, yv, gyt[:], ALU.add),
                                         reads=(gyt_t, yac_t),
                                         writes=(yac_t,))
                for dmb in range(DB):
                    def b2mm(e, dmb=dmb, big=big):
                        r = None
                        for n2 in range(2):
                            r = e.matmul(
                                pMA[:, n2 * 512:(n2 + 1) * 512],
                                b2b[0:8, dmb * 128:(dmb + 1) * 128],
                                gfb[0:8, n2 * 512:(n2 + 1) * 512],
                                start=True, stop=True)
                        return r
                    P.op("tensor", b2mm, reads=(gfb_t, b2b_t),
                         writes=(pMA_g[0], pMA_g[1]))
                    P.op("vector", lambda e, dmb=dmb: e.tensor_tensor(
                        yac[:, dmb * 1024:(dmb + 1) * 1024],
                        yac[:, dmb * 1024:(dmb + 1) * 1024],
                        pMA[:, 0:1024], ALU.add),
                        reads=(pMA_g[0], pMA_g[1], yac_t), writes=(yac_t,))
                for shalf in range(2):
                    s_i = big * 2 + shalf
                    P.dma("sync", lambda e, s_i=s_i: e.dma_start(
                        out=hcm[:], in_=sup(scr_h)[:, s_i, :, :]),
                        reads=(scrh_tb[s_i],), writes=(hcm_t,))
                    P.dma("sync", lambda e, s_i=s_i: e.dma_start(
                        out=x0b[:], in_=sup(scr_x0fm)[:, s_i, :, :]),
                        reads=(scrx0_tb,), writes=(x0b_t,))
                    for b in range(DB):
                        P.op("vector", lambda e, b=b, shalf=shalf:
                             e.scalar_tensor_tensor(
                                 xh[:, b * 512:(b + 1) * 512],
                                 yac[:, b * 1024 + shalf * 512:
                                     b * 1024 + (shalf + 1) * 512],
                                 mskbc[:, 3:4],
                                 hcm[:, b * 512:(b + 1) * 512],
                                 ALU.mult, ALU.add),
                             reads=(yac_t, mskbc_t, hcm_t),
                             writes=(xh_t,))
                    ln(xh, xh_t, xh, xh_t, None, EPS_OUT)
                    for b in range(DB):
                        P.op("vector", lambda e, b=b: e.tensor_tensor(
                            xh[:, b * 512:(b + 1) * 512],
                            xh[:, b * 512:(b + 1) * 512],
                            x0b[:, b * 512:(b + 1) * 512], ALU.add),
                            reads=(x0b_t, xh_t), writes=(xh_t,))
                    P.dma("gpsimd", lambda e, s_i=s_i: e.dma_start(
                        out=sup(scr_h)[:, s_i, :, :], in_=xh[:]),
                        reads=(xh_t,), writes=(scrh_tb[s_i],))

        if nphase >= 5:
            with ExitStack() as st:
                phase_moe(st)
            P.barrier()

        def phase_ffn(st):
            pF, _ = ps(st, "f_pF", (128, 2048))
            pF_g = [TB("fpFg%d" % i) for i in range(4)]
            pG, _ = ps(st, "f_pG", (128, 1024))
            pG_g = [TB("fpGg%d" % i) for i in range(2)]
            pT, pT_t = ps(st, "f_pT", (128, 1024), BF16)
            CPS[0], CPS[1] = pG[:, 0:512], pG_g[0]
            muc, muc_t = sb(st, "f_muc", (128, 4 * DB), F32)
            load_cols(IN["L0_ffn_mu_r"][None, :], D, muc, muc_t,
                      dst_cols=list(range(DB)))
            load_cols(IN["L0_ffn_mu_k"][None, :], D, muc, muc_t,
                      dst_cols=list(range(DB, 2 * DB)))
            P.op("vector", lambda e: e.tensor_scalar(
                muc[:, 2 * DB:4 * DB], muc[:, 0:2 * DB], -1.0, 1.0,
                ALU.mult, ALU.add), reads=(muc_t,), writes=(muc_t,))
            wT = {}
            wtmp, wtmp_t = sb(st, "f_wtmp", (128, DB * 128), F32)
            for nm, key in (("r", "L0_ffn_W_r"), ("k", "L0_ffn_W_k"),
                            ("v", "L0_ffn_W_v")):
                wT[nm] = sb(st, "f_wT" + nm, (128, DB * DB * 128), BF16)
                for m in range(DB):
                    P.dma("sync", lambda e, key=key, m=m: e.dma_start(
                        out=wtmp.rearrange("p (a d) -> p a d", d=128),
                        in_=IN[key][:, m * 128:(m + 1) * 128].rearrange(
                            "(a p) d -> p a d", p=128)),
                        reads=(wtmp_t,), writes=(wtmp_t,))
                    for a in range(DB):
                        P.op("tensor", lambda e, a=a: e.transpose(
                            pF[:, (a % 4) * 512:(a % 4) * 512 + 128],
                            wtmp[:, a * 128:(a + 1) * 128], idfT[:]),
                            reads=(wtmp_t, idfT_t), writes=(pF_g[a % 4],))
                        P.op("vector", lambda e, nm=nm, m=m, a=a: e.tensor_copy(
                            wT[nm][0][:, (m * DB + a) * 128:(m * DB + a + 1) * 128],
                            pF[:, (a % 4) * 512:(a % 4) * 512 + 128]),
                            reads=(pF_g[a % 4],), writes=(wT[nm][1],))
            hcm, hcm_t = sb(st, "f_hcm", (128, DB * 512), BF16)
            xh, xh_t = sb(st, "f_xh", (128, DB * 512), BF16)
            ln = make_ln(st, "f", pG[:, 0:512], pG_g[0], pG[:, 512:1024], pG_g[1])
            xr, xr_t = sb(st, "f_xr", (128, DB * 512), BF16)
            xk, xk_t = sb(st, "f_xk", (128, DB * 512), BF16)
            rb, rb_t = sb(st, "f_rb", (128, DB * 512), BF16)
            ka, ka_t = sb(st, "f_ka", (128, DB * 512), BF16)
            fb, fb_t = sb(st, "f_fb", (128, DB * 512), BF16)
            tbuf, tbuf_t = sb(st, "f_tbuf", (128, 512), BF16)
            cc, cc_t = sb(st, "f_cc", (128, DB), BF16)
            x1T, x1T_t = sb(st, "f_x1T", (128, 1024), BF16)
            ot, ot_t = sb(st, "f_ot", (128, 1024), F32)
            P.op("vector", lambda e: e.memset(cc[:], 0.0), writes=(cc_t,))
            for s_i in range(T // 512):
                P.dma("sync", lambda e, s_i=s_i: e.dma_start(
                    out=hcm[:], in_=sup(scr_h)[:, s_i, :, :]),
                    reads=(scrh_tb[s_i],), writes=(hcm_t,))
                ln(hcm, hcm_t, xh, xh_t, None, EPS_OUT)
                for b in range(DB):
                    xb_ = xh[:, b * 512:(b + 1) * 512]
                    for which, dst, dst_t, mo in (("r", xr, xr_t, 0),
                                                  ("k", xk, xk_t, DB)):
                        dv = dst[:, b * 512:(b + 1) * 512]
                        P.op("vector", lambda e, xb_=xb_, b=b, mo=mo: e.tensor_scalar(
                            tbuf[:], xb_, muc[:, mo + b:mo + b + 1], 0.0,
                            ALU.mult, ALU.add),
                            reads=(xh_t, muc_t), writes=(tbuf_t,))
                        P.op("vector", lambda e, xb_=xb_, b=b, mo=mo, dv=dv:
                             e.scalar_tensor_tensor(
                                 dv[:, 1:512], xb_[:, 0:511],
                                 muc[:, 2 * DB + mo + b:2 * DB + mo + b + 1],
                                 tbuf[:, 1:512], ALU.mult, ALU.add),
                             reads=(xh_t, muc_t, tbuf_t), writes=(dst_t,))
                        P.op("vector", lambda e, b=b, mo=mo, dv=dv:
                             e.scalar_tensor_tensor(
                                 dv[:, 0:1], cc[:, b:b + 1],
                                 muc[:, 2 * DB + mo + b:2 * DB + mo + b + 1],
                                 tbuf[:, 0:1], ALU.mult, ALU.add),
                             reads=(cc_t, muc_t, tbuf_t), writes=(dst_t,))
                P.op("vector", lambda e: e.tensor_copy(
                    cc[:], xh.rearrange("p (b t) -> p b t", b=DB)[:, :, 511]),
                    reads=(xh_t,), writes=(cc_t,))
                for m in range(DB):
                    g = m % 2
                    def rmm(e, m=m, g=g):
                        r = None
                        for kk in range(DB):
                            r = e.matmul(
                                pG[:, g * 512:(g + 1) * 512],
                                wT["r"][0][:, (kk * DB + m) * 128:
                                           (kk * DB + m + 1) * 128],
                                xr[:, kk * 512:(kk + 1) * 512],
                                start=(kk == 0), stop=(kk == DB - 1))
                        return r
                    P.op("tensor", rmm, reads=(wT["r"][1], xr_t), writes=(pG_g[g],))
                    P.op("scalar", lambda e, m=m, g=g: e.activation(
                        rb[:, m * 512:(m + 1) * 512],
                        pG[:, g * 512:(g + 1) * 512], AF.Sigmoid),
                        reads=(pG_g[g],), writes=(rb_t,))
                for m in range(DB):
                    g = m % 2
                    def kmm(e, m=m, g=g):
                        r = None
                        for kk in range(DB):
                            r = e.matmul(
                                pG[:, g * 512:(g + 1) * 512],
                                wT["k"][0][:, (kk * DB + m) * 128:
                                           (kk * DB + m + 1) * 128],
                                xk[:, kk * 512:(kk + 1) * 512],
                                start=(kk == 0), stop=(kk == DB - 1))
                        return r
                    P.op("tensor", kmm, reads=(wT["k"][1], xk_t), writes=(pG_g[g],))
                    P.op("scalar", lambda e, m=m, g=g: e.activation(
                        ka[:, m * 512:(m + 1) * 512],
                        pG[:, g * 512:(g + 1) * 512], AF.Relu),
                        reads=(pG_g[g],), writes=(ka_t,))
                    P.op("scalar", lambda e, m=m: e.activation(
                        ka[:, m * 512:(m + 1) * 512],
                        ka[:, m * 512:(m + 1) * 512], AF.Square),
                        reads=(ka_t,), writes=(ka_t,))
                for m in range(DB):
                    g = m % 2
                    def vmm(e, m=m, g=g):
                        r = None
                        for kk in range(DB):
                            r = e.matmul(
                                pG[:, g * 512:(g + 1) * 512],
                                wT["v"][0][:, (kk * DB + m) * 128:
                                           (kk * DB + m + 1) * 128],
                                ka[:, kk * 512:(kk + 1) * 512],
                                start=(kk == 0), stop=(kk == DB - 1))
                        return r
                    P.op("tensor", vmm, reads=(wT["v"][1], ka_t), writes=(pG_g[g],))
                    P.op("vector", lambda e, m=m, g=g: e.tensor_tensor(
                        fb[:, m * 512:(m + 1) * 512],
                        pG[:, g * 512:(g + 1) * 512],
                        rb[:, m * 512:(m + 1) * 512], ALU.mult),
                        reads=(pG_g[g], rb_t), writes=(fb_t,))
                ln(fb, fb_t, fb, fb_t, None, EPS_OUT)
                for tt in range(4):
                    def x1tr(e, tt=tt):
                        r = None
                        for b in range(DB):
                            r = e.transpose(
                                pT[:, b * 128:(b + 1) * 128],
                                hcm[:, b * 512 + tt * 128:b * 512 + (tt + 1) * 128],
                                K["ident128"][:])
                        return r
                    P.op("tensor", x1tr, reads=(hcm_t, KT["ident128"]),
                         writes=(pT_t,))
                    P.op("vector", lambda e: e.tensor_copy(x1T[:], pT[:]),
                         reads=(pT_t,), writes=(x1T_t,))
                    def ftr(e, tt=tt):
                        r = None
                        for b in range(DB):
                            r = e.transpose(
                                pT[:, b * 128:(b + 1) * 128],
                                fb[:, b * 512 + tt * 128:b * 512 + (tt + 1) * 128],
                                K["ident128"][:])
                        return r
                    P.op("tensor", ftr, reads=(fb_t, KT["ident128"]),
                         writes=(pT_t,))
                    P.op("vector", lambda e: e.tensor_tensor(
                        ot[:], pT[:], x1T[:], ALU.add),
                        reads=(pT_t, x1T_t), writes=(ot_t,))
                    P.dma("gpsimd", lambda e, s_i=s_i, tt=tt: e.dma_start(
                        out=out_hbm[s_i * 512 + tt * 128:
                                    s_i * 512 + (tt + 1) * 128, :], in_=ot[:]),
                        reads=(ot_t,))
            P.wait_all("gpsimd")

        if nphase >= 6:
            with ExitStack() as st:
                idfT, idfT_t = sb(st, "f_idfT", (128, 128), F32)
                P.op("vector", lambda e: e.tensor_copy(idfT[:], K["ident128"][:]),
                     reads=(KT["ident128"],), writes=(idfT_t,))
                phase_ffn(st)
            P.barrier()

        if nphase < 6:
          with ExitStack() as st:
              hc, hc_t = sb(st, "fin_hc", (128, 2 * DB * 512), BF16)
              ot, ot_t = sb(st, "fin_ot", (128, 2 * D), F32)
              pfin, pfin_t = ps(st, "fin_ps", (128, 1024), BF16)
              for s_idx in range(T // 512):
                  ri = s_idx % 2
                  hcv = hc[:, ri * DB * 512:(ri + 1) * DB * 512]
                  P.dma("sync", lambda e, hcv=hcv, s_idx=s_idx: e.dma_start(
                      out=hcv, in_=sup(scr_h)[:, s_idx, :, :]),
                      reads=(scrh_tb[s_idx],), writes=(hc_t,))
                  for tb in range(4):
                      ro = (s_idx * 4 + tb) % 2
                      ov = ot[:, ro * D:(ro + 1) * D]
                      for b in range(DB):
                          P.op("tensor", lambda e, hcv=hcv, b=b, tb=tb: e.transpose(
                              pfin[:, b * 128:(b + 1) * 128],
                              hcv.rearrange("p (b t) -> p b t", b=DB)[:, b,
                                            tb * 128:(tb + 1) * 128],
                              K["ident128"][:]), reads=(hc_t, KT["ident128"]),
                              writes=(pfin_t,))
                      P.op("vector", lambda e, ov=ov: e.tensor_copy(ov, pfin[:]),
                           reads=(pfin_t,), writes=(ot_t,))
                      P.dma("gpsimd", lambda e, ov=ov, s_idx=s_idx, tb=tb: e.dma_start(
                          out=out_hbm[s_idx * 512 + tb * 128:s_idx * 512 + (tb + 1) * 128, :],
                          in_=ov), reads=(ot_t,))
              P.wait_all("gpsimd")

        P.finalize()
    return nc, dumps


def kernel(**inputs):
    nc, _ = build_program()
    x = np.ascontiguousarray(inputs["L0_attn_hidden_states"], dtype=np.float32)
    B = x.shape[0]
    in_maps = []
    for b in range(B):
        m = {}
        for name, shape in INPUT_SPECS:
            if name == "L0_attn_hidden_states":
                m[name] = np.ascontiguousarray(x[b])
            else:
                a = np.ascontiguousarray(inputs[name], dtype=np.float32)
                if a.shape == ():
                    a = a.reshape(1)
                m[name] = a
        in_maps.append(m)
    res = run_bass_kernel_spmd(nc, in_maps, core_ids=list(range(B)))
    return np.stack([res.results[b]["out"] for b in range(B)], axis=0)


# version 11
# speedup vs baseline: 1.0243x; 1.0243x over previous
import numpy as np
import ml_dtypes
from contextlib import ExitStack

import concourse.bass as bass
import concourse.mybir as mybir
from concourse.bass_utils import run_bass_kernel_spmd

F32 = mybir.dt.float32
BF16 = mybir.dt.bfloat16
AF = mybir.ActivationFunctionType
ALU = mybir.AluOpType
AX = mybir.AxisListType
BFD = ml_dtypes.bfloat16

T = 4096
D = 1024
DB = D // 128
DI = 2048
DIB = DI // 128
NS = 64
NH = 16
HD = 64
NE = 8
FF = 4096
TCH = 512
NCH = T // TCH
SQ = 128
NSUB = TCH // SQ
SMID = SQ // 2 - 1

EPS_OUT = 1e-5
EPS_IN = 1e-6

ENGS = ("tensor", "vector", "scalar", "gpsimd", "sync")
DOMS = ("tensor", "vector", "scalar", "gpsimd", "sync_d", "gpsimd_d",
        "scalar_d")
DMA_DOM = {"sync": "sync_d", "gpsimd": "gpsimd_d", "scalar": "scalar_d"}
DOM_ENG = {"sync_d": "sync", "gpsimd_d": "gpsimd", "scalar_d": "scalar"}


class TB:
    __slots__ = ("w", "r", "name")

    def __init__(self, name=""):
        self.w = None
        self.r = {}
        self.name = name


class Prog:

    def __init__(self, block, sems):
        self.block = block
        self.sems = sems
        self.cnt = {d: 0 for d in DOMS}
        self.q = {e: [] for e in ENGS}
        self.wm = {e: {d: 0 for d in DOMS} for e in ENGS}
        self.fence_v = {d: 0 for d in ("sync_d", "gpsimd_d", "scalar_d")}

    def _fence(self, dom, upto):
        if self.fence_v[dom] >= upto:
            return self.fence_v[dom]
        eng = DOM_ENG[dom]
        v = self.cnt[dom]
        sem = self.sems[dom]
        self.q[eng].append(lambda e, s=sem, v=v: e.wait_ge(s, v))
        self.fence_v[dom] = v
        self.wm[eng][dom] = max(self.wm[eng][dom], v)
        return v

    def _dep(self, eng, deps):
        need = {}
        for t in deps:
            if t is None:
                continue
            de, dv = t
            if de in self.fence_v:
                dv = self._fence(de, dv)
            if de in DOM_ENG and DOM_ENG[de] == eng:
                continue
            if de == eng and eng == "tensor":
                continue
            if dv > self.wm[eng][de] and dv > need.get(de, 0):
                need[de] = dv
        for de in sorted(need):
            dv = need[de]
            sem = self.sems[de]
            self.q[eng].append(lambda e, s=sem, v=dv: e.wait_ge(s, v))
            self.wm[eng][de] = dv

    def op(self, eng, fn, reads=(), writes=(), inc=1, dom=None):
        if dom is None:
            dom = eng
        deps = []
        for b in reads:
            deps.append(b.w)
        for b in writes:
            deps.append(b.w)
            deps.extend(b.r.items())
        self._dep(eng, deps)
        sem = self.sems[dom]

        def run(e, fn=fn, s=sem, k=inc):
            fn(e).then_inc(s, k)

        self.q[eng].append(run)
        self.cnt[dom] += inc
        tok = (dom, self.cnt[dom])
        for b in reads:
            if b.r.get(dom, 0) < tok[1]:
                b.r[dom] = tok[1]
        for b in writes:
            b.w = tok
            b.r = {}
        return tok

    def dma(self, eng, fn, reads=(), writes=()):
        return self.op(eng, fn, reads, writes, inc=16, dom=DMA_DOM[eng])

    def barrier(self):
        for d in ("sync_d", "gpsimd_d", "scalar_d"):
            self._fence(d, self.cnt[d])
        snap = dict(self.cnt)
        for e in ENGS:
            self._dep(e, [(d, snap[d]) for d in DOMS
                          if d != e and DOM_ENG.get(d) != e])

    def wait_all(self, eng):
        for d in ("sync_d", "gpsimd_d", "scalar_d"):
            self._fence(d, self.cnt[d])
        snap = dict(self.cnt)
        self._dep(eng, [(d, snap[d]) for d in DOMS if d != eng])

    def finalize(self):
        for e in ENGS:
            ops = self.q[e]
            if not ops:
                continue

            def body(eng, ops=ops):
                for f in ops:
                    f(eng)

            getattr(self.block, e)(body)


def _consts():
    c = {}
    c["ident128"] = np.eye(128, dtype=BFD)
    c["one_f32"] = np.ones((1, 1), dtype=np.float32)
    c["ones_col"] = np.ones((128, 1), dtype=BFD)
    c["ones_row64"] = np.ones((1, 64), dtype=BFD)
    c["ones_row128"] = np.ones((1, 128), dtype=BFD)
    r = np.arange(SQ)[:, None]
    t = np.arange(SQ)[None, :]
    c["m_ref"] = ((r <= t).astype(np.float32)
                  - (r <= SMID).astype(np.float32)).astype(BFD)
    c["tril"] = np.triu(np.ones((SQ, SQ), dtype=np.float32))
    sel = np.zeros((8, 8 * 128), dtype=BFD)
    for e8 in range(8):
        sel[e8, e8 * 128:(e8 + 1) * 128] = 1
    c["sel8"] = sel
    wcol = np.zeros((SQ, 2), dtype=BFD)
    wcol[SMID + 1:, 0] = 1
    wcol[:SMID + 1, 1] = 1
    c["wcols"] = wcol
    return c


INPUT_SPECS = [
    ("L0_attn_hidden_states", (T, D)),
    ("L0_attn_router_w", (D, 4)), ("L0_attn_router_b", (4,)),
    ("L0_attn_ln_g", (4, D)), ("L0_attn_ln_b", (4, D)),
    ("L0_attn_m_win", (D, 2 * DI)), ("L0_attn_m_conv", (4, DI)),
    ("L0_attn_m_alog", (DI,)), ("L0_attn_m_wbc", (DI, 128)),
    ("L0_attn_m_wdt", (DI,)), ("L0_attn_m_dtb", (DI,)),
    ("L0_attn_m_dskip", (DI,)), ("L0_attn_m_wout", (DI, D)),
    ("L0_attn_tc_theta", ()), ("L0_attn_tc_w", (D, D)), ("L0_attn_tc_b", (D,)),
    ("L0_attn_w_q", (D, D)), ("L0_attn_w_k", (D, D)),
    ("L0_attn_w_v", (D, D)), ("L0_attn_w_o", (D, D)),
    ("L0_attn_wlam_conv", (4, D)),
    ("L0_attn_moe_wg", (D, NE)), ("L0_attn_moe_w1", (NE, D, FF)),
    ("L0_attn_moe_b1", (NE, FF)), ("L0_attn_moe_w2", (NE, FF, D)),
    ("L0_attn_moe_b2", (NE, D)),
    ("L0_ffn_W_r", (D, D)), ("L0_ffn_W_k", (D, D)), ("L0_ffn_W_v", (D, D)),
    ("L0_ffn_mu_r", (D,)), ("L0_ffn_mu_k", (D,)),
]


def build_program(dump=(), nphase=6):
    nc = bass.Bass("TRN2", target_bir_lowering=False, debug=False)
    IN = {}
    for name, shape in INPUT_SPECS:
        shp = list(shape) if shape else [1]
        IN[name] = nc.dram_tensor(name, shp, F32, kind="ExternalInput").ap()
    out_hbm = nc.dram_tensor("out", [T, D], F32, kind="ExternalOutput").ap()

    dumps = {}

    def dumpt(name, shape, dt=F32):
        if name in dump:
            dumps[name] = nc.dram_tensor("dump_" + name, list(shape), dt,
                                         kind="ExternalOutput").ap()
            return dumps[name]
        return None

    scr_x0fm = nc.dram_tensor("scr_x0fm", [128, T * DB], BF16, kind="Internal").ap()
    scr_h = nc.dram_tensor("scr_h", [128, DB * T], BF16, kind="Internal").ap()
    scr_x3 = nc.dram_tensor("scr_x3", [128, DB * T], BF16, kind="Internal").ap()
    scr_dp = nc.dram_tensor("scr_dp", [128, DB * (T // 2)], BF16, kind="Internal").ap()
    scr_y = nc.dram_tensor("scr_y", [128, DB * T], BF16, kind="Internal").ap()
    scr_w1 = nc.dram_tensor("scr_w1", [NE * D, FF], BF16, kind="Internal").ap()
    scr_w2 = nc.dram_tensor("scr_w2", [NE * FF, D], BF16, kind="Internal").ap()
    scr_win = nc.dram_tensor("scr_win", [D, 2 * DI], BF16, kind="Internal").ap()
    scr_gf = nc.dram_tensor("scr_gf", [8, T], BF16, kind="Internal").ap()
    scr_wout = nc.dram_tensor("scr_wout", [DI, D], BF16, kind="Internal").ap()

    def sup(scr):
        return scr.rearrange("p (s b t) -> p s b t", s=T // 512, b=DB)

    CN = _consts()
    CONST = {k: nc.inline_tensor(v, name="c_" + k).ap() for k, v in CN.items()}

    with ExitStack() as top:
        sems = {d: top.enter_context(nc.semaphore("sem_" + d)) for d in DOMS}
        block = top.enter_context(nc.Block())
        P = Prog(block, sems)

        def sb(st, name, shape, dt):
            ap = st.enter_context(nc.sbuf_tensor(name, list(shape), dt))
            return ap, TB(name)

        def ps(st, name, shape, dt=F32):
            ap = st.enter_context(nc.psum_tensor(name, list(shape), dt))
            return ap, TB(name)

        scrh_tb = [TB("scrh%d" % i) for i in range(T // 512)]
        scrx0_tb = TB("scrx0")
        scrdp_tb = TB("scrdp")
        scry_tb = TB("scry")
        scrx3_tb = TB("scrx3")
        scrw1_tb = TB("scrw1")
        scrw2_tb = TB("scrw2")
        scrwin_tb = TB("scrwin")
        scrgf_tb = TB("scrgf")
        scrwout_tb = TB("scrwout")
        K = {}
        KT = {}
        for k, v in CN.items():
            a, tb = sb(top, "k_" + k, v.shape, BF16 if v.dtype == BFD else F32)
            K[k], KT[k] = a, tb
            P.dma("sync", lambda e, a=a, k=k: e.dma_start(out=a[:], in_=CONST[k]),
                  writes=(tb,))
        mskbc, mskbc_t = sb(top, "mskbc", (128, 4), F32)
        lng, lng_t = sb(top, "lng", (128, 4 * DB), F32)
        lnb, lnb_t = sb(top, "lnb", (128, 4 * DB), F32)
        lngn, lngn_t = sb(top, "lngn", (128, 4 * DB), F32)

        rowbuf, rowbuf_t = sb(top, "rowbuf", (1, DI), F32)
        rowbufb, rowbufb_t = sb(top, "rowbufb", (1, DI), BF16)
        CPS = [None, None]

        def load_cols(src_row_ap, n, dst_ap, dst_tb, dst_cols=None, scale=None,
                      act=None):
            colps, colps_t = CPS
            """src_row_ap: HBM AP viewable as (1, n). dst columns written at
            dst_ap[:, c] for c in dst_cols (default range(n//128)).
            Optional ACT func + scale applied to the row before transposing."""
            nb = n // 128
            if dst_cols is None:
                dst_cols = list(range(nb))
            P.dma("sync", lambda e: e.dma_start(out=rowbuf[0:1, 0:n], in_=src_row_ap),
                  writes=(rowbuf_t,))
            if act is not None:
                P.op("scalar", lambda e: e.activation(rowbuf[0:1, 0:n],
                                                      rowbuf[0:1, 0:n], act),
                     reads=(rowbuf_t,), writes=(rowbuf_t,))
            for i in range(nb):
                P.op("tensor", lambda e, i=i: e.matmul(
                    colps[:, i % 4:i % 4 + 1], rowbuf[0:1, i * 128:(i + 1) * 128],
                    K["one_f32"][:], start=True, stop=True),
                    reads=(rowbuf_t, KT["one_f32"]), writes=(colps_t,))
                c = dst_cols[i]
                if scale is None:
                    P.op("vector", lambda e, i=i, c=c: e.tensor_copy(
                        dst_ap[:, c:c + 1], colps[:, i % 4:i % 4 + 1]),
                        reads=(colps_t,), writes=(dst_tb,))
                else:
                    P.op("vector", lambda e, i=i, c=c: e.tensor_single_scalar(
                        dst_ap[:, c:c + 1], colps[:, i % 4:i % 4 + 1], scale, ALU.mult),
                        reads=(colps_t,), writes=(dst_tb,))



        with ExitStack() as st:
            x0 = IN["L0_attn_hidden_states"]
            xt, xt_t = sb(st, "p0_xt", (128, 2 * D), F32)
            xhb, xhb_t = sb(st, "p0_xhb", (128, 2 * D), BF16)
            x0b, x0b_t = sb(st, "p0_x0b", (128, 2 * D), BF16)
            sqf, sqf_t = sb(st, "p0_sqf", (128, 2 * D), F32)
            cols, cols_t = sb(st, "p0_cols", (128, 8), F32)
            x0f, x0f_t = sb(st, "p0_x0f", (128, 2 * D), BF16)
            hstage, hstage_t = sb(st, "p0_hstage", (128, 2 * D), BF16)
            pp, pp_t = ps(st, "p0_pool", (1, D))
            cps0, cps0_t = ps(st, "p0_cps", (128, 512))
            CPS[0], CPS[1] = cps0, cps0_t
            P.dma("gpsimd", lambda e: e.dma_start(
                out=scr_win, in_=IN["L0_attn_m_win"]), writes=(scrwin_tb,))
            P.dma("gpsimd", lambda e: e.dma_start(
                out=scr_wout, in_=IN["L0_attn_m_wout"]), writes=(scrwout_tb,))
            for j in range(4):
                load_cols(IN["L0_attn_ln_g"][j, :][None, :], D, lng, lng_t,
                          dst_cols=[j * DB + i for i in range(DB)])
                load_cols(IN["L0_attn_ln_b"][j, :][None, :], D, lnb, lnb_t,
                          dst_cols=[j * DB + i for i in range(DB)])
            P.op("vector", lambda e: e.tensor_single_scalar(
                lngn[:], lng[:], -1.0, ALU.mult),
                reads=(lng_t,), writes=(lngn_t,))
            pt, pt_t = ps(st, "p0_pt", (128, 1024), BF16)
            pt2, pt2_t = ps(st, "p0_pt2", (128, 1024), BF16)

            for tt in range(T // 128):
                ri = tt % 2
                xv = xt[:, ri * D:(ri + 1) * D]
                xh = xhb[:, ri * D:(ri + 1) * D]
                xraw = x0b[:, ri * D:(ri + 1) * D]
                sq = sqf[:, ri * D:(ri + 1) * D]
                P.dma("sync", lambda e, tt=tt, xv=xv: e.dma_start(
                    out=xv, in_=x0[tt * 128:(tt + 1) * 128, :]), writes=(xt_t,))
                s_c = cols[:, 0:1]
                s2_c = cols[:, 1:2]
                m_c = cols[:, 2:3]
                v2_c = cols[:, 3:4]
                v3_c = cols[:, 4:5]
                srt_c = cols[:, 5:6]
                rst_c = cols[:, 6:7]
                mr_c = cols[:, 7:8]
                P.op("vector", lambda e, xv=xv, s_c=s_c: e.reduce_sum(s_c, xv, axis=AX.X),
                     reads=(xt_t,), writes=(cols_t,))
                P.op("scalar", lambda e, xv=xv, sq=sq: e.activation(sq, xv, AF.Square),
                     reads=(xt_t,), writes=(sqf_t,))
                P.op("vector", lambda e, sq=sq, s2_c=s2_c: e.reduce_sum(s2_c, sq, axis=AX.X),
                     reads=(sqf_t,), writes=(cols_t,))
                P.op("vector", lambda e, s_c=s_c, m_c=m_c: e.tensor_single_scalar(
                    m_c, s_c, 1.0 / D, ALU.mult), reads=(cols_t,), writes=(cols_t,))
                P.op("vector", lambda e, s2_c=s2_c, v2_c=v2_c: e.tensor_scalar(
                    v2_c, s2_c, 1.0 / D, EPS_OUT, ALU.mult, ALU.add),
                    reads=(cols_t,), writes=(cols_t,))
                P.op("vector", lambda e, m_c=m_c, v2_c=v2_c, v3_c=v3_c:
                     e.scalar_tensor_tensor(v3_c, m_c, m_c, v2_c, ALU.mult, ALU.subtract),
                     reads=(cols_t,), writes=(cols_t,))
                P.op("scalar", lambda e, v3_c=v3_c, srt_c=srt_c: e.activation(
                    srt_c, v3_c, AF.Ln, scale=-1.0), reads=(cols_t,), writes=(cols_t,))
                P.op("scalar", lambda e, srt_c=srt_c, rst_c=rst_c: e.activation(
                    rst_c, srt_c, AF.Exp, scale=-0.5), reads=(cols_t,), writes=(cols_t,))
                P.op("vector", lambda e, m_c=m_c, rst_c=rst_c, mr_c=mr_c: e.tensor_tensor(
                    mr_c, m_c, rst_c, ALU.mult), reads=(cols_t,), writes=(cols_t,))
                P.op("vector", lambda e, xv=xv, rst_c=rst_c, mr_c=mr_c, xh=xh:
                     e.tensor_scalar(xh, xv, rst_c, mr_c, ALU.mult, ALU.subtract),
                     reads=(xt_t, cols_t), writes=(xhb_t,))
                P.op("vector", lambda e, xv=xv, xraw=xraw: e.tensor_copy(xraw, xv),
                     reads=(xt_t,), writes=(x0b_t,))
                for half in range(2):
                    P.op("tensor", lambda e, xh=xh, half=half, tt=tt: e.matmul(
                        pp[0:1, half * 512:(half + 1) * 512], K["ones_col"][:],
                        xh[:, half * 512:(half + 1) * 512],
                        start=(tt == 0), stop=(tt == T // 128 - 1)),
                        reads=(xhb_t, KT["ones_col"]), writes=(pp_t,))
                s_idx, tloc = tt // 4, (tt % 4) * 128
                for b in range(DB):
                    P.op("tensor", lambda e, xh=xh, b=b: e.transpose(
                        pt[:, b * 128:(b + 1) * 128], xh[:, b * 128:(b + 1) * 128],
                        K["ident128"][:]), reads=(xhb_t, KT["ident128"]), writes=(pt_t,))
                hstg = hstage[:, ri * D:(ri + 1) * D]
                P.op("vector", lambda e, hstg=hstg: e.tensor_copy(hstg, pt[:]),
                     reads=(pt_t,), writes=(hstage_t,))
                P.dma("gpsimd", lambda e, hstg=hstg, s_idx=s_idx, tloc=tloc:
                      e.dma_start(
                          out=sup(scr_h)[:, s_idx, :, tloc:tloc + 128],
                          in_=hstg.rearrange("p (b t) -> p b t", b=DB)),
                      reads=(hstage_t,), writes=(scrh_tb[s_idx],))
                for b in range(DB):
                    P.op("tensor", lambda e, xraw=xraw, b=b: e.transpose(
                        pt2[:, b * 128:(b + 1) * 128], xraw[:, b * 128:(b + 1) * 128],
                        K["ident128"][:]), reads=(x0b_t, KT["ident128"]), writes=(pt2_t,))
                xf = x0f[:, ri * D:(ri + 1) * D]
                P.op("vector", lambda e, xf=xf: e.tensor_copy(xf, pt2[:]),
                     reads=(pt2_t,), writes=(x0f_t,))
                P.dma("gpsimd", lambda e, xf=xf, s_idx=s_idx, tloc=tloc: e.dma_start(
                    out=sup(scr_x0fm)[:, s_idx, :, tloc:tloc + 128],
                    in_=xf.rearrange("p (b t) -> p b t", b=DB)),
                    reads=(x0f_t,), writes=(scrx0_tb,))

            poolrow, poolrow_t = sb(st, "p0_poolrow", (1, D), F32)
            P.op("scalar", lambda e: e.copy(poolrow[:], pp[:]),
                 reads=(pp_t,), writes=(poolrow_t,))
            pfm, pfm_t = ps(st, "p0_pfm", (128, 8))
            for b in range(DB):
                P.op("tensor", lambda e, b=b: e.matmul(
                    pfm[:, b:b + 1], poolrow[0:1, b * 128:(b + 1) * 128],
                    K["one_f32"][:], start=True, stop=True),
                    reads=(poolrow_t, KT["one_f32"]), writes=(pfm_t,))
            pfs, pfs_t = sb(st, "p0_pfs", (128, 8), F32)
            P.op("vector", lambda e: e.tensor_single_scalar(pfs[:], pfm[:], 1.0 / T,
                                                            ALU.mult),
                 reads=(pfm_t,), writes=(pfs_t,))
            rwf, rwf_t = sb(st, "p0_rwf", (128, 32), F32)
            P.dma("sync", lambda e: e.dma_start(
                out=rwf.rearrange("p (a j) -> p a j", j=4),
                in_=IN["L0_attn_router_w"].rearrange("(a p) j -> p a j", p=128)),
                writes=(rwf_t,))
            prep, prep_t = sb(st, "p0_prep", (128, 32), F32)
            for j in range(4):
                P.op("vector", lambda e, j=j: e.tensor_copy(
                    prep.rearrange("p (a j) -> p a j", j=4)[:, :, j], pfs[:]),
                    reads=(pfs_t,), writes=(prep_t,))
            prod, prod_t = sb(st, "p0_prod", (128, 32), F32)
            P.op("vector", lambda e: e.tensor_tensor(prod[:], prep[:], rwf[:], ALU.mult),
                 reads=(prep_t, rwf_t), writes=(prod_t,))
            spk, spk_t = sb(st, "p0_spk", (128, 4), F32)
            for j in range(4):
                P.op("vector", lambda e, j=j: e.reduce_sum(
                    spk[:, j:j + 1],
                    prod.rearrange("p (a j) -> p a j", j=4)[:, :, j], axis=AX.X),
                    reads=(prod_t,), writes=(spk_t,))
            tmp4, tmp4_t = sb(st, "p0_tmp4", (128, 4), F32)
            lvl = 64
            while lvl >= 1:
                P.dma("sync", lambda e, lvl=lvl: e.dma_start(
                    out=tmp4[0:lvl, :], in_=spk[lvl:2 * lvl, :]),
                    reads=(spk_t,), writes=(tmp4_t,))
                P.op("vector", lambda e, lvl=lvl: e.tensor_tensor(
                    spk[0:lvl, :], spk[0:lvl, :], tmp4[0:lvl, :], ALU.add),
                    reads=(tmp4_t, spk_t), writes=(spk_t,))
                lvl //= 2
            rbrow, rbrow_t = sb(st, "p0_rbrow", (1, 4), F32)
            P.dma("sync", lambda e: e.dma_start(out=rbrow[:],
                                                in_=IN["L0_attn_router_b"][None, :]),
                  writes=(rbrow_t,))
            P.op("vector", lambda e: e.tensor_tensor(
                spk[0:1, :], spk[0:1, :], rbrow[0:1, :], ALU.add),
                reads=(rbrow_t, spk_t), writes=(spk_t,))
            P.op("scalar", lambda e: e.activation(spk[0:1, 1 * 0:4], spk[0:1, 0:4],
                                                  AF.Sigmoid),
                 reads=(spk_t,), writes=(spk_t,))
            mrow, mrow_t = sb(st, "p0_mrow", (1, 4), BF16)
            P.op("vector", lambda e: e.tensor_single_scalar(mrow[:], spk[0:1, 0:4], 0.3,
                                                            ALU.is_gt),
                 reads=(spk_t,), writes=(mrow_t,))
            mps, mps_t = ps(st, "p0_mps", (128, 512))
            P.op("tensor", lambda e: e.matmul(mps[:, 0:4], K["ones_row128"][:], mrow[:],
                                              start=True, stop=True),
                 reads=(mrow_t, KT["ones_row128"]), writes=(mps_t,))
            P.op("vector", lambda e: e.tensor_copy(mskbc[:], mps[:, 0:4]),
                 reads=(mps_t,), writes=(mskbc_t,))

            dpf = dumpt("pfs", (128, 8))
            if dpf is not None:
                P.dma("gpsimd", lambda e: e.dma_start(out=dpf, in_=pfs[:]), reads=(pfs_t,))
            dpr = dumpt("prod", (128, 32))
            if dpr is not None:
                P.dma("gpsimd", lambda e: e.dma_start(out=dpr, in_=prod[:]), reads=(prod_t,))
            dm = dumpt("mskbc", (128, 4))
            if dm is not None:
                P.dma("gpsimd", lambda e: e.dma_start(out=dm, in_=mskbc[:]),
                      reads=(mskbc_t,))
            dsc = dumpt("scores", (128, 4))
            if dsc is not None:
                P.dma("gpsimd", lambda e: e.dma_start(out=dsc, in_=spk[:]),
                      reads=(spk_t,))
            P.barrier()

        def phase_mamba(st):
            NSC = 4
            CDB = DB * 512
            pA, _ = ps(st, "m_pA", (128, 2048))
            pA_g = [TB("pAg%d" % i) for i in range(4)]
            pB, _ = ps(st, "m_pB", (128, 1024))
            pB_g = [TB("pBg%d" % i) for i in range(2)]
            pT, pT_t = ps(st, "m_pT", (128, 1024), BF16)
            pT2, pT2_t = ps(st, "m_pT2", (128, 1024), BF16)

            CPS[0], CPS[1] = pB[:, 0:512], pB_g[0]
            convc, convc_t = sb(st, "m_convc", (128, 64), F32)
            for kk in range(4):
                load_cols(IN["L0_attn_m_conv"][kk, :][None, :], DI, convc, convc_t,
                          dst_cols=[a * 4 + kk for a in range(DIB)])
            dskipc, dskipc_t = sb(st, "m_dskipc", (128, 16), F32)
            load_cols(IN["L0_attn_m_dskip"][None, :], DI, dskipc, dskipc_t)
            prow, prow_t = rowbuf, rowbuf_t
            prowb, prowb_t = rowbufb, rowbufb_t
            wdt_bc, wdt_bc_t = sb(st, "m_wdtbc", (128, DI), BF16)
            dtb_bc, dtb_bc_t = sb(st, "m_dtbbc", (128, DI), BF16)
            A_bc, A_bc_t = sb(st, "m_Abc", (128, DI), BF16)
            for wsrc, dst, dst_t, neg_exp in (
                    (IN["L0_attn_m_wdt"], wdt_bc, wdt_bc_t, False),
                    (IN["L0_attn_m_dtb"], dtb_bc, dtb_bc_t, False),
                    (IN["L0_attn_m_alog"], A_bc, A_bc_t, True)):
                P.dma("sync", lambda e, wsrc=wsrc: e.dma_start(out=prow[:],
                                                               in_=wsrc[None, :]),
                      reads=(prow_t,), writes=(prow_t,))
                if neg_exp:
                    P.op("scalar", lambda e: e.activation(prow[:], prow[:], AF.Exp),
                         reads=(prow_t,), writes=(prow_t,))
                    P.op("vector", lambda e: e.tensor_single_scalar(
                        prowb[:], prow[:], -1.0, ALU.mult),
                        reads=(prow_t,), writes=(prowb_t,))
                else:
                    P.op("vector", lambda e: e.tensor_copy(prowb[:], prow[:]),
                         reads=(prow_t,), writes=(prowb_t,))
                for g in range(4):
                    P.op("tensor", lambda e, g=g: e.matmul(
                        pA[:, g * 512:(g + 1) * 512], K["ones_row128"][:],
                        prowb[0:1, g * 512:(g + 1) * 512], start=True, stop=True),
                        reads=(prowb_t, KT["ones_row128"]), writes=(pA_g[g],))
                P.op("vector", lambda e, dst=dst: e.tensor_copy(dst[:], pA[:]),
                     reads=tuple(pA_g), writes=(dst_t,))
            cast_jobs = []
            w1f = IN["L0_attn_moe_w1"].rearrange("e d f -> (e d) f")
            w2f = IN["L0_attn_moe_w2"].rearrange("e f d -> (e f) d")
            for i in range(32):
                cast_jobs.append((scr_w1[i * 256:(i + 1) * 256, :],
                                  w1f[i * 256:(i + 1) * 256, :], scrw1_tb))
            for i in range(32):
                cast_jobs.append((scr_w2[i * 1024:(i + 1) * 1024, :],
                                  w2f[i * 1024:(i + 1) * 1024, :], scrw2_tb))
            wbc_sb, wbc_sb_t = sb(st, "m_wbc", (128, DIB * 128), BF16)
            P.dma("gpsimd", lambda e: e.dma_start(
                out=wbc_sb.rearrange("p (a j) -> p a j", j=128),
                in_=IN["L0_attn_m_wbc"].rearrange(
                    "(a p) j -> p a j", p=128)), writes=(wbc_sb_t,))

            hcm, _hcm = sb(st, "m_hcm", (128, 2 * CDB), BF16)
            hcm_rt = [TB("hcmr0"), TB("hcmr1")]
            xh, _xh = sb(st, "m_xh", (128, 2 * CDB), BF16)
            xh_rt = [TB("xhr0"), TB("xhr1")]
            sqb, sqb_t = sb(st, "m_sqb", (128, 2 * 512), BF16)
            rows, rows_t = sb(st, "m_rows", (1, 2 * 512), F32)
            rowsb, rowsb_t = sb(st, "m_rowsb", (1, 2 * 512), BF16)
            Rsb, Rsb_t = sb(st, "m_Rsb", (128, 512), BF16)
            MRsb, MRsb_t = sb(st, "m_MRsb", (128, 512), BF16)
            winp, winp_t = sb(st, "m_winp", (128, 2 * 8 * 128), BF16)
            winp_tb = [TB("winp0"), TB("winp1")]
            xcp, _xcp = sb(st, "m_xcp", (128, 2 * DIB * 515), BF16)
            xcp_ct = [[TB("xcp%d_%d" % (s, ct)) for ct in range(DIB)]
                      for s in range(2)]
            tail3, tail3_t = sb(st, "m_tail3", (128, 2 * DIB * 3), BF16)
            tail3_rt = [TB("tl0"), TB("tl1")]
            zbuf, zbuf_t = sb(st, "m_zbuf", (128, 2 * 512), BF16)
            xct, _xct = sb(st, "m_xct", (128, NSC * 2048), BF16)
            xct_qt = [TB("xctq%d" % q) for q in range(NSC)]
            dtsub, dtsub_t = sb(st, "m_dtsub", (128, 2048), BF16)
            usub, usub_t = dtsub, dtsub_t
            convb, convb_t = sb(st, "m_convb", (128, 4 * 512), F32)
            Bn, _Bn = sb(st, "m_Bn", (64, 2 * 512), BF16)
            Bn_rt = [TB("Bn0"), TB("Bn1")]
            Cn, _Cn = sb(st, "m_Cn", (64, 2 * 512), BF16)
            Cn_rt = [TB("Cn0"), TB("Cn1")]
            bct, _bct = sb(st, "m_bct", (128, 2 * NSC * 128), BF16)
            bct_rt = [TB("bct0"), TB("bct1")]
            dtA, dtA_t = sb(st, "m_dtA", (128, 2 * 2048), BF16)
            dtA_tb = [TB("dtA0"), TB("dtA1")]
            dcl, dcl_t = sb(st, "m_dcl", (128, 2048), BF16)
            Et, Et_t = sb(st, "m_Et", (128, 2048), BF16)
            Wrow = Et[0:1, 0:2048]
            e1t, e1t_t = dcl, dcl_t
            Ut, Ut_t = usub, usub_t
            Gm, Gm_t = sb(st, "m_Gm", (128, 128), BF16)
            hst, hst_t = sb(st, "m_hst", (64, 2048), BF16)
            tmpst, tmpst_t = sb(st, "m_tmpst", (64, 2048), BF16)
            yt, yt_t = sb(st, "m_yt", (128, NSC * 2048), BF16)
            Pf, Pf_t = sb(st, "m_Pf", (128, DIB * 512), BF16)
            woutp, woutp_t = sb(st, "m_woutp", (128, 2 * 4 * 128), BF16)
            woutp_tb = [TB("wop0"), TB("wop1")]

            P.op("vector", lambda e: e.memset(hst[:], 0.0), writes=(hst_t,))
            P.op("vector", lambda e: e.memset(tail3[:], 0.0),
                 writes=(tail3_rt[0], tail3_rt[1]))

            def xv(slot, ct):
                return xcp[:, slot * DIB * 515 + ct * 515 + 3:
                           slot * DIB * 515 + ct * 515 + 515]

            def xcp3(slot):
                return xcp.rearrange("p (s c k) -> p s c k", s=2, k=515)[:, slot]

            def t3v(slot):
                return tail3.rearrange("p (s c k) -> p s c k", s=2, k=3)[:, slot]

            def cast_drip(c):
                if c > 0 and cast_jobs:
                    dst, src2, tbw = cast_jobs.pop(0)
                    P.dma("gpsimd", lambda e, dst=dst, src2=src2:
                          e.dma_start(out=dst, in_=src2), writes=(tbw,))

            def stageA(c):
                slot = c % 2
                ycnt = [0]
                _oy = None
                hcv = hcm[:, slot * CDB:(slot + 1) * CDB]
                xhv = xh[:, slot * CDB:(slot + 1) * CDB]
                P.dma("sync", lambda e: e.dma_start(
                    out=hcv, in_=sup(scr_h)[:, c, :, :]),
                    reads=(scrh_tb[c],), writes=(hcm_rt[slot],))
                ycnt[0] += 1
                if ycnt[0] % 8 == 0:
                    cast_drip(c)
                yield
                P.op("vector", lambda e: e.tensor_copy(
                    xcp3(slot)[:, :, 0:3], t3v((c - 1) % 2)),
                    reads=(tail3_rt[(c - 1) % 2],),
                    writes=tuple(xcp_ct[slot]))
                ycnt[0] += 1
                if ycnt[0] % 8 == 0:
                    cast_drip(c)
                yield
                for b in range(DB):
                    sv = sqb[:, (b % 2) * 512:(b % 2 + 1) * 512]
                    P.op("scalar", lambda e, b=b, sv=sv: e.activation(
                        sv, hcv[:, b * 512:(b + 1) * 512], AF.Square),
                        reads=(hcm_rt[slot],), writes=(sqb_t,))
                    P.op("tensor", lambda e, b=b: e.matmul(
                        pB[0:1, 0:512], K["ones_col"][:], hcv[:, b * 512:(b + 1) * 512],
                        start=(b == 0), stop=(b == DB - 1)),
                        reads=(hcm_rt[slot], KT["ones_col"]), writes=(pB_g[0],))
                    P.op("tensor", lambda e, b=b, sv=sv: e.matmul(
                        pB[0:1, 512:1024], K["ones_col"][:], sv,
                        start=(b == 0), stop=(b == DB - 1)),
                        reads=(sqb_t, KT["ones_col"]), writes=(pB_g[1],))
                rm = rows[0:1, 0:512]
                rv = rows[0:1, 512:1024]
                P.op("vector", lambda e: e.tensor_single_scalar(
                    rm, pB[0:1, 0:512], 1.0 / D, ALU.mult),
                    reads=(pB_g[0],), writes=(rows_t,))
                P.op("vector", lambda e: e.tensor_scalar(
                    rv, pB[0:1, 512:1024], 1.0 / D, EPS_IN, ALU.mult, ALU.add),
                    reads=(pB_g[1],), writes=(rows_t,))
                ycnt[0] += 1
                if ycnt[0] % 8 == 0:
                    cast_drip(c)
                yield
                P.op("vector", lambda e: e.tensor_tensor(
                    rowsb[0:1, 0:512], rm, rm, ALU.mult),
                    reads=(rows_t,), writes=(rowsb_t,))
                P.op("vector", lambda e: e.tensor_tensor(
                    rv, rv, rowsb[0:1, 0:512], ALU.subtract),
                    reads=(rows_t, rowsb_t), writes=(rows_t,))
                P.op("scalar", lambda e: e.activation(rv, rv, AF.Ln),
                     reads=(rows_t,), writes=(rows_t,))
                P.op("scalar", lambda e: e.activation(rv, rv, AF.Exp, scale=-0.5),
                     reads=(rows_t,), writes=(rows_t,))
                P.op("vector", lambda e: e.tensor_copy(
                    rowsb[0:1, 0:512], rv), reads=(rows_t,), writes=(rowsb_t,))
                P.op("vector", lambda e: e.tensor_tensor(
                    rowsb[0:1, 512:1024], rm, rv, ALU.mult),
                    reads=(rows_t,), writes=(rowsb_t,))
                ycnt[0] += 1
                if ycnt[0] % 8 == 0:
                    cast_drip(c)
                yield
                P.op("tensor", lambda e: e.matmul(
                    pB[:, 0:512], K["ones_row128"][:], rowsb[0:1, 0:512],
                    start=True, stop=True),
                    reads=(rowsb_t, KT["ones_row128"]), writes=(pB_g[0],))
                P.op("vector", lambda e: e.tensor_copy(Rsb[:], pB[:, 0:512]),
                     reads=(pB_g[0],), writes=(Rsb_t,))
                P.op("tensor", lambda e: e.matmul(
                    pB[:, 512:1024], K["ones_row128"][:], rowsb[0:1, 512:1024],
                    start=True, stop=True),
                    reads=(rowsb_t, KT["ones_row128"]), writes=(pB_g[1],))
                P.op("vector", lambda e: e.tensor_copy(MRsb[:], pB[:, 512:1024]),
                     reads=(pB_g[1],), writes=(MRsb_t,))
                ycnt[0] += 1
                if ycnt[0] % 8 == 0:
                    cast_drip(c)
                yield
                for b in range(DB):
                    tv = convb[:, 0:512]
                    P.op("vector", lambda e, b=b, tv=tv: e.tensor_tensor(
                        tv, hcv[:, b * 512:(b + 1) * 512], Rsb[:], ALU.mult),
                        reads=(hcm_rt[slot], Rsb_t), writes=(convb_t,))
                    P.op("vector", lambda e, tv=tv: e.tensor_tensor(
                        tv, tv, MRsb[:], ALU.subtract),
                        reads=(MRsb_t, convb_t), writes=(convb_t,))
                    P.op("vector", lambda e, b=b, tv=tv: e.tensor_scalar(
                        xhv[:, b * 512:(b + 1) * 512], tv, lng[:, b:b + 1],
                        lnb[:, b:b + 1], ALU.mult, ALU.add),
                        reads=(convb_t, lng_t, lnb_t), writes=(xh_rt[slot],))
                    ycnt[0] += 1
                    if ycnt[0] % 8 == 0:
                        cast_drip(c)
                    yield
                for pc in range(16):
                    wsl = pc % 2
                    wv = winp[:, wsl * 1024:(wsl + 1) * 1024]
                    P.dma("sync", lambda e, pc=pc, wv=wv: e.dma_start(
                        out=wv.rearrange("p (k f) -> p k f", f=128),
                        in_=scr_win[:, pc * 128:(pc + 1) * 128].rearrange(
                            "(k p) f -> p k f", p=128)),
                        reads=(scrwin_tb,), writes=(winp_tb[wsl],))
                    g = pc % 2
                    def mmfn(e, wv=wv, g=g):
                        r = None
                        for kk in range(8):
                            r = e.matmul(
                                pB[:, g * 512:(g + 1) * 512],
                                wv[:, kk * 128:(kk + 1) * 128],
                                xhv[:, kk * 512:(kk + 1) * 512],
                                start=(kk == 0), stop=(kk == 7))
                        return r
                    P.op("tensor", mmfn, reads=(winp_tb[wsl], xh_rt[slot]),
                         writes=(pB_g[g],))
                    P.op("scalar", lambda e, g=g, pc=pc: e.copy(
                        xcp[:, slot * DIB * 515 + pc * 515 + 3:
                            slot * DIB * 515 + pc * 515 + 515],
                        pB[:, g * 512:(g + 1) * 512]),
                        reads=(pB_g[g],), writes=(xcp_ct[slot][pc],))
                    ycnt[0] += 1
                    if ycnt[0] % 8 == 0:
                        cast_drip(c)
                    yield
                P.op("vector", lambda e: e.tensor_copy(
                    t3v(slot), xcp3(slot)[:, :, 512:515]),
                    reads=tuple(xcp_ct[slot]), writes=(tail3_rt[slot],))
                ycnt[0] += 1
                if ycnt[0] % 8 == 0:
                    cast_drip(c)
                yield
                for ct in range(DIB):
                    cb = convb[:, (ct % 2) * 512:(ct % 2 + 1) * 512]
                    base = slot * DIB * 515 + ct * 515
                    P.op("vector", lambda e, cb=cb, base=base, ct=ct: e.tensor_scalar(
                        cb, xcp[:, base:base + 512], convc[:, ct * 4:ct * 4 + 1],
                        0.0, ALU.mult, ALU.add),
                        reads=(xcp_ct[slot][ct], convc_t), writes=(convb_t,))
                    for kk in range(1, 4):
                        P.op("vector", lambda e, cb=cb, base=base, ct=ct, kk=kk:
                             e.scalar_tensor_tensor(
                                 cb, xcp[:, base + kk:base + kk + 512],
                                 convc[:, ct * 4 + kk:ct * 4 + kk + 1], cb,
                                 ALU.mult, ALU.add),
                             reads=(xcp_ct[slot][ct], convc_t, convb_t),
                             writes=(convb_t,))
                    P.op("scalar", lambda e, cb=cb, ct=ct, base=base: e.activation(
                        xcp[:, base + 3:base + 515], cb, AF.Silu),
                        reads=(convb_t, tail3_rt[slot]),
                        writes=(xcp_ct[slot][ct],))
                    ycnt[0] += 1
                    if ycnt[0] % 8 == 0:
                        cast_drip(c)
                    yield
                rd_all = tuple(xcp_ct[slot])
                def bcfn(e):
                    r = None
                    for kk in range(DIB):
                        r = e.matmul(pB[:, 0:512], wbc_sb[:, kk * 128:(kk + 1) * 128],
                                     xv(slot, kk), start=(kk == 0),
                                     stop=(kk == DIB - 1))
                    return r
                P.op("tensor", bcfn, reads=(wbc_sb_t,) + rd_all, writes=(pB_g[0],))
                P.op("vector", lambda e: e.tensor_copy(
                    Bn[:, slot * 512:(slot + 1) * 512], pB[0:64, 0:512]),
                    reads=(pB_g[0],), writes=(Bn_rt[slot],))
                P.op("vector", lambda e: e.tensor_copy(
                    Cn[:, slot * 512:(slot + 1) * 512], pB[64:128, 0:512]),
                    reads=(pB_g[0],), writes=(Cn_rt[slot],))
                ycnt[0] += 1
                if ycnt[0] % 8 == 0:
                    cast_drip(c)
                yield
                for m in range(NSC):
                    def bctfn(e, m=m):
                        r = None
                        for kk in range(DIB):
                            r = e.matmul(
                                pB[:, 512:640],
                                xv(slot, kk)[:, m * 128:(m + 1) * 128],
                                wbc_sb[:, kk * 128:(kk + 1) * 128],
                                start=(kk == 0), stop=(kk == DIB - 1))
                        return r
                    P.op("tensor", bctfn, reads=(wbc_sb_t,) + rd_all,
                         writes=(pB_g[1],))
                    P.op("vector", lambda e, m=m: e.tensor_copy(
                        bct[:, slot * 512 + m * 128:slot * 512 + (m + 1) * 128],
                        pB[:, 512:640]),
                        reads=(pB_g[1],), writes=(bct_rt[slot],))
                    ycnt[0] += 1
                    if ycnt[0] % 8 == 0:
                        cast_drip(c)
                    yield
                for q in range(NSC):
                    for half in range(2):
                        def trfn(e, q=q, half=half):
                            r = None
                            for j in range(8):
                                ct = half * 8 + j
                                r = e.transpose(
                                    pT[:, j * 128:(j + 1) * 128],
                                    xv(slot, ct)[:, q * 128:(q + 1) * 128],
                                    K["ident128"][:])
                            return r
                        P.op("tensor", trfn, reads=rd_all + (KT["ident128"],),
                             writes=(pT_t,))
                        P.op("vector", lambda e, q=q, half=half: e.tensor_copy(
                            xct[:, q * 2048 + half * 1024:q * 2048 + (half + 1) * 1024],
                            pT[:]), reads=(pT_t,), writes=(xct_qt[q],))
                        ycnt[0] += 1
                        if ycnt[0] % 8 == 0:
                            cast_drip(c)
                        yield

            def stageB(c):
                slot = c % 2
                hcv = hcm[:, slot * CDB:(slot + 1) * CDB]
                xhv = xh[:, slot * CDB:(slot + 1) * CDB]
                for q in range(NSC):
                    qg = c * NSC + q
                    qs = slice(q * 2048, (q + 1) * 2048)
                    sl = dtA[:, (qg % 2) * 2048:(qg % 2 + 1) * 2048]
                    slp = dtA[:, ((qg - 1) % 2) * 2048:(((qg - 1) % 2) + 1) * 2048]
                    P.op("vector", lambda e, qs=qs, q=q: e.tensor_tensor(
                        dtsub[:], xct[:, qs], wdt_bc[:], ALU.mult),
                        reads=(xct_qt[q], wdt_bc_t), writes=(dtsub_t,))
                    P.op("vector", lambda e: e.tensor_tensor(
                        dtsub[:], dtsub[:], dtb_bc[:], ALU.add),
                        reads=(dtsub_t, dtb_bc_t), writes=(dtsub_t,))
                    yield
                    P.op("scalar", lambda e: e.activation(
                        dtsub[:], dtsub[:], AF.Exp),
                        reads=(dtsub_t,), writes=(dtsub_t,))
                    P.op("scalar", lambda e: e.activation(
                        dtsub[:], dtsub[:], AF.Ln, bias=1.0),
                        reads=(dtsub_t,), writes=(dtsub_t,))
                    yield
                    P.op("vector", lambda e, sl=sl: e.tensor_tensor(
                        sl, dtsub[:], A_bc[:], ALU.mult),
                        reads=(dtsub_t, A_bc_t), writes=(dtA_tb[qg % 2],))
                    P.op("vector", lambda e, qs=qs, q=q: e.tensor_tensor(
                        usub[:], dtsub[:], xct[:, qs], ALU.mult),
                        reads=(dtsub_t, xct_qt[q]), writes=(usub_t,))
                    yield
                    if qg > 0:
                        def wfn(e, sl=sl, slp=slp):
                            r = None
                            for g in range(4):
                                e.matmul(pA[0:1, g * 512:(g + 1) * 512],
                                         K["wcols"][:, 0:1],
                                         slp[:, g * 512:(g + 1) * 512],
                                         start=True, stop=False)
                                r = e.matmul(pA[0:1, g * 512:(g + 1) * 512],
                                             K["wcols"][:, 1:2],
                                             sl[:, g * 512:(g + 1) * 512],
                                             start=False, stop=True)
                            return r
                        P.op("tensor", wfn,
                             reads=(dtA_tb[qg % 2], dtA_tb[(qg - 1) % 2], KT["wcols"]),
                             writes=tuple(pA_g))
                        P.op("scalar", lambda e: e.activation(
                            Wrow, pA[0:1, 0:2048], AF.Exp),
                            reads=tuple(pA_g), writes=(Et_t,))
                        def wbcfn(e):
                            r = None
                            for g in range(4):
                                r = e.matmul(pA[0:64, g * 512:(g + 1) * 512],
                                             K["ones_row64"][:],
                                             Wrow[0:1, g * 512:(g + 1) * 512],
                                             start=True, stop=True)
                            return r
                        P.op("tensor", wbcfn, reads=(Et_t, KT["ones_row64"]),
                             writes=tuple(pA_g))
                        P.op("vector", lambda e: e.tensor_tensor(
                            hst[:], tmpst[:], pA[0:64, 0:2048], ALU.mult),
                            reads=(tmpst_t,) + tuple(pA_g), writes=(hst_t,))
                        yield
                    def d1fn(e, sl=sl):
                        r = None
                        for g in range(4):
                            r = e.matmul(pA[:, g * 512:(g + 1) * 512], K["m_ref"][:],
                                         sl[:, g * 512:(g + 1) * 512],
                                         start=True, stop=True)
                        return r
                    P.op("tensor", d1fn, reads=(dtA_tb[qg % 2], KT["m_ref"]),
                         writes=tuple(pA_g))
                    P.op("vector", lambda e: e.tensor_scalar(
                        dcl[:], pA[:, 0:2048], -80.0, 80.0, ALU.max, ALU.min),
                        reads=tuple(pA_g), writes=(dcl_t,))
                    yield
                    P.op("scalar", lambda e: e.activation(Et[:], dcl[:], AF.Exp),
                         reads=(dcl_t,), writes=(Et_t,))
                    P.op("scalar", lambda e: e.activation(e1t[:], dcl[:], AF.Exp,
                                                          scale=-1.0),
                         reads=(dcl_t,), writes=(e1t_t,))
                    yield
                    P.op("vector", lambda e: e.tensor_tensor(
                        Ut[:], e1t[:], usub[:], ALU.mult),
                        reads=(e1t_t, usub_t), writes=(Ut_t,))
                    yield
                    P.op("tensor", lambda e, q=q: e.matmul(
                        pA[:, 0:128],
                        Bn[:, slot * 512 + q * 128:slot * 512 + (q + 1) * 128],
                        Cn[:, slot * 512 + q * 128:slot * 512 + (q + 1) * 128],
                        start=True, stop=True),
                        reads=(Bn_rt[slot], Cn_rt[slot]), writes=(pA_g[0],))
                    P.op("vector", lambda e: e.tensor_tensor(
                        Gm[:], pA[:, 0:128], K["tril"][:], ALU.mult),
                        reads=(pA_g[0], KT["tril"]), writes=(Gm_t,))
                    yield
                    def yfn(e, q=q):
                        r = None
                        for g in range(4):
                            e.matmul(pA[:, g * 512:(g + 1) * 512], Gm[:],
                                     Ut[:, g * 512:(g + 1) * 512],
                                     start=True, stop=False)
                            r = e.matmul(pA[:, g * 512:(g + 1) * 512],
                                         Cn[:, slot * 512 + q * 128:
                                             slot * 512 + (q + 1) * 128],
                                         hst[:, g * 512:(g + 1) * 512],
                                         start=False, stop=True)
                        return r
                    P.op("tensor", yfn, reads=(Gm_t, Ut_t, Cn_rt[slot], hst_t),
                         writes=tuple(pA_g))
                    P.op("vector", lambda e, q=q: e.tensor_tensor(
                        yt[:, q * 2048:(q + 1) * 2048], pA[:, 0:2048], Et[:],
                        ALU.mult), reads=tuple(pA_g) + (Et_t,), writes=(yt_t,))
                    yield
                    if qg < T // SQ - 1:
                        def dhfn(e, q=q):
                            r = None
                            for g in range(4):
                                e.matmul(pA[0:64, g * 512:(g + 1) * 512],
                                         bct[:, slot * 512 + q * 128:
                                             slot * 512 + q * 128 + 64],
                                         Ut[:, g * 512:(g + 1) * 512],
                                         start=True, stop=False)
                                r = e.matmul(pA[0:64, g * 512:(g + 1) * 512],
                                             K["ident128"][0:64, 0:64],
                                             hst[:, g * 512:(g + 1) * 512],
                                             start=False, stop=True)
                            return r
                        P.op("tensor", dhfn,
                             reads=(bct_rt[slot], Ut_t, hst_t, KT["ident128"]),
                             writes=tuple(pA_g))
                        P.op("scalar", lambda e: e.copy(
                            tmpst[:], pA[0:64, 0:2048]),
                            reads=tuple(pA_g), writes=(tmpst_t,))
                        yield
                for ct in range(DIB):
                    ys = ct % 2
                    pz = 16 + ct
                    wsl = pz % 2
                    wv = winp[:, wsl * 1024:(wsl + 1) * 1024]
                    P.dma("sync", lambda e, pz=pz, wv=wv: e.dma_start(
                        out=wv.rearrange("p (k f) -> p k f", f=128),
                        in_=scr_win[:, pz * 128:(pz + 1) * 128].rearrange(
                            "(k p) f -> p k f", p=128)),
                        reads=(scrwin_tb,), writes=(winp_tb[wsl],))
                    g = ct % 2
                    def zmm(e, wv=wv, g=g):
                        r = None
                        for kk in range(8):
                            r = e.matmul(
                                pB[:, g * 512:(g + 1) * 512],
                                wv[:, kk * 128:(kk + 1) * 128],
                                xhv[:, kk * 512:(kk + 1) * 512],
                                start=(kk == 0), stop=(kk == 7))
                        return r
                    P.op("tensor", zmm, reads=(winp_tb[wsl], xh_rt[slot]),
                         writes=(pB_g[g],))
                    P.op("scalar", lambda e, g=g, ys=ys: e.activation(
                        zbuf[:, ys * 512:(ys + 1) * 512],
                        pB[:, g * 512:(g + 1) * 512], AF.Silu),
                        reads=(pB_g[g],), writes=(zbuf_t,))
                    yield
                    def ytr(e, ct=ct):
                        r = None
                        for q in range(NSC):
                            r = e.transpose(
                                pT2[:, q * 128:(q + 1) * 128],
                                yt[:, q * 2048 + ct * 128:q * 2048 + (ct + 1) * 128],
                                K["ident128"][:])
                        return r
                    P.op("tensor", ytr, reads=(yt_t, KT["ident128"]), writes=(pT2_t,))
                    cb = convb[:, (2 + ys) * 512:(3 + ys) * 512]
                    P.op("vector", lambda e, ct=ct, cb=cb:
                         e.scalar_tensor_tensor(
                             cb, xv(slot, ct), dskipc[:, ct:ct + 1],
                             pT2[:, 0:512], ALU.mult, ALU.add),
                         reads=(xcp_ct[slot][ct], dskipc_t, pT2_t),
                         writes=(convb_t,))
                    yield
                    P.op("vector", lambda e, ct=ct, cb=cb, ys=ys: e.tensor_tensor(
                        Pf[:, ct * 512:(ct + 1) * 512], cb,
                        zbuf[:, ys * 512:(ys + 1) * 512], ALU.mult),
                        reads=(convb_t, zbuf_t), writes=(Pf_t,))
                    yield
                for m in range(DB):
                    g = m % 2
                    for hp in range(4):
                        wsl = hp % 2
                        wv = woutp[:, wsl * 512:(wsl + 1) * 512]
                        P.dma("gpsimd", lambda e, m=m, hp=hp, wv=wv: e.dma_start(
                            out=wv.rearrange("p (k d) -> p k d", d=128),
                            in_=scr_wout[hp * 512:(hp + 1) * 512,
                                         m * 128:(m + 1) * 128].rearrange(
                                "(k p) d -> p k d", p=128)),
                            reads=(scrwout_tb,), writes=(woutp_tb[wsl],))
                        def wofn(e, wv=wv, g=g, hp=hp):
                            r = None
                            for kk4 in range(4):
                                kk = hp * 4 + kk4
                                r = e.matmul(pB[:, g * 512:(g + 1) * 512],
                                             wv[:, kk4 * 128:(kk4 + 1) * 128],
                                             Pf[:, kk * 512:(kk + 1) * 512],
                                             start=(kk == 0), stop=(kk == DIB - 1))
                            return r
                        P.op("tensor", wofn, reads=(woutp_tb[wsl], Pf_t),
                             writes=(pB_g[g],))
                    P.op("vector", lambda e, m=m, g=g: e.scalar_tensor_tensor(
                        hcv[:, m * 512:(m + 1) * 512],
                        pB[:, g * 512:(g + 1) * 512], mskbc[:, 0:1],
                        hcv[:, m * 512:(m + 1) * 512], ALU.mult, ALU.add),
                        reads=(pB_g[g], mskbc_t, hcm_rt[slot]),
                        writes=(hcm_rt[slot],))
                    yield
                P.dma("gpsimd", lambda e: e.dma_start(
                    out=sup(scr_h)[:, c, :, :], in_=hcv),
                    reads=(hcm_rt[slot],), writes=(scrh_tb[c],))
                yield

            SEN = object()
            ga = stageA(0)
            for _ in ga:
                pass
            for c in range(T // 512):
                gb = stageB(c)
                ga = stageA(c + 1) if c + 1 < T // 512 else None
                done_b = done_a = ga is None
                done_b = False
                while not (done_b and done_a):
                    if not done_b:
                        done_b = next(gb, SEN) is SEN
                    if not done_a:
                        done_a = next(ga, SEN) is SEN
            while cast_jobs:
                dst, src2, tbw = cast_jobs.pop(0)
                P.dma("gpsimd", lambda e, dst=dst, src2=src2:
                      e.dma_start(out=dst, in_=src2), writes=(tbw,))

        if nphase >= 2:
            with ExitStack() as st:
                phase_mamba(st)
            P.barrier()

        def phase_tc(st):
            pA, _ = ps(st, "t_pA", (128, 2048))
            pA_g = [TB("tpAg%d" % i) for i in range(4)]
            pB, _ = ps(st, "t_pB", (128, 1024))
            pB_g = [TB("tpBg%d" % i) for i in range(2)]
            srow, srow_t = sb(st, "t_srow", (1, 16), F32)
            srowb, srowb_t = sb(st, "t_srowb", (1, 2), BF16)
            C_HI = 6.28125
            C_LO = 2.0 * np.pi - C_HI
            RND = 12582912.0
            P.dma("sync", lambda e: e.dma_start(
                out=srow[0:1, 0:1], in_=IN["L0_attn_tc_theta"][None, :]),
                writes=(srow_t,))
            P.op("vector", lambda e: e.tensor_scalar(
                srow[0:1, 1:2], srow[0:1, 0:1], 1.0, float(np.pi / 2),
                ALU.mult, ALU.add), reads=(srow_t,), writes=(srow_t,))
            for i, (src, red) in enumerate(((0, 4), (1, 5))):
                P.op("vector", lambda e, src=src, i=i: e.tensor_scalar(
                    srow[0:1, 2 + i:3 + i], srow[0:1, src:src + 1],
                    float(1.0 / (2 * np.pi)), RND, ALU.mult, ALU.add),
                    reads=(srow_t,), writes=(srow_t,))
                P.op("vector", lambda e, i=i: e.tensor_scalar(
                    srow[0:1, 2 + i:3 + i], srow[0:1, 2 + i:3 + i], 1.0, -RND,
                    ALU.mult, ALU.add), reads=(srow_t,), writes=(srow_t,))
                P.op("vector", lambda e, src=src, i=i, red=red:
                     e.scalar_tensor_tensor(
                         srow[0:1, red:red + 1], srow[0:1, 2 + i:3 + i],
                         -C_HI, srow[0:1, src:src + 1], ALU.mult, ALU.add),
                     reads=(srow_t,), writes=(srow_t,))
                P.op("vector", lambda e, i=i, red=red: e.scalar_tensor_tensor(
                    srow[0:1, red:red + 1], srow[0:1, 2 + i:3 + i], -C_LO,
                    srow[0:1, red:red + 1], ALU.mult, ALU.add),
                    reads=(srow_t,), writes=(srow_t,))
            P.op("scalar", lambda e: e.activation(srow[0:1, 6:8], srow[0:1, 4:6],
                                                  AF.Sin),
                 reads=(srow_t,), writes=(srow_t,))
            P.op("vector", lambda e: e.tensor_copy(srowb[:], srow[0:1, 6:8]),
                 reads=(srow_t,), writes=(srowb_t,))
            P.op("tensor", lambda e: e.matmul(pB[:, 0:2], K["ones_row128"][:],
                                              srowb[:], start=True, stop=True),
                 reads=(srowb_t, KT["ones_row128"]), writes=(pB_g[0],))
            msc, msc_t = sb(st, "t_msc", (128, 2), F32)
            P.op("vector", lambda e: e.tensor_scalar(
                msc[:, 0:1], pB[:, 0:1], mskbc[:, 1:2], 0.0, ALU.mult, ALU.add),
                reads=(pB_g[0], mskbc_t), writes=(msc_t,))
            P.op("vector", lambda e: e.tensor_scalar(
                msc[:, 1:2], pB[:, 1:2], mskbc[:, 1:2], 0.0, ALU.mult, ALU.add),
                reads=(pB_g[0], mskbc_t), writes=(msc_t,))
            CPS[0], CPS[1] = pB[:, 0:512], pB_g[0]
            tcb_c, tcb_c_t = sb(st, "t_tcbc", (128, DB), F32)
            load_cols(IN["L0_attn_tc_b"][None, :], D, tcb_c, tcb_c_t)

            hcm, _hcm0 = sb(st, "t_hcm", (128, 2 * DB * 512), BF16)
            xh, _xh0 = sb(st, "t_xh", (128, 2 * DB * 512), BF16)
            hcm_tb = [TB("thcm0"), TB("thcm1")]
            xh_tb = [TB("txh0"), TB("txh1")]
            sqb, sqb_t = sb(st, "t_sqb", (128, 2 * 512), BF16)
            rows, rows_t = sb(st, "t_rows", (1, 4 * 512), F32)
            rowsb, rowsb_t = sb(st, "t_rowsb", (1, 2 * 512), BF16)
            Rsb, Rsb_t = sb(st, "t_Rsb", (128, 512), BF16)
            MRsb, MRsb_t = sb(st, "t_MRsb", (128, 512), BF16)
            t1f, t1f_t = sb(st, "t_t1f", (128, 512), F32)
            tnh, tnh_t = sb(st, "t_tnh", (128, 2 * 512), BF16)
            wp, wp_t = sb(st, "t_wp", (128, DB * DB * 128), BF16)
            wp_tb = TB("twp")
            tcw = IN["L0_attn_tc_w"]
            for m in range(DB):
                wv = wp[:, m * DB * 128:(m + 1) * DB * 128]
                P.dma("gpsimd", lambda e, m=m, wv=wv: e.dma_start(
                    out=wv.rearrange("p (a d) -> p a d", d=128),
                    in_=tcw[:, m * 128:(m + 1) * 128].rearrange(
                        "(a p) d -> p a d", p=128)), writes=(wp_tb,))

            def ln_pass(hcv, htb, xhv, xtb, po, pb, j, eps):
                for b in range(DB):
                    sv = sqb[:, (b % 2) * 512:(b % 2 + 1) * 512]
                    P.op("scalar", lambda e, hcv=hcv, b=b, sv=sv, htb=htb:
                         e.activation(
                             sv, hcv[:, b * 512:(b + 1) * 512], AF.Square),
                         reads=(htb,), writes=(sqb_t,))
                    P.op("tensor", lambda e, hcv=hcv, b=b, po=po, htb=htb:
                         e.matmul(
                             pA[0:1, po:po + 512], K["ones_col"][:],
                             hcv[:, b * 512:(b + 1) * 512],
                             start=(b == 0), stop=(b == DB - 1)),
                         reads=(htb, KT["ones_col"]), writes=(pA_g[pb],))
                    P.op("tensor", lambda e, b=b, sv=sv, po=po: e.matmul(
                        pA[0:1, po + 512:po + 1024], K["ones_col"][:], sv,
                        start=(b == 0), stop=(b == DB - 1)),
                        reads=(sqb_t, KT["ones_col"]), writes=(pA_g[pb + 1],))
                rm = rows[0:1, 0:512]
                rv = rows[0:1, 512:1024]
                rsd = rows[0:1, 1024:1536]
                rmr = rows[0:1, 1536:2048]
                P.op("vector", lambda e, rm=rm, po=po: e.tensor_single_scalar(
                    rm, pA[0:1, po:po + 512], 1.0 / D, ALU.mult),
                    reads=(pA_g[pb],), writes=(rows_t,))
                P.op("vector", lambda e, rv=rv, po=po, eps=eps: e.tensor_scalar(
                    rv, pA[0:1, po + 512:po + 1024], 1.0 / D, eps,
                    ALU.mult, ALU.add),
                    reads=(pA_g[pb + 1],), writes=(rows_t,))
                P.op("vector", lambda e, rm=rm, rmr=rmr: e.tensor_tensor(
                    rmr, rm, rm, ALU.mult), reads=(rows_t,), writes=(rows_t,))
                P.op("vector", lambda e, rv=rv, rmr=rmr: e.tensor_tensor(
                    rv, rmr, rv, ALU.subtract), reads=(rows_t,), writes=(rows_t,))
                P.op("scalar", lambda e, rv=rv: e.activation(rv, rv, AF.Ln,
                                                             scale=-1.0),
                     reads=(rows_t,), writes=(rows_t,))
                P.op("scalar", lambda e, rv=rv, rsd=rsd: e.activation(
                    rsd, rv, AF.Exp, scale=-0.5),
                     reads=(rows_t,), writes=(rows_t,))
                P.op("vector", lambda e, rm=rm, rsd=rsd, rmr=rmr: e.tensor_tensor(
                    rmr, rm, rsd, ALU.mult), reads=(rows_t,), writes=(rows_t,))
                P.op("vector", lambda e, rsd=rsd: e.tensor_copy(
                    rowsb[0:1, 0:512], rsd), reads=(rows_t,), writes=(rowsb_t,))
                P.op("vector", lambda e, rmr=rmr: e.tensor_copy(
                    rowsb[0:1, 512:1024], rmr), reads=(rows_t,), writes=(rowsb_t,))
                P.op("tensor", lambda e, po=po: e.matmul(
                    pA[:, po:po + 512], K["ones_row128"][:], rowsb[0:1, 0:512],
                    start=True, stop=True),
                    reads=(rowsb_t, KT["ones_row128"]), writes=(pA_g[pb],))
                P.op("tensor", lambda e, po=po: e.matmul(
                    pA[:, po + 512:po + 1024], K["ones_row128"][:],
                    rowsb[0:1, 512:1024],
                    start=True, stop=True),
                    reads=(rowsb_t, KT["ones_row128"]), writes=(pA_g[pb + 1],))
                P.op("vector", lambda e, po=po: e.tensor_copy(
                    Rsb[:], pA[:, po:po + 512]),
                     reads=(pA_g[pb],), writes=(Rsb_t,))
                P.op("vector", lambda e, po=po: e.tensor_copy(
                    MRsb[:], pA[:, po + 512:po + 1024]),
                     reads=(pA_g[pb + 1],), writes=(MRsb_t,))
                for b in range(DB):
                    tv = t1f[:, 0:512]
                    P.op("vector", lambda e, hcv=hcv, b=b, tv=tv, htb=htb:
                         e.tensor_tensor(
                             tv, hcv[:, b * 512:(b + 1) * 512], Rsb[:],
                             ALU.mult),
                         reads=(htb, Rsb_t), writes=(t1f_t,))
                    P.op("vector", lambda e, tv=tv: e.tensor_tensor(
                        tv, tv, MRsb[:], ALU.subtract),
                        reads=(MRsb_t, t1f_t), writes=(t1f_t,))
                    if j is None:
                        P.op("vector", lambda e, b=b, tv=tv, xhv=xhv, xtb=xtb:
                             e.tensor_copy(
                                 xhv[:, b * 512:(b + 1) * 512], tv),
                             reads=(t1f_t,), writes=(xtb,))
                    else:
                        P.op("vector", lambda e, b=b, tv=tv, j=j, xhv=xhv,
                             xtb=xtb: e.tensor_scalar(
                                 xhv[:, b * 512:(b + 1) * 512], tv,
                                 lng[:, j * DB + b:j * DB + b + 1],
                                 lnb[:, j * DB + b:j * DB + b + 1],
                                 ALU.mult, ALU.add),
                             reads=(t1f_t, lng_t, lnb_t), writes=(xtb,))

            def emit_A(c):
                par = c % 2
                hcv = hcm[:, par * DB * 512:(par + 1) * DB * 512]
                htb = hcm_tb[par]
                xhv = xh[:, par * DB * 512:(par + 1) * DB * 512]
                xtb = xh_tb[par]
                P.dma("sync", lambda e, hcv=hcv, c=c: e.dma_start(
                    out=hcv, in_=sup(scr_h)[:, c, :, :]),
                    reads=(scrh_tb[c],), writes=(htb,))
                ln_pass(hcv, htb, xhv, xtb, par * 1024, par * 2, 1, EPS_IN)

            def emit_B(c):
                par = c % 2
                hcv = hcm[:, par * DB * 512:(par + 1) * DB * 512]
                htb = hcm_tb[par]
                xhv = xh[:, par * DB * 512:(par + 1) * DB * 512]
                xtb = xh_tb[par]
                po = par * 1024
                pb = par * 2
                for m in range(DB):
                    g = m % 2
                    wv = wp[:, m * DB * 128:(m + 1) * DB * 128]
                    def tcmm(e, wv=wv, g=g, po=po, xhv=xhv):
                        r = None
                        for kk in range(DB):
                            r = e.matmul(pA[:, po + g * 512:po + (g + 1) * 512],
                                         wv[:, kk * 128:(kk + 1) * 128],
                                         xhv[:, kk * 512:(kk + 1) * 512],
                                         start=(kk == 0), stop=(kk == DB - 1))
                        return r
                    P.op("tensor", tcmm, reads=(wp_tb, xtb),
                         writes=(pA_g[pb + g],))
                    tv = tnh[:, g * 512:(g + 1) * 512]
                    P.op("scalar", lambda e, g=g, m=m, tv=tv, po=po, pb=pb:
                         e.activation(
                             tv, pA[:, po + g * 512:po + (g + 1) * 512],
                             AF.Tanh, bias=tcb_c[:, m:m + 1]),
                         reads=(pA_g[pb + g], tcb_c_t), writes=(tnh_t,))
                    P.op("vector", lambda e, m=m, tv=tv, hcv=hcv, htb=htb:
                         e.scalar_tensor_tensor(
                             hcv[:, m * 512:(m + 1) * 512], tv, msc[:, 0:1],
                             hcv[:, m * 512:(m + 1) * 512], ALU.mult, ALU.add),
                         reads=(tnh_t, msc_t, htb), writes=(htb,))
                    P.op("vector", lambda e, m=m, hcv=hcv, htb=htb, xhv=xhv,
                         xtb=xtb: e.scalar_tensor_tensor(
                             hcv[:, m * 512:(m + 1) * 512],
                             xhv[:, m * 512:(m + 1) * 512], msc[:, 1:2],
                             hcv[:, m * 512:(m + 1) * 512], ALU.mult, ALU.add),
                         reads=(xtb, msc_t, htb), writes=(htb,))
                P.dma("gpsimd", lambda e, hcv=hcv, c=c, htb=htb: e.dma_start(
                    out=sup(scr_h)[:, c, :, :], in_=hcv),
                    reads=(htb,), writes=(scrh_tb[c],))

            emit_A(0)
            for c in range(T // 512):
                if c + 1 < T // 512:
                    emit_A(c + 1)
                emit_B(c)

        if nphase >= 3:
            with ExitStack() as st:
                phase_tc(st)
            P.barrier()

        def make_ln(st, tag, g0ap, g0tb, g1ap, g1tb):
            sqb, sqb_t = sb(st, tag + "_sqb", (128, 2 * 512), BF16)
            rows, rows_t = sb(st, tag + "_rows", (1, 4 * 512), F32)
            rowsb, rowsb_t = sb(st, tag + "_rowsb", (1, 2 * 512), BF16)
            Rsb, Rsb_t = sb(st, tag + "_Rsb", (128, 512), BF16)
            MRsb, MRsb_t = sb(st, tag + "_MRsb", (128, 512), BF16)
            t1f, t1f_t = sb(st, tag + "_t1f", (128, 512), F32)

            def ln_pass(hcv, hc_tb, xhap, xh_tb, j, eps):
                for b in range(DB):
                    sv = sqb[:, (b % 2) * 512:(b % 2 + 1) * 512]
                    P.op("scalar", lambda e, hcv=hcv, b=b, sv=sv: e.activation(
                        sv, hcv[:, b * 512:(b + 1) * 512], AF.Square),
                        reads=(hc_tb,), writes=(sqb_t,))
                    P.op("tensor", lambda e, hcv=hcv, b=b: e.matmul(
                        g0ap[0:1, :], K["ones_col"][:],
                        hcv[:, b * 512:(b + 1) * 512],
                        start=(b == 0), stop=(b == DB - 1)),
                        reads=(hc_tb, KT["ones_col"]), writes=(g0tb,))
                    P.op("tensor", lambda e, b=b, sv=sv: e.matmul(
                        g1ap[0:1, :], K["ones_col"][:], sv,
                        start=(b == 0), stop=(b == DB - 1)),
                        reads=(sqb_t, KT["ones_col"]), writes=(g1tb,))
                rm = rows[0:1, 0:512]
                rv = rows[0:1, 512:1024]
                rsd = rows[0:1, 1024:1536]
                rmr = rows[0:1, 1536:2048]
                P.op("vector", lambda e, rm=rm: e.tensor_single_scalar(
                    rm, g0ap[0:1, :], 1.0 / D, ALU.mult),
                    reads=(g0tb,), writes=(rows_t,))
                P.op("vector", lambda e, rv=rv, eps=eps: e.tensor_scalar(
                    rv, g1ap[0:1, :], 1.0 / D, eps, ALU.mult, ALU.add),
                    reads=(g1tb,), writes=(rows_t,))
                P.op("vector", lambda e, rm=rm, rmr=rmr: e.tensor_tensor(
                    rmr, rm, rm, ALU.mult), reads=(rows_t,), writes=(rows_t,))
                P.op("vector", lambda e, rv=rv, rmr=rmr: e.tensor_tensor(
                    rv, rmr, rv, ALU.subtract), reads=(rows_t,), writes=(rows_t,))
                P.op("scalar", lambda e, rv=rv: e.activation(rv, rv, AF.Ln,
                                                             scale=-1.0),
                     reads=(rows_t,), writes=(rows_t,))
                P.op("scalar", lambda e, rv=rv, rsd=rsd: e.activation(
                    rsd, rv, AF.Exp, scale=-0.5),
                     reads=(rows_t,), writes=(rows_t,))
                P.op("vector", lambda e, rm=rm, rsd=rsd, rmr=rmr: e.tensor_tensor(
                    rmr, rm, rsd, ALU.mult), reads=(rows_t,), writes=(rows_t,))
                P.op("vector", lambda e, rsd=rsd: e.tensor_copy(
                    rowsb[0:1, 0:512], rsd), reads=(rows_t,), writes=(rowsb_t,))
                P.op("vector", lambda e, rmr=rmr: e.tensor_copy(
                    rowsb[0:1, 512:1024], rmr), reads=(rows_t,), writes=(rowsb_t,))
                P.op("tensor", lambda e: e.matmul(
                    g0ap[:, :], K["ones_row128"][:], rowsb[0:1, 0:512],
                    start=True, stop=True),
                    reads=(rowsb_t, KT["ones_row128"]), writes=(g0tb,))
                P.op("tensor", lambda e: e.matmul(
                    g1ap[:, :], K["ones_row128"][:], rowsb[0:1, 512:1024],
                    start=True, stop=True),
                    reads=(rowsb_t, KT["ones_row128"]), writes=(g1tb,))
                P.op("vector", lambda e: e.tensor_copy(Rsb[:], g0ap[:, :]),
                     reads=(g0tb,), writes=(Rsb_t,))
                P.op("vector", lambda e: e.tensor_copy(MRsb[:], g1ap[:, :]),
                     reads=(g1tb,), writes=(MRsb_t,))
                for b in range(DB):
                    tv = t1f[:, 0:512]
                    P.op("vector", lambda e, hcv=hcv, b=b, tv=tv: e.tensor_tensor(
                        tv, hcv[:, b * 512:(b + 1) * 512], Rsb[:], ALU.mult),
                        reads=(hc_tb, Rsb_t), writes=(t1f_t,))
                    P.op("vector", lambda e, tv=tv: e.tensor_tensor(
                        tv, tv, MRsb[:], ALU.subtract),
                        reads=(MRsb_t, t1f_t), writes=(t1f_t,))
                    if j is None:
                        P.op("vector", lambda e, b=b, tv=tv: e.tensor_copy(
                            xhap[:, b * 512:(b + 1) * 512], tv),
                            reads=(t1f_t,), writes=(xh_tb,))
                    else:
                        P.op("vector", lambda e, b=b, tv=tv, j=j: e.tensor_scalar(
                            xhap[:, b * 512:(b + 1) * 512], tv,
                            lng[:, j * DB + b:j * DB + b + 1],
                            lnb[:, j * DB + b:j * DB + b + 1], ALU.mult, ALU.add),
                            reads=(t1f_t, lng_t, lnb_t), writes=(xh_tb,))
            return ln_pass

        def phase_wlam(st):
            L = T // 2
            pS, _ = ps(st, "w_pS", (128, 2048))
            pS_g = [TB("wpSg%d" % i) for i in range(4)]
            pO = None
            pO_g = [TB("wpOg%d" % i) for i in range(4)]
            CPS[0], CPS[1] = pS[:, 0:512], pS_g[0]
            wlc, wlc_t = sb(st, "w_wlc", (128, DB * 4), F32)
            for kk in range(4):
                load_cols(IN["L0_attn_wlam_conv"][kk, :][None, :], D, wlc, wlc_t,
                          dst_cols=[b * 4 + kk for b in range(DB)])
            Ap, Ap_t = sb(st, "w_Ap", (128, DB * L), BF16)
            st1 = ExitStack()
            hcm, hcm_t = sb(st1, "w_hcm", (128, DB * 512), BF16)
            xh, xh_t = sb(st1, "w_xh", (128, DB * 512), BF16)
            ln = make_ln(st1, "w", pS[:, 0:512], pS_g[0], pS[:, 512:1024], pS_g[1])
            ddb, ddb_t = sb(st1, "w_ddb", (128, DB * 256), BF16)
            for s_i in range(T // 512):
                hcv = hcm[:, :]
                P.dma("sync", lambda e, hcv=hcv, s_i=s_i: e.dma_start(
                    out=hcv, in_=sup(scr_h)[:, s_i, :, :]),
                    reads=(scrh_tb[s_i],), writes=(hcm_t,))
                ln(hcv, hcm_t, xh, xh_t, 2, EPS_IN)
                for b in range(DB):
                    ev = xh.rearrange("p (b two l) -> p b two l", b=DB, two=256)
                    P.op("vector", lambda e, b=b, s_i=s_i, ev=ev: e.tensor_tensor(
                        Ap[:, b * L + s_i * 256:b * L + (s_i + 1) * 256],
                        ev[:, b, :, 0], ev[:, b, :, 1], ALU.add),
                        reads=(xh_t,), writes=(Ap_t,))
                    P.op("vector", lambda e, b=b, ev=ev: e.tensor_tensor(
                        ddb[:, b * 256:(b + 1) * 256],
                        ev[:, b, :, 0], ev[:, b, :, 1], ALU.subtract),
                        reads=(xh_t,), writes=(ddb_t,))
                    P.op("vector", lambda e, b=b: e.tensor_single_scalar(
                        ddb[:, b * 256:(b + 1) * 256],
                        ddb[:, b * 256:(b + 1) * 256], 0.5, ALU.mult),
                        reads=(ddb_t,), writes=(ddb_t,))
                P.dma("gpsimd", lambda e, s_i=s_i: e.dma_start(
                    out=scr_dp.rearrange("p (b l) -> p b l", b=DB)[
                        :, :, s_i * 256:(s_i + 1) * 256],
                    in_=ddb.rearrange("p (b l) -> p b l", b=DB)),
                    reads=(ddb_t,), writes=(scrdp_tb,))
            st1.close()
            qT, qT_t = sb(st, "w_qT", (128, DB * L), BF16)
            kT, kT_t = sb(st, "w_kT", (128, DB * L), BF16)
            vext, vext_t = sb(st, "w_vext", (128, 16 * NH * 65), BF16)
            wqp, wqp_t = sb(st, "w_wqp", (128, 2 * DB * 128), BF16)
            wqp_tb = [TB("wq0"), TB("wq1")]
            P.op("vector", lambda e: e.memset(
                vext.rearrange("p (kt h j) -> p kt h j", h=NH, j=65)[:, :, :, 64:65],
                1.0), writes=(vext_t,))
            for wsrc, dstap, dsttb in ((IN["L0_attn_w_q"], qT, qT_t),
                                       (IN["L0_attn_w_k"], kT, kT_t)):
                for m in range(DB):
                    slot = m % 2
                    wv = wqp[:, slot * DB * 128:(slot + 1) * DB * 128]
                    P.dma("gpsimd", lambda e, m=m, wv=wv, wsrc=wsrc: e.dma_start(
                        out=wv.rearrange("p (a d) -> p a d", d=128),
                        in_=wsrc[:, m * 128:(m + 1) * 128].rearrange(
                            "(a p) d -> p a d", p=128)), writes=(wqp_tb[slot],))
                    def qmm(e, wv=wv):
                        r = None
                        for kk in range(DB):
                            for n4 in range(4):
                                r = e.matmul(
                                    pS[:, n4 * 512:(n4 + 1) * 512],
                                    wv[:, kk * 128:(kk + 1) * 128],
                                    Ap[:, kk * L + n4 * 512:kk * L + (n4 + 1) * 512],
                                    start=(kk == 0), stop=(kk == DB - 1))
                        return r
                    P.op("tensor", qmm, reads=(wqp_tb[slot], Ap_t),
                         writes=tuple(pS_g))
                    P.op("scalar", lambda e, m=m, dstap=dstap: e.copy(
                        dstap[:, m * L:(m + 1) * L], pS[:, 0:2048]),
                        reads=tuple(pS_g), writes=(dsttb,))
            vex4 = vext.rearrange("p (kt h j) -> p kt h j", h=NH, j=65)
            vtmp, vtmp_t = sb(st, "w_vtmp", (128, L), BF16)
            stv = ExitStack()
            pTb, pTb_t = ps(stv, "w_pTb", (128, 1024), BF16)
            for m in range(DB):
                slot = m % 2
                wv = wqp[:, slot * DB * 128:(slot + 1) * DB * 128]
                P.dma("gpsimd", lambda e, m=m, wv=wv: e.dma_start(
                    out=wv.rearrange("p (a d) -> p a d", d=128),
                    in_=IN["L0_attn_w_v"][:, m * 128:(m + 1) * 128].rearrange(
                        "(a p) d -> p a d", p=128)), writes=(wqp_tb[slot],))
                def vmmf(e, wv=wv):
                    r = None
                    for kk in range(DB):
                        for n4 in range(4):
                            r = e.matmul(
                                pS[:, n4 * 512:(n4 + 1) * 512],
                                wv[:, kk * 128:(kk + 1) * 128],
                                Ap[:, kk * L + n4 * 512:kk * L + (n4 + 1) * 512],
                                start=(kk == 0), stop=(kk == DB - 1))
                    return r
                P.op("tensor", vmmf, reads=(wqp_tb[slot], Ap_t), writes=tuple(pS_g))
                P.op("scalar", lambda e: e.copy(vtmp[:], pS[:, 0:2048]),
                     reads=tuple(pS_g), writes=(vtmp_t,))
                for kt in range(16):
                    P.op("tensor", lambda e, kt=kt: e.transpose(
                        pTb[:, (kt % 8) * 128:((kt % 8) + 1) * 128],
                        vtmp[:, kt * 128:(kt + 1) * 128], K["ident128"][:]),
                        reads=(vtmp_t, KT["ident128"]), writes=(pTb_t,))
                    P.op("vector", lambda e, kt=kt, m=m: e.tensor_copy(
                        vex4[:, kt, 2 * m:2 * m + 2, 0:64],
                        pTb[:, (kt % 8) * 128:((kt % 8) + 1) * 128].rearrange(
                            "p (h j) -> p h j", j=64)),
                        reads=(pTb_t,), writes=(vext_t,))
            stv.close()
            pO, _ = ps(st, "w_pO", (128, 2048))
            Pb2, _pb = sb(st, "w_Pb2", (128, 2 * 2048), BF16)
            Pb2_tb = [TB("wPb0"), TB("wPb1")]
            recb, recb_t = sb(st, "w_recb", (1, 2048), BF16)
            recl, recl_t = sb(st, "w_recl", (1, 2048), BF16)
            ohb, ohb_t = sb(st, "w_ohb", (64, 2048), BF16)
            oTall, oTall_t = Ap, Ap_t
            for hh in range(NH):
                phh = 64 * (hh % 2)
                blk = hh // 2
                def emit_s(kt):
                    po = (kt % 2) * 0
                    for half in range(2):
                        def smm(e, hh=hh, phh=phh, blk=blk, kt=kt, half=half):
                            r = None
                            for n2 in range(2):
                                n4 = half * 2 + n2
                                r = e.matmul(
                                    pS[:, n4 * 512:(n4 + 1) * 512],
                                    kT[phh:phh + 64,
                                       blk * L + kt * 128:blk * L + (kt + 1) * 128],
                                    qT[phh:phh + 64,
                                       blk * L + n4 * 512:blk * L + (n4 + 1) * 512],
                                    start=True, stop=True)
                            return r
                        P.op("tensor", smm, reads=(kT_t, qT_t),
                             writes=(pS_g[half * 2], pS_g[half * 2 + 1]))
                        P.op("scalar", lambda e, half=half, kt=kt: e.activation(
                            Pb2[:, (kt % 2) * 2048 + half * 1024:
                                (kt % 2) * 2048 + (half + 1) * 1024],
                            pS[:, half * 1024:(half + 1) * 1024], AF.Exp,
                            scale=1.0 / 16.0),
                            reads=(pS_g[half * 2], pS_g[half * 2 + 1]),
                            writes=(Pb2_tb[kt % 2],))
                def emit_pv(kt):
                    def pvmm(e, hh=hh, kt=kt):
                        r = None
                        for n4 in range(4):
                            r = e.matmul(
                                pO[0:65, n4 * 512:(n4 + 1) * 512],
                                vext[:, kt * NH * 65 + hh * 65:
                                     kt * NH * 65 + (hh + 1) * 65],
                                Pb2[:, (kt % 2) * 2048 + n4 * 512:
                                    (kt % 2) * 2048 + (n4 + 1) * 512],
                                start=(kt == 0), stop=(kt == 15))
                        return r
                    P.op("tensor", pvmm, reads=(vext_t, Pb2_tb[kt % 2]),
                         writes=tuple(pO_g))
                emit_s(0)
                for kt in range(1, 16):
                    emit_s(kt)
                    emit_pv(kt - 1)
                emit_pv(15)
                P.op("scalar", lambda e: e.activation(recl[:], pO[64:65, 0:2048],
                                                      AF.Ln),
                     reads=tuple(pO_g), writes=(recl_t,))
                P.op("scalar", lambda e: e.activation(recb[:], recl[:], AF.Exp,
                                                      scale=-1.0),
                     reads=(recl_t,), writes=(recb_t,))
                P.op("scalar", lambda e: e.copy(ohb[:], pO[0:64, 0:2048]),
                     reads=tuple(pO_g), writes=(ohb_t,))
                def rbc(e):
                    r = None
                    for n4 in range(4):
                        r = e.matmul(pO[0:64, n4 * 512:(n4 + 1) * 512],
                                     K["ones_row64"][:],
                                     recb[0:1, n4 * 512:(n4 + 1) * 512],
                                     start=True, stop=True)
                    return r
                P.op("tensor", rbc, reads=(recb_t, KT["ones_row64"]),
                     writes=tuple(pO_g))
                P.op("vector", lambda e, phh=phh, blk=blk: e.tensor_tensor(
                    oTall[phh:phh + 64, blk * L:(blk + 1) * L], ohb[:],
                    pO[0:64, 0:2048], ALU.mult),
                    reads=(ohb_t,) + tuple(pO_g), writes=(oTall_t,))
            dpb, dpb_t = sb(st, "w_dpb", (128, L), BF16)
            dph, dph_t = dpb, dpb_t
            ystg, ystg_t = sb(st, "w_ystg", (128, T), BF16)
            for m in range(DB):
                slot = m % 2
                wv = wqp[:, slot * DB * 128:(slot + 1) * DB * 128]
                P.dma("gpsimd", lambda e, m=m, wv=wv: e.dma_start(
                    out=wv.rearrange("p (a d) -> p a d", d=128),
                    in_=IN["L0_attn_w_o"][:, m * 128:(m + 1) * 128].rearrange(
                        "(a p) d -> p a d", p=128)), writes=(wqp_tb[slot],))
                def womm(e, wv=wv):
                    r = None
                    for kk in range(DB):
                        for n4 in range(4):
                            r = e.matmul(
                                pS[:, n4 * 512:(n4 + 1) * 512],
                                wv[:, kk * 128:(kk + 1) * 128],
                                oTall[:, kk * L + n4 * 512:kk * L + (n4 + 1) * 512],
                                start=(kk == 0), stop=(kk == DB - 1))
                    return r
                P.op("tensor", womm, reads=(wqp_tb[slot], oTall_t),
                     writes=tuple(pS_g))
                P.dma("sync", lambda e, m=m: e.dma_start(
                    out=dpb[:], in_=scr_dp[:, m * L:(m + 1) * L]),
                    reads=(scrdp_tb,), writes=(dpb_t,))
                ys2 = ystg.rearrange("p (l two) -> p l two", two=2)
                P.op("vector", lambda e, ys2=ys2: e.scalar_tensor_tensor(
                    ys2[:, :, 0], pS[:, 0:2048], 0.5, dph[:], ALU.mult, ALU.add),
                    reads=tuple(pS_g) + (dph_t,), writes=(ystg_t,))
                P.op("vector", lambda e, ys2=ys2: e.scalar_tensor_tensor(
                    ys2[:, :, 1], pS[:, 0:2048], 0.5, dph[:], ALU.mult,
                    ALU.subtract), reads=tuple(pS_g) + (dph_t,), writes=(ystg_t,))
                P.dma("gpsimd", lambda e, m=m: e.dma_start(
                    out=scr_y[:, m * T:(m + 1) * T], in_=ystg[:]),
                    reads=(ystg_t,), writes=(scry_tb,))
            ywin, ywin_t = sb(st, "w_ywin", (128, 520), BF16)
            cacc, cacc_t = sb(st, "w_cacc", (128, 512), F32)
            hcm2, hcm2_t = sb(st, "w_hcm2", (128, DB * 512), BF16)
            hcm, hcm_t = hcm2, hcm2_t
            for s_i in range(T // 512):
                hcv = hcm[:, :]
                P.dma("sync", lambda e, hcv=hcv, s_i=s_i: e.dma_start(
                    out=hcv, in_=sup(scr_h)[:, s_i, :, :]),
                    reads=(scrh_tb[s_i],), writes=(hcm_t,))
                for b in range(DB):
                    lo = s_i * 512 - 2
                    j0 = max(0, -lo)
                    j1 = min(515, T - lo)
                    if j0 > 0 or j1 < 515:
                        P.op("vector", lambda e: e.memset(ywin[:], 0.0),
                             writes=(ywin_t,))
                    P.dma("sync", lambda e, b=b, lo=lo, j0=j0, j1=j1: e.dma_start(
                        out=ywin[:, j0:j1],
                        in_=scr_y[:, b * T + lo + j0:b * T + lo + j1]),
                        reads=(ywin_t, scry_tb), writes=(ywin_t,))
                    P.op("vector", lambda e, b=b: e.tensor_scalar(
                        cacc[:], ywin[:, 0:512], wlc[:, b * 4:b * 4 + 1], 0.0,
                        ALU.mult, ALU.add),
                        reads=(ywin_t, wlc_t), writes=(cacc_t,))
                    for kk in range(1, 4):
                        P.op("vector", lambda e, b=b, kk=kk: e.scalar_tensor_tensor(
                            cacc[:], ywin[:, kk:kk + 512],
                            wlc[:, b * 4 + kk:b * 4 + kk + 1], cacc[:],
                            ALU.mult, ALU.add),
                            reads=(ywin_t, wlc_t, cacc_t), writes=(cacc_t,))
                    P.op("vector", lambda e, b=b, hcv=hcv: e.scalar_tensor_tensor(
                        hcv[:, b * 512:(b + 1) * 512], cacc[:], mskbc[:, 2:3],
                        hcv[:, b * 512:(b + 1) * 512], ALU.mult, ALU.add),
                        reads=(cacc_t, mskbc_t, hcm_t), writes=(hcm_t,))
                P.dma("gpsimd", lambda e, hcv=hcv, s_i=s_i: e.dma_start(
                    out=sup(scr_h)[:, s_i, :, :], in_=hcv),
                    reads=(hcm_t,), writes=(scrh_tb[s_i],))

        if nphase >= 4:
            with ExitStack() as st:
                phase_wlam(st)
            P.barrier()

        def phase_moe(st):
            GELU_C3 = 0.044715
            GELU_2S = 2.0 * 0.7978845608028654
            GT, GT_t = sb(st, "e_GT", (128, 32 * NE), F32)
            gfm, gfm_t = sb(st, "e_gfm", (8, 512), BF16)
            gfb, gfb_t = sb(st, "e_gfb", (8, 1024), BF16)
            b2b, b2b_t = sb(st, "e_b2b", (8, D), BF16)
            b1c, b1c_t = sb(st, "e_b1c", (128, NE * 32), F32)
            yac, yac_t = sb(st, "e_yac", (128, 8 * D), F32)
            pMA, _ = ps(st, "e_pMA", (128, 2048))
            pMA_g = [TB("epMAg%d" % i) for i in range(4)]
            pMB, _ = ps(st, "e_pMB", (128, 2048))
            pMB_g = [TB("epMBg%d" % i) for i in range(4)]
            pall_g = pMA_g + pMB_g
            pGa, pGa_t = pMA[:, 0:512], pMA_g[0]
            pGb, pGb_t = pMA[:, 512:1024], pMA_g[1]
            pGt, pGt_ts = pMB[:, 0:512], (pMB_g[0],)
            CPS[0], CPS[1] = pGa, pGa_t
            idf, idf_t = sb(st, "e_idf", (128, 128), F32)
            P.op("vector", lambda e: e.tensor_copy(idf[:], K["ident128"][:]),
                 reads=(KT["ident128"],), writes=(idf_t,))
            hcm, hcm_t = sb(st, "e_hcm", (128, DB * 512), BF16)
            xh, xh_t = sb(st, "e_xh", (128, DB * 512), BF16)
            x0b, x0b_t = sb(st, "e_x0b", (128, DB * 512), BF16)
            ln = make_ln(st, "e", pGa, pGa_t, pGb, pGb_t)
            wgb, wgb_t = sb(st, "e_wgb", (128, DB * NE), BF16)
            lgs, lgs_t = sb(st, "e_lgs", (128, 64), F32)
            xT, xT_t = sb(st, "e_xT", (128, DB * 1024), BF16)
            hact, hact_t = sb(st, "e_hact", (128, 32 * 1024), BF16)
            hact_kt = [TB("ehact%d" % k) for k in range(32)]
            w1p, w1p_t = sb(st, "e_w1p", (128, 2 * 8 * 512), BF16)
            w1p_tb = [TB("ew10"), TB("ew11")]
            w2p, w2p_t = sb(st, "e_w2p", (128, 2 * 4 * 256), BF16)
            gbt, gbt_t = sb(st, "e_gbt", (128, 1024), BF16)
            gyt, gyt_t = sb(st, "e_gyt", (128, 512), F32)
            w2p_tb = [TB("ew20"), TB("ew21")]
            gx, gx_t = sb(st, "e_gx", (128, 1024), BF16)
            gq, gq_t = sb(st, "e_gq", (128, 1024), BF16)
            gt2, gt2_t = gq, gq_t
            P.dma("gpsimd", lambda e: e.dma_start(out=b2b[:],
                                                  in_=IN["L0_attn_moe_b2"]),
                  writes=(b2b_t,))
            P.dma("gpsimd", lambda e: e.dma_start(
                out=wgb.rearrange("p (a j) -> p a j", j=NE),
                in_=IN["L0_attn_moe_wg"].rearrange("(a p) j -> p a j", p=128)),
                writes=(wgb_t,))
            b1r, b1r_t = sb(st, "e_b1r", (32, 128), F32)
            for ee in range(NE):
                P.dma("sync", lambda e, ee=ee: e.dma_start(
                    out=b1r[:], in_=IN["L0_attn_moe_b1"][ee, :].rearrange(
                        "(a f) -> a f", f=128)), writes=(b1r_t,))
                P.op("tensor", lambda e: e.transpose(
                    pGt[0:128, 0:32], b1r[:], idf[0:32, 0:32]),
                    reads=(b1r_t, idf_t), writes=pGt_ts)
                P.op("vector", lambda e, ee=ee: e.tensor_copy(
                    b1c[:, ee * 32:(ee + 1) * 32], pGt[0:128, 0:32]),
                    reads=pGt_ts, writes=(b1c_t,))

            for s_i in range(T // 512):
                P.dma("sync", lambda e, s_i=s_i: e.dma_start(
                    out=hcm[:], in_=sup(scr_h)[:, s_i, :, :]),
                    reads=(scrh_tb[s_i],), writes=(hcm_t,))
                ln(hcm, hcm_t, xh, xh_t, 3, EPS_IN)
                P.dma("gpsimd", lambda e, s_i=s_i: e.dma_start(
                    out=sup(scr_x3)[:, s_i, :, :],
                    in_=xh.rearrange("p (b t) -> p b t", b=DB)),
                    reads=(xh_t,), writes=(scrx3_tb,))
                for tt in range(4):
                    tg = s_i * 4 + tt
                    def lgmm(e, tt=tt):
                        r = None
                        for kk in range(DB):
                            r = e.matmul(
                                pGa[:, 0:NE],
                                xh.rearrange("p (b t) -> p b t", b=DB)[
                                    :, kk, tt * 128:(tt + 1) * 128],
                                wgb[:, kk * NE:(kk + 1) * NE],
                                start=(kk == 0), stop=(kk == DB - 1))
                        return r
                    P.op("tensor", lgmm, reads=(xh_t, wgb_t), writes=(pGa_t,))
                    lg = lgs[:, 0:8]
                    m1c = lgs[:, 8:9]
                    m2c = lgs[:, 9:10]
                    dcol = lgs[:, 10:11]
                    g1c = lgs[:, 11:12]
                    g2c = lgs[:, 12:13]
                    eq1 = lgs[:, 16:24]
                    eq2 = lgs[:, 24:32]
                    l2 = lgs[:, 32:40]
                    ga = lgs[:, 40:48]
                    P.op("vector", lambda e, lg=lg: e.tensor_copy(lg, pGa[:, 0:NE]),
                         reads=(pGa_t,), writes=(lgs_t,))
                    P.op("vector", lambda e, lg=lg, m1c=m1c: e.reduce_max(
                        m1c, lg, axis=AX.X), reads=(lgs_t,), writes=(lgs_t,))
                    P.op("vector", lambda e, lg=lg, m1c=m1c, eq1=eq1:
                         e.tensor_scalar(eq1, lg, m1c, 0.0, ALU.is_ge, ALU.add),
                         reads=(lgs_t,), writes=(lgs_t,))
                    P.op("vector", lambda e, eq1=eq1, lg=lg, l2=l2:
                         e.scalar_tensor_tensor(l2, eq1, -1e30, lg,
                                                ALU.mult, ALU.add),
                         reads=(lgs_t,), writes=(lgs_t,))
                    P.op("vector", lambda e, l2=l2, m2c=m2c: e.reduce_max(
                        m2c, l2, axis=AX.X), reads=(lgs_t,), writes=(lgs_t,))
                    P.op("vector", lambda e, m1c=m1c, m2c=m2c, dcol=dcol:
                         e.tensor_tensor(dcol, m1c, m2c, ALU.subtract),
                         reads=(lgs_t,), writes=(lgs_t,))
                    P.op("scalar", lambda e, dcol=dcol, g1c=g1c: e.activation(
                        g1c, dcol, AF.Sigmoid), reads=(lgs_t,), writes=(lgs_t,))
                    P.op("vector", lambda e, g1c=g1c, g2c=g2c: e.tensor_scalar(
                        g2c, g1c, -1.0, 1.0, ALU.mult, ALU.add),
                        reads=(lgs_t,), writes=(lgs_t,))
                    P.op("vector", lambda e, lg=lg, m2c=m2c, eq2=eq2:
                         e.tensor_scalar(eq2, lg, m2c, 0.0, ALU.is_ge, ALU.add),
                         reads=(lgs_t,), writes=(lgs_t,))
                    P.op("vector", lambda e, eq2=eq2, eq1=eq1: e.tensor_tensor(
                        eq2, eq2, eq1, ALU.subtract),
                        reads=(lgs_t,), writes=(lgs_t,))
                    P.op("vector", lambda e, eq1=eq1, g1c=g1c, ga=ga:
                         e.tensor_scalar(ga, eq1, g1c, 0.0, ALU.mult, ALU.add),
                         reads=(lgs_t,), writes=(lgs_t,))
                    P.op("vector", lambda e, eq2=eq2, g2c=g2c, ga=ga, tg=tg:
                         e.scalar_tensor_tensor(
                             GT[:, tg * NE:(tg + 1) * NE], eq2, g2c, ga,
                             ALU.mult, ALU.add),
                         reads=(lgs_t,), writes=(GT_t,))
            for grp in range(8):
                def gtr(e, grp=grp):
                    r = None
                    for j in range(4):
                        tg = grp * 4 + j
                        r = e.transpose(pGt[0:8, j * 128:(j + 1) * 128],
                                        GT[:, tg * NE:(tg + 1) * NE], idf[:])
                    return r
                P.op("tensor", gtr, reads=(GT_t, idf_t), writes=pGt_ts)
                P.op("vector", lambda e: e.tensor_copy(
                    gfm[0:8, 0:512], pGt[0:8, 0:512]),
                    reads=pGt_ts, writes=(gfm_t,))
                P.dma("gpsimd", lambda e, grp=grp: e.dma_start(
                    out=scr_gf[0:8, grp * 512:(grp + 1) * 512], in_=gfm[0:8, :]),
                    reads=(gfm_t,), writes=(scrgf_tb,))

            w1src = scr_w1
            w2src = scr_w2
            for big in range(4):
                P.dma("sync", lambda e, big=big: e.dma_start(
                    out=gfb[0:8, :],
                    in_=scr_gf[0:8, big * 1024:(big + 1) * 1024]),
                    reads=(scrgf_tb,), writes=(gfb_t,))
                for sh in range(2):
                    P.dma("sync", lambda e, big=big, sh=sh: e.dma_start(
                        out=xT.rearrange("p (b t) -> p b t", b=DB)[
                            :, :, sh * 512:(sh + 1) * 512],
                        in_=sup(scr_x3)[:, 2 * big + sh, :, :]),
                        reads=(scrx3_tb,), writes=(xT_t,))
                for ee in range(NE):
                    for kk in range(32):
                        if kk % 4 == 0 and not (kk == 0 and ee > 0):
                            slot = (kk // 4) % 2
                            wv = w1p[:, slot * 4096:(slot + 1) * 4096]
                            P.dma("gpsimd", lambda e, ee=ee, kk=kk, wv=wv:
                                  e.dma_start(
                                      out=wv.rearrange("p (a f) -> p a f", f=512),
                                      in_=w1src[ee * D:(ee + 1) * D,
                                                kk * 128:(kk + 4) * 128]
                                      .rearrange("(a p) f -> p a f", p=128)),
                                  reads=(scrw1_tb,), writes=(w1p_tb[slot],))
                        slot = (kk // 4) % 2
                        wv = w1p[:, slot * 4096:(slot + 1) * 4096]
                        gg = (kk % 2) * 2
                        def m1fn(e, wv=wv, kk=kk, gg=gg):
                            r = None
                            for kp in range(DB):
                                for n2 in range(2):
                                    r = e.matmul(
                                        pMA[:, (gg + n2) * 512:(gg + n2 + 1) * 512],
                                        wv[:, kp * 512 + (kk % 4) * 128:
                                           kp * 512 + (kk % 4 + 1) * 128],
                                        xT[:, kp * 1024 + n2 * 512:
                                           kp * 1024 + (n2 + 1) * 512],
                                        start=(kp == 0), stop=(kp == DB - 1))
                            return r
                        P.op("tensor", m1fn, reads=(w1p_tb[slot], xT_t),
                             writes=(pMA_g[gg], pMA_g[gg + 1]))
                        pin = pMA[:, gg * 512:(gg + 2) * 512]
                        pgs = (pMA_g[gg], pMA_g[gg + 1])
                        bcol = b1c[:, ee * 32 + kk:ee * 32 + kk + 1]
                        P.op("scalar", lambda e, pin=pin, bcol=bcol: e.activation(
                            gx[:], pin, AF.Identity, bias=bcol),
                            reads=pgs + (b1c_t,), writes=(gx_t,))
                        P.op("scalar", lambda e, pin=pin, bcol=bcol: e.activation(
                            gq[:], pin, AF.Square, bias=bcol),
                            reads=pgs + (b1c_t,), writes=(gq_t,))
                        P.op("vector", lambda e: e.tensor_scalar(
                            gt2[:], gq[:], GELU_C3, 1.0, ALU.mult, ALU.add),
                            reads=(gq_t,), writes=(gt2_t,))
                        P.op("vector", lambda e: e.tensor_tensor(
                            gt2[:], gt2[:], gx[:], ALU.mult),
                            reads=(gt2_t, gx_t), writes=(gt2_t,))
                        P.op("scalar", lambda e: e.activation(
                            gt2[:], gt2[:], AF.Sigmoid, scale=GELU_2S),
                            reads=(gt2_t,), writes=(gt2_t,))
                        P.op("vector", lambda e, kk=kk: e.tensor_tensor(
                            hact[:, kk * 1024:(kk + 1) * 1024], gt2[:], gx[:],
                            ALU.mult), reads=(gt2_t, gx_t),
                            writes=(hact_kt[kk],))
                    if ee + 1 < NE:
                        wv0 = w1p[:, 0:4096]
                        P.dma("gpsimd", lambda e, ee=ee, wv0=wv0: e.dma_start(
                            out=wv0.rearrange("p (a f) -> p a f", f=512),
                            in_=w1src[(ee + 1) * D:(ee + 2) * D, 0:512]
                            .rearrange("(a p) f -> p a f", p=128)),
                            reads=(scrw1_tb,), writes=(w1p_tb[0],))
                    def gbtmm(e, ee=ee, big=big):
                        r = None
                        for n2 in range(2):
                            r = e.matmul(
                                pMB[:, n2 * 512:(n2 + 1) * 512],
                                K["sel8"][:, ee * 128:(ee + 1) * 128],
                                gfb[0:8, n2 * 512:(n2 + 1) * 512],
                                start=True, stop=True)
                        return r
                    P.op("tensor", gbtmm, reads=(KT["sel8"], gfb_t),
                         writes=(pMB_g[0], pMB_g[1]))
                    P.op("vector", lambda e: e.tensor_copy(
                        gbt[:], pMB[:, 0:1024]),
                        reads=(pMB_g[0], pMB_g[1]), writes=(gbt_t,))
                    for dp in range(4):
                        ppp = pMB if dp % 2 == 0 else pMA
                        ppg = pMB_g if dp % 2 == 0 else pMA_g
                        for kkg in range(8):
                            slot = kkg % 2
                            wv = w2p[:, slot * 1024:(slot + 1) * 1024]
                            P.dma("sync" if kkg % 2 == 0 else "gpsimd",
                                  lambda e, ee=ee, kkg=kkg, dp=dp,
                                  wv=wv: e.dma_start(
                                      out=wv.rearrange("p (a f) -> p a f", f=256),
                                      in_=w2src[ee * FF + kkg * 512:
                                                ee * FF + (kkg + 1) * 512,
                                                dp * 256:(dp + 1) * 256]
                                      .rearrange("(a p) f -> p a f", p=128)),
                                  reads=(scrw2_tb,), writes=(w2p_tb[slot],))
                            def m2fn(e, wv=wv, kkg=kkg, ppp=ppp):
                                r = None
                                for kl in range(4):
                                    kk = kkg * 4 + kl
                                    for dmb2 in range(2):
                                        for th in range(2):
                                            gidx = dmb2 * 2 + th
                                            r = e.matmul(
                                                ppp[:, gidx * 512:
                                                    (gidx + 1) * 512],
                                                wv[:, kl * 256 + dmb2 * 128:
                                                   kl * 256 + (dmb2 + 1) * 128],
                                                hact[:, kk * 1024 + th * 512:
                                                     kk * 1024 + (th + 1) * 512],
                                                start=(kk == 0), stop=(kk == 31))
                                return r
                            P.op("tensor", m2fn,
                                 reads=(w2p_tb[slot],)
                                 + tuple(hact_kt[kkg * 4:kkg * 4 + 4]),
                                 writes=tuple(ppg))
                        for dmb2 in range(2):
                            for th in range(2):
                                gidx = dmb2 * 2 + th
                                dmb = dp * 2 + dmb2
                                yv = yac[:, dmb * 1024 + th * 512:
                                         dmb * 1024 + (th + 1) * 512]
                                P.op("vector", lambda e, gidx=gidx, th=th,
                                     ppp=ppp: e.tensor_tensor(
                                         gyt[:], ppp[:, gidx * 512:
                                                     (gidx + 1) * 512],
                                         gbt[:, th * 512:(th + 1) * 512],
                                         ALU.mult),
                                     reads=(ppg[gidx], gbt_t),
                                     writes=(gyt_t,))
                                if ee == 0:
                                    P.op("vector", lambda e, yv=yv:
                                         e.tensor_copy(yv, gyt[:]),
                                         reads=(gyt_t,), writes=(yac_t,))
                                else:
                                    P.op("vector", lambda e, yv=yv:
                                         e.tensor_tensor(
                                             yv, yv, gyt[:], ALU.add),
                                         reads=(gyt_t, yac_t),
                                         writes=(yac_t,))
                for dmb in range(DB):
                    def b2mm(e, dmb=dmb, big=big):
                        r = None
                        for n2 in range(2):
                            r = e.matmul(
                                pMA[:, n2 * 512:(n2 + 1) * 512],
                                b2b[0:8, dmb * 128:(dmb + 1) * 128],
                                gfb[0:8, n2 * 512:(n2 + 1) * 512],
                                start=True, stop=True)
                        return r
                    P.op("tensor", b2mm, reads=(gfb_t, b2b_t),
                         writes=(pMA_g[0], pMA_g[1]))
                    P.op("vector", lambda e, dmb=dmb: e.tensor_tensor(
                        yac[:, dmb * 1024:(dmb + 1) * 1024],
                        yac[:, dmb * 1024:(dmb + 1) * 1024],
                        pMA[:, 0:1024], ALU.add),
                        reads=(pMA_g[0], pMA_g[1], yac_t), writes=(yac_t,))
                for shalf in range(2):
                    s_i = big * 2 + shalf
                    P.dma("sync", lambda e, s_i=s_i: e.dma_start(
                        out=hcm[:], in_=sup(scr_h)[:, s_i, :, :]),
                        reads=(scrh_tb[s_i],), writes=(hcm_t,))
                    P.dma("sync", lambda e, s_i=s_i: e.dma_start(
                        out=x0b[:], in_=sup(scr_x0fm)[:, s_i, :, :]),
                        reads=(scrx0_tb,), writes=(x0b_t,))
                    for b in range(DB):
                        P.op("vector", lambda e, b=b, shalf=shalf:
                             e.scalar_tensor_tensor(
                                 xh[:, b * 512:(b + 1) * 512],
                                 yac[:, b * 1024 + shalf * 512:
                                     b * 1024 + (shalf + 1) * 512],
                                 mskbc[:, 3:4],
                                 hcm[:, b * 512:(b + 1) * 512],
                                 ALU.mult, ALU.add),
                             reads=(yac_t, mskbc_t, hcm_t),
                             writes=(xh_t,))
                    ln(xh, xh_t, xh, xh_t, None, EPS_OUT)
                    for b in range(DB):
                        P.op("vector", lambda e, b=b: e.tensor_tensor(
                            xh[:, b * 512:(b + 1) * 512],
                            xh[:, b * 512:(b + 1) * 512],
                            x0b[:, b * 512:(b + 1) * 512], ALU.add),
                            reads=(x0b_t, xh_t), writes=(xh_t,))
                    P.dma("gpsimd", lambda e, s_i=s_i: e.dma_start(
                        out=sup(scr_h)[:, s_i, :, :], in_=xh[:]),
                        reads=(xh_t,), writes=(scrh_tb[s_i],))

        if nphase >= 5:
            with ExitStack() as st:
                phase_moe(st)
            P.barrier()

        def phase_ffn(st):
            pF, _ = ps(st, "f_pF", (128, 2048))
            pF_g = [TB("fpFg%d" % i) for i in range(4)]
            pG, _ = ps(st, "f_pG", (128, 1024))
            pG_g = [TB("fpGg%d" % i) for i in range(2)]
            pT, pT_t = ps(st, "f_pT", (128, 1024), BF16)
            CPS[0], CPS[1] = pG[:, 0:512], pG_g[0]
            muc, muc_t = sb(st, "f_muc", (128, 4 * DB), F32)
            load_cols(IN["L0_ffn_mu_r"][None, :], D, muc, muc_t,
                      dst_cols=list(range(DB)))
            load_cols(IN["L0_ffn_mu_k"][None, :], D, muc, muc_t,
                      dst_cols=list(range(DB, 2 * DB)))
            P.op("vector", lambda e: e.tensor_scalar(
                muc[:, 2 * DB:4 * DB], muc[:, 0:2 * DB], -1.0, 1.0,
                ALU.mult, ALU.add), reads=(muc_t,), writes=(muc_t,))
            wT = {}
            wtmp, wtmp_t = sb(st, "f_wtmp", (128, DB * 128), F32)
            for nm, key in (("r", "L0_ffn_W_r"), ("k", "L0_ffn_W_k"),
                            ("v", "L0_ffn_W_v")):
                wT[nm] = sb(st, "f_wT" + nm, (128, DB * DB * 128), BF16)
                for m in range(DB):
                    P.dma("sync", lambda e, key=key, m=m: e.dma_start(
                        out=wtmp.rearrange("p (a d) -> p a d", d=128),
                        in_=IN[key][:, m * 128:(m + 1) * 128].rearrange(
                            "(a p) d -> p a d", p=128)),
                        reads=(wtmp_t,), writes=(wtmp_t,))
                    for a in range(DB):
                        P.op("tensor", lambda e, a=a: e.transpose(
                            pF[:, (a % 4) * 512:(a % 4) * 512 + 128],
                            wtmp[:, a * 128:(a + 1) * 128], idfT[:]),
                            reads=(wtmp_t, idfT_t), writes=(pF_g[a % 4],))
                        P.op("vector", lambda e, nm=nm, m=m, a=a: e.tensor_copy(
                            wT[nm][0][:, (m * DB + a) * 128:(m * DB + a + 1) * 128],
                            pF[:, (a % 4) * 512:(a % 4) * 512 + 128]),
                            reads=(pF_g[a % 4],), writes=(wT[nm][1],))
            hcm, hcm_t = sb(st, "f_hcm", (128, DB * 512), BF16)
            xh, xh_t = sb(st, "f_xh", (128, DB * 512), BF16)
            ln = make_ln(st, "f", pG[:, 0:512], pG_g[0], pG[:, 512:1024], pG_g[1])
            xr, xr_t = sb(st, "f_xr", (128, DB * 512), BF16)
            xk, xk_t = sb(st, "f_xk", (128, DB * 512), BF16)
            rb, rb_t = sb(st, "f_rb", (128, DB * 512), BF16)
            ka, ka_t = sb(st, "f_ka", (128, DB * 512), BF16)
            fb, fb_t = sb(st, "f_fb", (128, DB * 512), BF16)
            tbuf, tbuf_t = sb(st, "f_tbuf", (128, 512), BF16)
            cc, cc_t = sb(st, "f_cc", (128, DB), BF16)
            x1T, x1T_t = sb(st, "f_x1T", (128, 1024), BF16)
            ot, ot_t = sb(st, "f_ot", (128, 1024), F32)
            P.op("vector", lambda e: e.memset(cc[:], 0.0), writes=(cc_t,))
            for s_i in range(T // 512):
                P.dma("sync", lambda e, s_i=s_i: e.dma_start(
                    out=hcm[:], in_=sup(scr_h)[:, s_i, :, :]),
                    reads=(scrh_tb[s_i],), writes=(hcm_t,))
                ln(hcm, hcm_t, xh, xh_t, None, EPS_OUT)
                for b in range(DB):
                    xb_ = xh[:, b * 512:(b + 1) * 512]
                    for which, dst, dst_t, mo in (("r", xr, xr_t, 0),
                                                  ("k", xk, xk_t, DB)):
                        dv = dst[:, b * 512:(b + 1) * 512]
                        P.op("vector", lambda e, xb_=xb_, b=b, mo=mo: e.tensor_scalar(
                            tbuf[:], xb_, muc[:, mo + b:mo + b + 1], 0.0,
                            ALU.mult, ALU.add),
                            reads=(xh_t, muc_t), writes=(tbuf_t,))
                        P.op("vector", lambda e, xb_=xb_, b=b, mo=mo, dv=dv:
                             e.scalar_tensor_tensor(
                                 dv[:, 1:512], xb_[:, 0:511],
                                 muc[:, 2 * DB + mo + b:2 * DB + mo + b + 1],
                                 tbuf[:, 1:512], ALU.mult, ALU.add),
                             reads=(xh_t, muc_t, tbuf_t), writes=(dst_t,))
                        P.op("vector", lambda e, b=b, mo=mo, dv=dv:
                             e.scalar_tensor_tensor(
                                 dv[:, 0:1], cc[:, b:b + 1],
                                 muc[:, 2 * DB + mo + b:2 * DB + mo + b + 1],
                                 tbuf[:, 0:1], ALU.mult, ALU.add),
                             reads=(cc_t, muc_t, tbuf_t), writes=(dst_t,))
                P.op("vector", lambda e: e.tensor_copy(
                    cc[:], xh.rearrange("p (b t) -> p b t", b=DB)[:, :, 511]),
                    reads=(xh_t,), writes=(cc_t,))
                for m in range(DB):
                    g = m % 2
                    def rmm(e, m=m, g=g):
                        r = None
                        for kk in range(DB):
                            r = e.matmul(
                                pG[:, g * 512:(g + 1) * 512],
                                wT["r"][0][:, (kk * DB + m) * 128:
                                           (kk * DB + m + 1) * 128],
                                xr[:, kk * 512:(kk + 1) * 512],
                                start=(kk == 0), stop=(kk == DB - 1))
                        return r
                    P.op("tensor", rmm, reads=(wT["r"][1], xr_t), writes=(pG_g[g],))
                    P.op("scalar", lambda e, m=m, g=g: e.activation(
                        rb[:, m * 512:(m + 1) * 512],
                        pG[:, g * 512:(g + 1) * 512], AF.Sigmoid),
                        reads=(pG_g[g],), writes=(rb_t,))
                for m in range(DB):
                    g = m % 2
                    def kmm(e, m=m, g=g):
                        r = None
                        for kk in range(DB):
                            r = e.matmul(
                                pG[:, g * 512:(g + 1) * 512],
                                wT["k"][0][:, (kk * DB + m) * 128:
                                           (kk * DB + m + 1) * 128],
                                xk[:, kk * 512:(kk + 1) * 512],
                                start=(kk == 0), stop=(kk == DB - 1))
                        return r
                    P.op("tensor", kmm, reads=(wT["k"][1], xk_t), writes=(pG_g[g],))
                    P.op("scalar", lambda e, m=m, g=g: e.activation(
                        ka[:, m * 512:(m + 1) * 512],
                        pG[:, g * 512:(g + 1) * 512], AF.Relu),
                        reads=(pG_g[g],), writes=(ka_t,))
                    P.op("scalar", lambda e, m=m: e.activation(
                        ka[:, m * 512:(m + 1) * 512],
                        ka[:, m * 512:(m + 1) * 512], AF.Square),
                        reads=(ka_t,), writes=(ka_t,))
                for m in range(DB):
                    g = m % 2
                    def vmm(e, m=m, g=g):
                        r = None
                        for kk in range(DB):
                            r = e.matmul(
                                pG[:, g * 512:(g + 1) * 512],
                                wT["v"][0][:, (kk * DB + m) * 128:
                                           (kk * DB + m + 1) * 128],
                                ka[:, kk * 512:(kk + 1) * 512],
                                start=(kk == 0), stop=(kk == DB - 1))
                        return r
                    P.op("tensor", vmm, reads=(wT["v"][1], ka_t), writes=(pG_g[g],))
                    P.op("vector", lambda e, m=m, g=g: e.tensor_tensor(
                        fb[:, m * 512:(m + 1) * 512],
                        pG[:, g * 512:(g + 1) * 512],
                        rb[:, m * 512:(m + 1) * 512], ALU.mult),
                        reads=(pG_g[g], rb_t), writes=(fb_t,))
                ln(fb, fb_t, fb, fb_t, None, EPS_OUT)
                for tt in range(4):
                    def x1tr(e, tt=tt):
                        r = None
                        for b in range(DB):
                            r = e.transpose(
                                pT[:, b * 128:(b + 1) * 128],
                                hcm[:, b * 512 + tt * 128:b * 512 + (tt + 1) * 128],
                                K["ident128"][:])
                        return r
                    P.op("tensor", x1tr, reads=(hcm_t, KT["ident128"]),
                         writes=(pT_t,))
                    P.op("vector", lambda e: e.tensor_copy(x1T[:], pT[:]),
                         reads=(pT_t,), writes=(x1T_t,))
                    def ftr(e, tt=tt):
                        r = None
                        for b in range(DB):
                            r = e.transpose(
                                pT[:, b * 128:(b + 1) * 128],
                                fb[:, b * 512 + tt * 128:b * 512 + (tt + 1) * 128],
                                K["ident128"][:])
                        return r
                    P.op("tensor", ftr, reads=(fb_t, KT["ident128"]),
                         writes=(pT_t,))
                    P.op("vector", lambda e: e.tensor_tensor(
                        ot[:], pT[:], x1T[:], ALU.add),
                        reads=(pT_t, x1T_t), writes=(ot_t,))
                    P.dma("gpsimd", lambda e, s_i=s_i, tt=tt: e.dma_start(
                        out=out_hbm[s_i * 512 + tt * 128:
                                    s_i * 512 + (tt + 1) * 128, :], in_=ot[:]),
                        reads=(ot_t,))
            P.wait_all("gpsimd")

        if nphase >= 6:
            with ExitStack() as st:
                idfT, idfT_t = sb(st, "f_idfT", (128, 128), F32)
                P.op("vector", lambda e: e.tensor_copy(idfT[:], K["ident128"][:]),
                     reads=(KT["ident128"],), writes=(idfT_t,))
                phase_ffn(st)
            P.barrier()

        if nphase < 6:
          with ExitStack() as st:
              hc, hc_t = sb(st, "fin_hc", (128, 2 * DB * 512), BF16)
              ot, ot_t = sb(st, "fin_ot", (128, 2 * D), F32)
              pfin, pfin_t = ps(st, "fin_ps", (128, 1024), BF16)
              for s_idx in range(T // 512):
                  ri = s_idx % 2
                  hcv = hc[:, ri * DB * 512:(ri + 1) * DB * 512]
                  P.dma("sync", lambda e, hcv=hcv, s_idx=s_idx: e.dma_start(
                      out=hcv, in_=sup(scr_h)[:, s_idx, :, :]),
                      reads=(scrh_tb[s_idx],), writes=(hc_t,))
                  for tb in range(4):
                      ro = (s_idx * 4 + tb) % 2
                      ov = ot[:, ro * D:(ro + 1) * D]
                      for b in range(DB):
                          P.op("tensor", lambda e, hcv=hcv, b=b, tb=tb: e.transpose(
                              pfin[:, b * 128:(b + 1) * 128],
                              hcv.rearrange("p (b t) -> p b t", b=DB)[:, b,
                                            tb * 128:(tb + 1) * 128],
                              K["ident128"][:]), reads=(hc_t, KT["ident128"]),
                              writes=(pfin_t,))
                      P.op("vector", lambda e, ov=ov: e.tensor_copy(ov, pfin[:]),
                           reads=(pfin_t,), writes=(ot_t,))
                      P.dma("gpsimd", lambda e, ov=ov, s_idx=s_idx, tb=tb: e.dma_start(
                          out=out_hbm[s_idx * 512 + tb * 128:s_idx * 512 + (tb + 1) * 128, :],
                          in_=ov), reads=(ot_t,))
              P.wait_all("gpsimd")

        P.finalize()
    return nc, dumps


def kernel(**inputs):
    nc, _ = build_program()
    x = np.ascontiguousarray(inputs["L0_attn_hidden_states"], dtype=np.float32)
    B = x.shape[0]
    in_maps = []
    for b in range(B):
        m = {}
        for name, shape in INPUT_SPECS:
            if name == "L0_attn_hidden_states":
                m[name] = np.ascontiguousarray(x[b])
            else:
                a = np.ascontiguousarray(inputs[name], dtype=np.float32)
                if a.shape == ():
                    a = a.reshape(1)
                m[name] = a
        in_maps.append(m)
    res = run_bass_kernel_spmd(nc, in_maps, core_ids=list(range(B)))
    return np.stack([res.results[b]["out"] for b in range(B)], axis=0)
